# version 09
# baseline (speedup 1.0000x reference)
import numpy as np
import ml_dtypes
import concourse.bass as bass
import concourse.mybir as mybir
from concourse.bass_utils import run_bass_kernel_spmd
from contextlib import ExitStack

F32 = mybir.dt.float32
BF16 = mybir.dt.bfloat16
AF = mybir.ActivationFunctionType
ALU = mybir.AluOpType
AX = mybir.AxisListType
BF = ml_dtypes.bfloat16

NCORES = 8
D = 1024
NH4 = 4096
T = 4096
NDT = D // 128
NTC = T // 512
EPS = 1e-5

ENGINES = ("sync", "gpsimd", "tensor", "vector", "scalar")
ESEM = {"tensor": "T", "vector": "V", "scalar": "SC", "gpsimd": "P"}


class Sched:

    def __init__(self, nc):
        self.nc = nc
        self.q = {e: [] for e in ENGINES}
        self.cnt = {}
        self.sems = {}
        self.es = ExitStack()

    def sem(self, name):
        if name not in self.sems:
            self.sems[name] = self.es.enter_context(self.nc.semaphore(name))
            self.cnt[name] = 0
        return self.sems[name]

    def op(self, eng, fn, inc=None, inc_n=1):
        if inc is True:
            inc = ESEM[eng]
        if eng in ("vector", "scalar", "gpsimd"):
            own = ESEM[eng]
            self.sem(own)
            if self.cnt[own] > 0:
                self.q[eng].append(("wait", own, self.cnt[own]))
        if inc is not None:
            self.sem(inc)
            self.cnt[inc] += inc_n
        self.q[eng].append(("op", fn, inc, inc_n))
        return self.cnt[inc] if inc is not None else None

    NSTREAM = {"sync": 6, "gpsimd": 6, "scalar": 2, "vector": 2}

    def dma(self, eng, out, in_, **kw):
        if not hasattr(self, "_rr"):
            self._rr = {}
        i = self._rr.get(eng, 0)
        self._rr[eng] = (i + 1) % self.NSTREAM[eng]
        sname = f"q_{eng}_{i}"
        self.sem(sname)
        if self.cnt[sname] > 0:
            self.q[eng].append(("wait", sname, self.cnt[sname]))
        self.cnt[sname] += 16
        self.q[eng].append(("dma", out, in_, sname, kw))
        return (sname, self.cnt[sname])

    def wait_tok(self, eng, tok):
        if tok is None:
            return
        self.q[eng].append(("wait", tok[0], tok[1]))

    def wait_queue(self, eng, qeng):
        for i in range(self.NSTREAM[qeng]):
            sname = f"q_{qeng}_{i}"
            if self.cnt.get(sname, 0) > 0:
                self.q[eng].append(("wait", sname, self.cnt[sname]))

    def wait(self, eng, sem, val=None):
        self.sem(sem)
        v = self.cnt[sem] if val is None else val
        if v > 0:
            self.q[eng].append(("wait", sem, v))
        return v

    def finalize(self):
        for s in list(self.cnt):
            self.q["sync"].append(("wait", s, self.cnt[s]))

    def barrier(self):
        snap = dict(self.cnt)
        for e in ENGINES:
            for s, v in snap.items():
                if v > 0:
                    self.q[e].append(("wait", s, v))

    def build(self):
        nc = self.nc
        with self.es, nc.Block() as block:
            def runner(eng_name):
                def _run(eng):
                    for item in self.q[eng_name]:
                        kind = item[0]
                        if kind == "op":
                            _, fn, inc, inc_n = item
                            instr = fn(eng)
                            if inc is not None:
                                instr.then_inc(self.sems[inc], inc_n)
                        elif kind == "dma":
                            _, out, in_, semn, kw = item
                            eng.dma_start(out=out, in_=in_, **kw).then_inc(
                                self.sems[semn], 16)
                        else:
                            _, semn, v = item
                            eng.wait_ge(self.sems[semn], v)
                return _run
            block.sync(runner("sync"))
            block.gpsimd(runner("gpsimd"))
            block.tensor(runner("tensor"))
            block.vector(runner("vector"))
            block.scalar(runner("scalar"))



def t_mm(sch, out, lhsT, rhs, start, stop, inc=True):
    return sch.op("tensor",
                  lambda e, o=out, l=lhsT, r=rhs, s=start, p=stop:
                  e.matmul(o, l, r, start=s, stop=p), inc=inc)


def t_tr(sch, out, in_, iden, inc=True):
    return sch.op("tensor", lambda e, o=out, i=in_, d=iden: e.transpose(o, i, d),
                  inc=inc)


def v_tt(sch, out, in0, in1, alu, inc=True):
    return sch.op("vector", lambda e, o=out, a=in0, b=in1, p=alu:
                  e.tensor_tensor(o, a, b, p), inc=inc)


def v_ts(sch, out, in0, s1, s2, op0, op1=None, inc=True):
    if op1 is None:
        return sch.op("vector", lambda e, o=out, a=in0, s=s1, p=op0:
                      e.tensor_scalar(o, a, s, None, p), inc=inc)
    return sch.op("vector", lambda e, o=out, a=in0, x=s1, y=s2, p=op0, q=op1:
                  e.tensor_scalar(o, a, x, y, p, q), inc=inc)


def v_stt(sch, out, in0, scalar, in1, op0, op1, inc=True):
    return sch.op("vector", lambda e, o=out, a=in0, s=scalar, b=in1, p=op0, q=op1:
                  e.scalar_tensor_tensor(o, a, s, b, p, q), inc=inc)


def v_copy(sch, out, in_, inc=True):
    return sch.op("vector", lambda e, o=out, a=in_: e.tensor_copy(o, a), inc=inc)


def v_red(sch, out, in_, alu, negate=False, inc=True):
    return sch.op("vector", lambda e, o=out, a=in_, p=alu, n=negate:
                  e.tensor_reduce(o, a, axis=AX.X, op=p, negate=n), inc=inc)


def v_scan(sch, out, d0, d1, init, inc=True):
    return sch.op("vector", lambda e, o=out, a=d0, b=d1, i=init:
                  e.tensor_tensor_scan(o, a, b, i, ALU.mult, ALU.add), inc=inc)


def v_recip(sch, out, in_, inc=True):
    return sch.op("vector", lambda e, o=out, a=in_: e.reciprocal(o, a), inc=inc)


def v_memset(sch, ap, c, inc=True):
    return sch.op("vector", lambda e, o=ap, k=c: e.memset(o, k), inc=inc)


def s_act(sch, out, in_, func, bias=None, scale=None, accum=None, inc=True):
    kw = {}
    if bias is not None:
        kw["bias"] = bias
    if scale is not None:
        kw["scale"] = scale
    if accum is not None:
        kw["accum_out"] = accum
    return sch.op("scalar", lambda e, o=out, a=in_, f=func, k=dict(kw):
                  e.activation(out=o, in_=a, func=f, **k), inc=inc)


def g_tt(sch, out, in0, in1, alu, inc=True):
    return sch.op("gpsimd", lambda e, o=out, a=in0, b=in1, p=alu:
                  e.tensor_tensor(o, a, b, p), inc=inc)


class PsumPool:

    def __init__(self, sch, slots, sem="V"):
        self.sch = sch
        self.slots = [[ap, 0, "V"] for ap in slots]
        self.i = 0

    def acquire(self):
        ap, cnt, sem = self.slots[self.i % len(self.slots)]
        self.cur = self.i % len(self.slots)
        self.i += 1
        if cnt > 0:
            self.sch.wait("tensor", sem, cnt)
        return ap

    def release(self, sem, cnt):
        self.slots[self.cur][1] = cnt
        self.slots[self.cur][2] = sem


class Ctx:
    def __init__(self, nc):
        self.nc = nc
        self.sch = Sched(nc)
        es = self.sch.es
        sb = lambda name, shape, dt: es.enter_context(nc.sbuf_tensor(name, shape, dt))
        self.AMID = sb("AMID", [128, 32768], BF16)
        self.ABIG = sb("ABIG", [128, 36864], BF16)
        self.AF = sb("AF", [128, 9216], F32)
        self.ACOL = sb("ACOL", [128, 320], F32)
        self.ONE = sb("ONE", [128, 72], BF16)
        self.ONEF = sb("ONEF", [128, 2], F32)
        self.IDEN = sb("IDEN", [128, 128], BF16)
        self.IDENF = sb("IDENF", [128, 128], F32)
        self.PS = es.enter_context(nc.psum_tensor("PS", [128, 3584], F32))
        self.PSS = es.enter_context(nc.psum_tensor("PSS", [128, 512], F32))
        self.dram = {}
        self.inputs = set()
        self.acol_used = 0
        self.ln_count = 0

    def din(self, name, shape, dtype=F32):
        if name not in self.dram:
            self.dram[name] = self.nc.dram_tensor(
                name, list(shape), dtype, kind="ExternalInput").ap()
            self.inputs.add(name)
        return self.dram[name]

    def dout(self, name, shape, dtype=F32):
        if name not in self.dram:
            self.dram[name] = self.nc.dram_tensor(
                name, list(shape), dtype, kind="ExternalOutput").ap()
        return self.dram[name]

    def dtmp(self, name, shape, dtype=F32):
        if name not in self.dram:
            self.dram[name] = self.nc.dram_tensor(name, list(shape), dtype).ap()
        return self.dram[name]

    def col(self, n=1):
        i = self.acol_used
        self.acol_used += n
        assert self.acol_used <= 320, "ACOL overflow"
        return i

    def const_col(self, value):
        if not hasattr(self, "_ccols"):
            self._ccols = {}
        if value not in self._ccols:
            i = self.col(1)
            ap = self.ACOL[:, i:i + 1]
            self.sch.op("vector", lambda e, a=ap, v=value: e.memset(a, v), inc=True)
            self._ccols[value] = ap
        return self._ccols[value]

    def xlnT(self, dt):
        return self.AMID[:, dt * T:(dt + 1) * T]


def emit_setup(ctx):
    sch = ctx.sch
    sch.op("vector", lambda e: e.memset(ctx.ONE[:, :], 1.0), inc=True)
    sch.op("vector", lambda e: e.memset(ctx.ONEF[:, :], 1.0), inc=True)
    iden = ctx.din("C_iden", [128, 128], BF16)
    sch.dma("sync", ctx.IDEN[:, :], iden)
    idenf = ctx.din("C_idenf", [128, 128], F32)
    sch.dma("sync", ctx.IDENF[:, :], idenf)
    ctx.const_col(EPS)
    ctx.const_col(-0.5)
    ctx.const_col(-1.0)
    ctx.const_col(1.0)
    ctx.const_col(1.0 / 32768.0)
    sch.barrier()


def load_cols(ctx, name, n):
    if not hasattr(ctx, "_colcache"):
        ctx._colcache = {}
    if name in ctx._colcache:
        return ctx._colcache[name]
    i = ctx.col(n)
    ap = ctx.din(name, [n * 128, 1], F32)
    for j in range(n):
        ctx.sch.dma("sync", ctx.ACOL[:, i + j:i + j + 1],
                    ap[j * 128:(j + 1) * 128, :])
    ctx._colcache[name] = ctx.ACOL[:, i:i + n]
    return ctx._colcache[name]



def emit_ln(ctx, x_dram, xln_dram=None, chunks=None, dst=None):
    nc, sch = ctx.nc, ctx.sch
    ABIG, AF_, PS, PSS = ctx.ABIG, ctx.AF, ctx.PS, ctx.PSS
    tag = ctx.ln_count
    ctx.ln_count += 1
    scr = ctx.dtmp(f"lnrow{tag}", [2, T], F32)
    if chunks is None:
        chunks = list(range(NTC))
    if dst is None:
        dst = ctx.xlnT
    c0 = chunks[0]
    nck = len(chunks)
    sls = 512 * nck * NDT

    xbf = [ABIG[:, dt * 512 * nck:(dt + 1) * 512 * nck] for dt in range(NDT)]
    ld_done = []
    for dt in range(NDT):
        ld_done.append(sch.dma("gpsimd", xbf[dt],
                               x_dram[dt * 128:(dt + 1) * 128,
                                      c0 * 512:(c0 + nck) * 512]))

    v_after_stats = 0
    bcast_reads = []
    for ci, c in enumerate(chunks):
        t0 = (c - c0) * 512
        sq_done = []
        for dt in range(NDT):
            xsq = ABIG[:, sls + (dt % 2) * 512: sls + 512 + (dt % 2) * 512]
            sch.wait_tok("scalar", ld_done[dt])
            if dt >= 2:
                sch.wait("scalar", "T", sq_done[dt - 2][1])
            s = s_act(sch, xsq, xbf[dt][:, t0:t0 + 512], AF.Square)
            sch.wait("tensor", "SC", s)
            if dt == 0 and v_after_stats:
                sch.wait("tensor", "V", v_after_stats)
            t_mm(sch, PSS[0:1, 0:512], ctx.ONE[:, 0:1], xbf[dt][:, t0:t0 + 512],
                 dt == 0, dt == NDT - 1, inc=None)
            tt = t_mm(sch, PS[0:1, 3072:3584], ctx.ONE[:, 0:1], xsq,
                      dt == 0, dt == NDT - 1)
            sq_done.append((s, tt))
        mu = AF_[0:1, 6656:7168]
        var = AF_[0:1, 7168:7680]
        tmp = AF_[0:1, 7680:8192]
        a_r = AF_[0:1, 8192:8704]
        b_r = AF_[0:1, 8704:9216]
        sch.wait("vector", "T")
        v_ts(sch, mu, PSS[0:1, 0:512], 1.0 / D, None, ALU.mult)
        v_ts(sch, var, PS[0:1, 3072:3584], 1.0 / D, None, ALU.mult)
        v_after_stats = sch.cnt["V"]
        v_tt(sch, tmp, mu, mu, ALU.mult)
        vv = v_tt(sch, var, var, tmp, ALU.subtract)
        sch.wait("scalar", "V", vv)
        s_act(sch, var, var, AF.Ln, bias=ctx.const_col(EPS)[0:1, :])
        ss = s_act(sch, a_r, var, AF.Exp, scale=ctx.const_col(-0.5)[0:1, :])
        sch.wait("vector", "SC", ss)
        v_tt(sch, b_r, mu, a_r, ALU.mult)
        vb = v_ts(sch, b_r, b_r, -1.0, None, ALU.mult)
        sch.wait("gpsimd", "V", vb)
        w1 = sch.dma("gpsimd", scr[0:1, t0:t0 + 512], a_r)
        w2 = sch.dma("gpsimd", scr[1:2, t0:t0 + 512], b_r)
        sch.wait_tok("gpsimd", w1)
        sch.wait_tok("gpsimd", w2)
        Ab = ABIG[:, sls + 1024 + (ci % 2) * 1024: sls + 1536 + (ci % 2) * 1024]
        Bb = ABIG[:, sls + 1536 + (ci % 2) * 1024: sls + 2048 + (ci % 2) * 1024]
        if ci >= 2:
            sch.wait("gpsimd", "V", bcast_reads[ci - 2])
        ra = sch.dma("gpsimd", Ab, scr[0:1, t0:t0 + 512].to_broadcast((128, 512)))
        rb = sch.dma("gpsimd", Bb, scr[1:2, t0:t0 + 512].to_broadcast((128, 512)))
        sch.wait_tok("vector", ra)
        sch.wait_tok("vector", rb)
        last = None
        for dt in range(NDT):
            if ci == 0:
                sch.wait_tok("vector", ld_done[dt])
            tmpb = ABIG[:, sls + 3072 + (dt % 2) * 512: sls + 3584 + (dt % 2) * 512]
            v_tt(sch, tmpb, xbf[dt][:, t0:t0 + 512], Ab, ALU.mult)
            last = v_tt(sch, dst(dt)[:, t0:t0 + 512], tmpb, Bb, ALU.add)
        bcast_reads.append(last)
    if xln_dram is not None:
        sch.wait("sync", "V")
        for dt in range(NDT):
            sch.dma("sync", xln_dram[dt * 128:(dt + 1) * 128, :], dst(dt))



def emit_ffn0(ctx, x_dram, xo_dram, pfx="L0_ffn"):
    nc, sch = ctx.nc, ctx.sch
    ABIG, AF_, PS, PSS = ctx.ABIG, ctx.AF, ctx.PS, ctx.PSS
    emit_ln(ctx, x_dram)
    bcol = load_cols(ctx, pfx + "_bcol", NDT)

    WT = ctx.din(pfx + "_WT", [D, D], BF16)
    Vr = ctx.din(pfx + "_Vr", [D, 32], BF16)
    CT = ctx.din(pfx + "_CT", [32, D], BF16)
    sWT = [ABIG[:, dt * 1024:(dt + 1) * 1024] for dt in range(NDT)]
    sVr = [ABIG[:, 8192 + dt * 32: 8224 + dt * 32] for dt in range(NDT)]
    sCT = ABIG[0:32, 8448:9472]
    rT = ABIG[0:32, 9472:13568]

    sch.wait("gpsimd", "V")
    for dt in range(NDT):
        sch.dma("gpsimd", sWT[dt], WT[dt * 128:(dt + 1) * 128, :])
        sch.dma("gpsimd", sVr[dt], Vr[dt * 128:(dt + 1) * 128, :])
    wload = sch.dma("gpsimd", sCT, CT)

    sch.wait_queue("tensor", "gpsimd")
    sch.wait("tensor", "V")
    ev = 0
    for c in range(NTC):
        if c >= 1 and ev:
            sch.wait("tensor", "V", ev)
        for dt in range(NDT):
            t_mm(sch, PSS[0:32, 0:512], sVr[dt], ctx.xlnT(dt)[:, c * 512:(c + 1) * 512],
                 dt == 0, dt == NDT - 1)
        sch.wait("vector", "T")
        ev = v_copy(sch, rT[:, c * 512:(c + 1) * 512], PSS[0:32, 0:512])

    pool = PsumPool(sch, [ctx.PS[:, i * 512:(i + 1) * 512] for i in range(4)])
    xin_ld = {}
    out_st = {}
    for m in range(NDT):
        for c in range(NTC):
            idx = m * NTC + c
            slot = idx % 4
            xin = AF_[:, slot * 512:(slot + 1) * 512]
            if idx >= 4:
                sch.wait("sync", "V", xin_ld[idx - 4])
            ldx = sch.dma("sync", xin,
                          x_dram[m * 128:(m + 1) * 128, c * 512:(c + 1) * 512])
            ps = pool.acquire()
            for dt in range(NDT):
                t_mm(sch, ps, sWT[dt][:, m * 128:(m + 1) * 128],
                     ctx.xlnT(dt)[:, c * 512:(c + 1) * 512], dt == 0, False)
            sch.wait("tensor", "V", ev)
            tdone = t_mm(sch, ps, sCT[:, m * 128:(m + 1) * 128],
                         rT[:, c * 512:(c + 1) * 512], False, True)
            out = AF_[:, 2048 + slot * 512: 2560 + slot * 512]
            sch.wait("vector", "T", tdone)
            sch.wait_tok("vector", ldx)
            if idx >= 4:
                sch.wait_tok("vector", out_st[idx - 4])
            vdone = v_stt(sch, out, ps, bcol[:, m:m + 1], xin, ALU.add, ALU.add)
            pool.release("V", vdone)
            xin_ld[idx] = vdone
            sch.wait("sync", "V", vdone)
            out_st[idx] = sch.dma(
                "sync", xo_dram[m * 128:(m + 1) * 128, c * 512:(c + 1) * 512], out)



def emit_attn_a(ctx, x_dram, xo_dram):
    nc, sch = ctx.nc, ctx.sch
    ABIG, AMID, AF_, PS, PSS = ctx.ABIG, ctx.AMID, ctx.AF, ctx.PS, ctx.PSS
    pfx = "L0A"
    NH = 16

    arep = load_cols(ctx, pfx + "_arep", NDT)
    bq = load_cols(ctx, pfx + "_bq", NDT)
    bk = load_cols(ctx, pfx + "_bk", NDT)
    bv = load_cols(ctx, pfx + "_bv", NDT)
    bo = load_cols(ctx, pfx + "_bo", NDT)
    gam = load_cols(ctx, pfx + "_gamma", NDT)
    bet = load_cols(ctx, pfx + "_beta", NDT)
    qs_i = ctx.col(NDT)
    qscale = ctx.ACOL[:, qs_i:qs_i + NDT]
    bq_i = ctx.col(NDT)
    bqs = ctx.ACOL[:, bq_i:bq_i + NDT]
    sch.wait_queue("vector", "sync")
    v_ts(sch, qscale, arep, 0.125, None, ALU.mult)
    v_tt(sch, bqs, bq, qscale, ALU.mult)

    zT_d = ctx.dtmp("att_zT", [D, T], BF16)
    QT_d = ctx.dtmp("att_QT", [D, T], BF16)
    KT_d = ctx.dtmp("att_KT", [D, T], BF16)
    VT_d = ctx.dtmp("att_VT", [D, T], BF16)

    emit_ln(ctx, x_dram, xln_dram=zT_d)

    WqT = ctx.din(pfx + "_WqT", [D, D], BF16)
    WkT = ctx.din(pfx + "_WkT", [D, D], BF16)
    WvT = ctx.din(pfx + "_WvT", [D, D], BF16)
    sch.wait("gpsimd", "V")
    slabs = {}
    for wi, Wd in enumerate([WqT, WkT, WvT]):
        for dt in range(NDT):
            sl = ABIG[:, (wi * NDT + dt) * 1024:(wi * NDT + dt + 1) * 1024]
            slabs[(wi, dt)] = sl
            sch.dma("gpsimd", sl, Wd[dt * 128:(dt + 1) * 128, :])
    sch.wait_queue("tensor", "gpsimd")

    pool = PsumPool(sch, [PS[:, i * 512:(i + 1) * 512] for i in range(4)])
    stage = [ABIG[:, 24576 + i * 512:25088 + i * 512] for i in range(2)]
    st_done = {}
    for wi, (dst, bcolv, scolv) in enumerate(
            [(QT_d, bqs, qscale), (KT_d, bk, None), (VT_d, bv, None)]):
        for hp in range(NDT):
            for c in range(NTC):
                idx = (wi * NDT + hp) * NTC + c
                ps = pool.acquire()
                for dt in range(NDT):
                    t_mm(sch, ps, slabs[(wi, dt)][:, hp * 128:(hp + 1) * 128],
                         ctx.xlnT(dt)[:, c * 512:(c + 1) * 512],
                         dt == 0, dt == NDT - 1)
                sg = stage[idx % 2]
                sch.wait("scalar", "T")
                if idx >= 2:
                    sch.wait_tok("scalar", st_done[idx - 2])
                sc = s_act(sch, sg, ps, AF.Identity, bias=bcolv[:, hp:hp + 1],
                           scale=(scolv[:, hp:hp + 1] if scolv is not None else None))
                pool.release("SC", sc)
                sch.wait("sync", "SC", sc)
                st_done[idx] = sch.dma(
                    "sync", dst[hp * 128:(hp + 1) * 128, c * 512:(c + 1) * 512], sg)

    sch.barrier()

    KTa = [ABIG[:, 0:4096], ABIG[:, 4096:8192]]
    QTa = [ABIG[:, 8192:12288], ABIG[:, 12288:16384]]
    VTs = [ABIG[:, 16384:20480], ABIG[:, 20480:24576]]
    VA = [ABIG[:, 24576 + i * 2080:24576 + (i + 1) * 2080] for i in range(2)]
    eTb = [ABIG[:, 28736 + i * 512:29248 + i * 512] for i in range(2)]
    astgs = [ABIG[:, 30272:30784], ABIG[:, 35392:35904]]
    afins = [ABIG[:, 30784:31296], ABIG[:, 35904:36416]]
    rrBB = [AF_[0:64, 8192:8704], AF_[0:64, 8704:9216]]
    vrFs = [AF_[64:65, 512:1024], AF_[64:65, 4096:4608]]
    rsc = ctx.dtmp("att_rsc", [2, 512], F32)
    SQB = ABIG[:, 31296:35392]
    rrF = AF_[:, 512:1024]
    qbrow = AF_[0:1, 0:4096]
    knrow = AF_[0:1, 4096:8192]
    knmax = AF_[0:1, 8192:8193]

    buf_free = {0: 0, 1: 0}
    flat_done = {}
    attnT = lambda ct: AMID[:, ct * T:(ct + 1) * T]

    prep_state = {"last_flat": None, "sq_T": 0, "nbank": {}}

    def prep(h):
        b = h % 2
        if buf_free[b]:
            sch.wait("sync", "T", buf_free[b])
        kld = sch.dma("sync", KTa[b][0:64, :], KT_d[h * 64:(h + 1) * 64, :])
        qld = sch.dma("sync", QTa[b][0:64, :], QT_d[h * 64:(h + 1) * 64, :])
        vld = sch.dma("sync", VTs[b][0:64, :], VT_d[h * 64:(h + 1) * 64, :])
        if h < 2:
            sch.wait_tok("vector", kld)
            v_memset(sch, KTa[b][64:65, :], 1.0)
            for kt in range(32):
                v_memset(sch, VA[b][:, kt * 65 + 64:kt * 65 + 65], 1.0)
        sch.wait_tok("tensor", vld)
        ev_prev = {}
        tr_done = {}
        for kt in range(32):
            slot = kt % 2
            pstr = PS[:, 1024 + slot * 512:1024 + slot * 512 + 32].bitcast(BF16)
            if kt >= 2:
                sch.wait("tensor", "V", ev_prev[kt - 2])
            tr_done[kt] = t_tr(sch, pstr, VTs[b][0:64, kt * 128:(kt + 1) * 128],
                               ctx.IDEN[0:64, 0:64])
            sch.wait("vector", "T", tr_done[kt])
            ev_prev[kt] = v_copy(sch, VA[b][:, kt * 65:kt * 65 + 64], pstr)
        if prep_state["sq_T"]:
            sch.wait("scalar", "T", prep_state["sq_T"])
        if prep_state["last_flat"] is not None:
            sch.wait_tok("scalar", prep_state["last_flat"])
            sch.wait_tok("vector", prep_state["last_flat"])
        sch.wait_tok("scalar", kld)
        sqk = s_act(sch, SQB[0:64, :], KTa[b][0:64, :], AF.Square)
        sch.wait("tensor", "SC", sqk)
        ev = prep_state["nbank"]
        for c8 in range(8):
            bank = PS[0:1, (c8 % 2) * 512:(c8 % 2) * 512 + 512]
            if (c8 % 2) in ev:
                sch.wait("tensor", "SC", ev[c8 % 2])
            mm = t_mm(sch, bank, ctx.ONE[0:64, 0:1], SQB[0:64, c8 * 512:(c8 + 1) * 512],
                      True, True)
            sch.wait("scalar", "T", mm)
            ev[c8 % 2] = s_act(sch, knrow[:, c8 * 512:(c8 + 1) * 512], bank, AF.Ln)
        sch.wait("vector", "SC", ev[1])
        vkm = v_red(sch, knmax, knrow, ALU.max)
        sch.wait_tok("scalar", qld)
        sch.wait("scalar", "T")
        sqq = s_act(sch, SQB[0:64, :], QTa[b][0:64, :], AF.Square)
        sch.wait("tensor", "SC", sqq)
        for c8 in range(8):
            bank = PS[0:1, (c8 % 2) * 512:(c8 % 2) * 512 + 512]
            sch.wait("tensor", "SC", ev[c8 % 2])
            mm = t_mm(sch, bank, ctx.ONE[0:64, 0:1], SQB[0:64, c8 * 512:(c8 + 1) * 512],
                      True, True)
            sch.wait("scalar", "T", mm)
            ev[c8 % 2] = s_act(sch, qbrow[:, c8 * 512:(c8 + 1) * 512], bank, AF.Ln)
        prep_state["sq_T"] = sch.cnt["T"]
        sch.wait("vector", "SC", ev[0])
        sch.wait("vector", "SC", ev[1])
        sch.wait("vector", "V", vkm)
        vq = v_ts(sch, qbrow, qbrow, knmax[0:1, 0:1], None, ALU.add)
        sch.wait("scalar", "V", vq)
        se = s_act(sch, qbrow, qbrow, AF.Exp, scale=ctx.const_col(0.5)[0:1, :])
        sch.wait("vector", "SC", se)
        vrow = v_ts(sch, qbrow, qbrow, -1.02, 70.0, ALU.mult, ALU.add)
        sch.wait("gpsimd", "V", vrow)
        flat_done[h] = sch.dma("gpsimd", QTa[b][64:65, 0:4096], qbrow)
        prep_state["last_flat"] = flat_done[h]

    def stream(h):
        b = h % 2
        ct = h // 2
        pbase = (h % 2) * 64
        sch.wait_tok("tensor", flat_done[h])
        av_done = {}
        exp_done = {}
        st_done = {}
        kg = 0
        for qc in range(NTC):
            for kt in range(32):
                slot = kg % 2
                ps_sT = PS[:, 2048 + slot * 512:2560 + slot * 512]
                if kg >= 2:
                    sch.wait("tensor", "SC", exp_done[kg - 2])
                st_done[kg] = t_mm(
                    sch, ps_sT, KTa[b][0:65, kt * 128:(kt + 1) * 128],
                    QTa[b][0:65, qc * 512:(qc + 1) * 512], True, True)
                if kg >= 1 and (kg - 1) not in av_done:
                    pk = kg - 1
                    sch.wait("tensor", "SC", exp_done[pk])
                    if (pk % 32) == 0 and prep_state.get("pss_sa") is not None:
                        sch.wait("tensor", "SC", prep_state["pss_sa"])
                        sch.wait("tensor", "V", prep_state["pss_vr"])
                    av_done[pk] = t_mm(
                        sch, PSS[0:65, 0:512],
                        VA[b][:, (pk % 32) * 65:(pk % 32) * 65 + 65],
                        eTb[pk % 2], (pk % 32) == 0, (pk % 32) == 31)
                sch.wait("scalar", "T", st_done[kg])
                if kg >= 2:
                    sch.wait("scalar", "T", av_done[kg - 2])
                exp_done[kg] = s_act(sch, eTb[slot], ps_sT, AF.Exp)
                kg += 1
            pk = kg - 1
            sch.wait("tensor", "SC", exp_done[pk])
            av_done[pk] = t_mm(sch, PSS[0:65, 0:512],
                               VA[b][:, 31 * 65:31 * 65 + 65],
                               eTb[pk % 2], False, True)
            q2 = qc % 2
            sch.wait("vector", "T", av_done[pk])
            if prep_state.get(("dd", q2)) is not None:
                sch.wait_tok("vector", prep_state[("dd", q2)])
            if prep_state.get(("rw", q2)) is not None:
                sch.wait_tok("vector", prep_state[("rw", q2)])
            vr = v_recip(sch, vrFs[q2], PSS[64:65, 0:512])
            prep_state["pss_vr"] = vr
            sch.wait("gpsimd", "V", vr)
            if prep_state.get(("rr", q2)) is not None:
                sch.wait_tok("gpsimd", prep_state[("rr", q2)])
            w1 = sch.dma("gpsimd", rsc[q2:q2 + 1, :], vrFs[q2])
            prep_state[("rw", q2)] = w1
            sch.wait_tok("gpsimd", w1)
            if prep_state.get(("rbu", q2)) is not None:
                sch.wait("gpsimd", "V", prep_state[("rbu", q2)])
            r1 = sch.dma("gpsimd", rrBB[q2],
                         rsc[q2:q2 + 1, :].to_broadcast((64, 512)))
            prep_state[("rr", q2)] = r1
            sch.wait("scalar", "T", av_done[pk])
            if prep_state.get(("sau", q2)) is not None:
                sch.wait("scalar", "V", prep_state[("sau", q2)])
            sa = s_act(sch, astgs[q2][0:64, :], PSS[0:64, 0:512], AF.Identity)
            prep_state["pss_sa"] = sa
            sch.wait("vector", "SC", sa)
            sch.wait_tok("vector", r1)
            vf = v_tt(sch, afins[q2][0:64, :], astgs[q2][0:64, :], rrBB[q2],
                      ALU.mult)
            prep_state[("rbu", q2)] = vf
            prep_state[("sau", q2)] = vf
            sch.wait("gpsimd", "V", vf)
            prep_state[("dd", q2)] = sch.dma(
                "gpsimd", attnT(ct)[pbase:pbase + 64, qc * 512:(qc + 1) * 512],
                afins[q2][0:64, :])
        buf_free[b] = sch.cnt["T"]

    for h in range(NH + 1):
        if h < NH:
            prep(h)
        if h >= 1:
            stream(h - 1)

    sch.barrier()

    WoT = ctx.din(pfx + "_WoT", [D, D], BF16)
    wslab = [ABIG[:, dt * 1024:(dt + 1) * 1024] for dt in range(NDT)]
    for dt in range(NDT):
        sch.dma("gpsimd", wslab[dt], WoT[dt * 128:(dt + 1) * 128, :])
    sch.wait_queue("tensor", "gpsimd")
    zslot = [ABIG[:, 8192 + i * 512:8704 + i * 512] for i in range(2)]
    pool4 = PsumPool(sch, [PS[:, i * 512:(i + 1) * 512] for i in range(4)])
    tmp = AF_[:, 4096:4608]
    Ab4 = AF_[:, 4608:5120]
    Bb4 = AF_[:, 5120:5632]
    xslot = [AF_[:, 5632 + i * 512:6144 + i * 512] for i in range(2)]
    scr4 = ctx.dtmp("att_lnrow", [2, 512], F32)
    mu = AF_[0:1, 6656:7168]
    var = AF_[0:1, 7168:7680]
    tmr = AF_[0:1, 7680:8192]
    a_r = AF_[0:1, 8192:8704]
    b_r = AF_[0:1, 8704:9216]
    v_stats_prev = 0
    xin_free = {}
    out_free = {}
    z_free = {}
    for tc in range(NTC):
        for m in range(NDT):
            zidx = tc * NDT + m
            if zidx >= 2:
                sch.wait("sync", "V", z_free[zidx - 2])
            zld = sch.dma("sync", zslot[m % 2],
                          zT_d[m * 128:(m + 1) * 128, tc * 512:(tc + 1) * 512])
            ps = pool4.acquire()
            for ctt in range(NDT):
                t_mm(sch, ps, wslab[ctt][:, m * 128:(m + 1) * 128],
                     attnT(ctt)[:, tc * 512:(tc + 1) * 512], ctt == 0, ctt == NDT - 1)
            y = AF_[:, m * 512:(m + 1) * 512]
            sch.wait("vector", "T")
            sch.wait_tok("vector", zld)
            vy = v_stt(sch, y, ps, bo[:, m:m + 1], zslot[m % 2], ALU.add, ALU.add)
            z_free[zidx] = vy
            pool4.release("V", vy)
            sch.wait("scalar", "V", vy)
            sq = s_act(sch, tmp, y, AF.Square)
            sch.wait("tensor", "SC", sq)
            if m == 0 and v_stats_prev:
                sch.wait("tensor", "V", v_stats_prev)
            t_mm(sch, PSS[0:1, 0:512], ctx.ONEF[:, 0:1], y, m == 0, m == NDT - 1)
            t_mm(sch, PS[0:1, 3072:3584], ctx.ONEF[:, 0:1], tmp,
                 m == 0, m == NDT - 1)
        sch.wait("vector", "T")
        v_ts(sch, mu, PSS[0:1, 0:512], 1.0 / D, None, ALU.mult)
        v_ts(sch, var, PS[0:1, 3072:3584], 1.0 / D, None, ALU.mult)
        v_stats_prev = sch.cnt["V"]
        v_tt(sch, tmr, mu, mu, ALU.mult)
        vv = v_tt(sch, var, var, tmr, ALU.subtract)
        sch.wait("scalar", "V", vv)
        s_act(sch, var, var, AF.Ln, bias=ctx.const_col(EPS)[0:1, :])
        ss = s_act(sch, a_r, var, AF.Exp, scale=ctx.const_col(-0.5)[0:1, :])
        sch.wait("vector", "SC", ss)
        v_tt(sch, b_r, mu, a_r, ALU.mult)
        vb = v_ts(sch, b_r, b_r, -1.0, None, ALU.mult)
        sch.wait("gpsimd", "V", vb)
        w1 = sch.dma("gpsimd", scr4[0:1, :], a_r)
        w2 = sch.dma("gpsimd", scr4[1:2, :], b_r)
        sch.wait_tok("gpsimd", w1)
        sch.wait_tok("gpsimd", w2)
        sch.wait("gpsimd", "V")
        ra = sch.dma("gpsimd", Ab4, scr4[0:1, :].to_broadcast((128, 512)))
        rb = sch.dma("gpsimd", Bb4, scr4[1:2, :].to_broadcast((128, 512)))
        sch.wait_tok("vector", ra)
        sch.wait_tok("vector", rb)
        for m in range(NDT):
            idx = tc * NDT + m
            if idx >= 2:
                sch.wait("sync", "V", xin_free[idx - 2])
            xld = sch.dma("sync", xslot[m % 2],
                          x_dram[m * 128:(m + 1) * 128, tc * 512:(tc + 1) * 512])
            y = AF_[:, m * 512:(m + 1) * 512]
            v_tt(sch, tmp, y, Ab4, ALU.mult)
            v_tt(sch, tmp, tmp, Bb4, ALU.add)
            v_ts(sch, tmp, tmp, gam[:, m:m + 1], bet[:, m:m + 1],
                 ALU.mult, ALU.add)
            sch.wait_tok("vector", xld)
            if idx >= 2:
                sch.wait_tok("vector", out_free[idx - 2])
            vo = v_tt(sch, y, tmp, xslot[m % 2], ALU.add)
            xin_free[idx] = vo
            sch.wait("sync", "V", vo)
            out_free[idx] = sch.dma(
                "sync", xo_dram[m * 128:(m + 1) * 128, tc * 512:(tc + 1) * 512], y)



def emit_ttt(ctx, x_dram, xo_dram):
    nc, sch = ctx.nc, ctx.sch
    ABIG, AMID, AF_, PS, PSS = ctx.ABIG, ctx.AMID, ctx.AF, ctx.PS, ctx.PSS
    pfx = "L1B"
    SC2 = 2.0 / 4096.0

    dKT = ctx.dtmp("tt_kT", [D, T], BF16)
    dKR = ctx.dtmp("tt_kR", [T, D], BF16)
    dVT = ctx.dtmp("tt_vT", [D, T], BF16)
    dQT = ctx.dtmp("tt_qT", [D, T], BF16)
    dA1T = ctx.dtmp("tt_a1T", [D, T], BF16)
    dA1R = ctx.dtmp("tt_a1R", [T, D], BF16)
    dSDR = ctx.dtmp("tt_sdR", [T, D], BF16)
    dD2T = ctx.dtmp("tt_d2T", [D, T], BF16)
    dD2R = ctx.dtmp("tt_d2R", [T, D], BF16)
    dD1R = ctx.dtmp("tt_d1R", [T, D], BF16)
    dHT = ctx.dtmp("tt_hT", [D, T], BF16)
    dH2T = ctx.dtmp("tt_h2T", [D, T], BF16)
    AR1_in = ctx.dtmp("tt_ar1_in", [D + 1, D], F32)
    AR1_out = ctx.dtmp("tt_ar1_out", [D + 1, D], F32)
    AR2_in = ctx.dtmp("tt_ar2_in", [D, D], F32)
    AR2_out = ctx.dtmp("tt_ar2_out", [D, D], F32)

    WgD = load_cols(ctx, pfx + "_WgD", NDT)
    WgL = load_cols(ctx, pfx + "_WgL", NDT)
    bgD = ctx.din(pfx + "_bgD", [1, 1], F32)
    bgL = ctx.din(pfx + "_bgL", [1, 1], F32)
    i_bg = ctx.col(2)
    sch.dma("sync", ctx.ACOL[0:1, i_bg:i_bg + 1], bgD)
    sch.dma("sync", ctx.ACOL[0:1, i_bg + 1:i_bg + 2], bgL)
    i_xs = ctx.col(NDT)
    XS = ctx.ACOL[:, i_xs:i_xs + NDT]
    i_g = ctx.col(2)
    GC = ctx.ACOL[:, i_g:i_g + 2]
    cinv = ctx.const_col(1.0 / 32768.0)
    grow = ctx.dtmp("tt_grow", [1, 2], F32)

    emit_ln(ctx, x_dram)

    stg = [ABIG[:, 32768 + i * 512:33280 + i * 512] for i in range(2)]
    aux = [ABIG[:, 33792 + i * 512:34304 + i * 512] for i in range(2)]
    st2 = [ABIG[:, 34816 + i * 512:35328 + i * 512] for i in range(2)]
    trst = [ABIG[:, 35840 + i * 128:35968 + i * 128] for i in range(2)]
    W0T = [ABIG[:, dt * 1024:(dt + 1) * 1024] for dt in range(NDT)]
    W1S = [ABIG[:, 8192 + dt * 1024:9216 + dt * 1024] for dt in range(NDT)]
    W1T = [ABIG[:, 16384 + dt * 1024:17408 + dt * 1024] for dt in range(NDT)]
    WOUT = [ABIG[:, 24576 + dt * 1024:25600 + dt * 1024] for dt in range(NDT)]
    ROT = W0T
    W0nT = W0T

    pool = PsumPool(sch, [PS[:, i * 512:(i + 1) * 512] for i in range(4)])
    state = {"st": {}, "aux": {}, "st2": {}}

    def load_slabs(dname, region):
        Wd = ctx.din(dname, [D, D], BF16)
        sch.wait("gpsimd", "T")
        toks = []
        for dt in range(NDT):
            toks.append(sch.dma("gpsimd", region[dt], Wd[dt * 128:(dt + 1) * 128, :]))
        return toks

    def pass_T(lhs_slabs, rhs_src, evac):
        for c in range(NTC):
            for m in range(NDT):
                ps = pool.acquire()
                for dt in range(NDT):
                    rhs = rhs_src(dt, c)
                    t_mm(sch, ps, lhs_slabs[dt][:, m * 128:(m + 1) * 128], rhs,
                         dt == 0, dt == NDT - 1)
                evac(m, c, ps)

    def stream_blocks(dram, tag, tmin=None):
        slots = [AMID[:, i * 4096:(i + 1) * 4096] for i in range(2)]
        def load(c):
            blk = slots[c % 2]
            if tmin is not None and c < 2:
                sch.wait("sync", "T", tmin)
            key = (tag, c - 2)
            if key in state:
                sch.wait("sync", "T", state[key])
            cc = c % NTC
            tok = sch.dma("sync", blk,
                          dram[:, cc * 512:(cc + 1) * 512]
                          .rearrange("(dt p) t -> p dt t", p=128))
            return blk, tok
        return load

    def mk_evac_sc(dst_dram, func):
        def evac(m, c, ps):
            sl = stg[(m * NTC + c) % 2]
            sch.wait("scalar", "T")
            tok = state["st"].get((m * NTC + c) % 2)
            if tok is not None:
                sch.wait_tok("scalar", tok)
            scc = s_act(sch, sl, ps, func)
            pool.release("SC", scc)
            sch.wait("sync", "SC", scc)
            state["st"][(m * NTC + c) % 2] = sch.dma(
                "sync", dst_dram[m * 128:(m + 1) * 128, c * 512:(c + 1) * 512], sl)
        return evac

    sch.wait("gpsimd", "V")
    load_slabs(pfx + "_WkT", ROT)
    load_slabs(pfx + "_WvT", W1S)
    load_slabs(pfx + "_WqT", W1T)
    sch.wait("tensor", "V")
    sch.wait_queue("tensor", "gpsimd")
    pass_T(ROT, lambda dt, c: ctx.xlnT(dt)[:, c * 512:(c + 1) * 512],
           mk_evac_sc(dKT, AF.Identity))

    def pass_row_from_xln(rhs_slabs, evac):
        for tt in range(32):
            for n in range(2):
                ps = pool.acquire()
                for dt in range(NDT):
                    t_mm(sch, ps, ctx.xlnT(dt)[:, tt * 128:(tt + 1) * 128],
                         rhs_slabs[dt][:, n * 512:(n + 1) * 512],
                         dt == 0, dt == NDT - 1)
                evac(tt, n, ps)

    def mk_evac_row(dst_dram, func):
        def evac(tt, n, ps):
            sl = stg[(tt * 2 + n) % 2]
            sch.wait("scalar", "T")
            tok = state["st"].get((tt * 2 + n) % 2)
            if tok is not None:
                sch.wait_tok("scalar", tok)
            scc = s_act(sch, sl, ps, func)
            pool.release("SC", scc)
            sch.wait("sync", "SC", scc)
            state["st"][(tt * 2 + n) % 2] = sch.dma(
                "sync", dst_dram[tt * 128:(tt + 1) * 128, n * 512:(n + 1) * 512], sl)
        return evac

    pass_row_from_xln(ROT, mk_evac_row(dKR, AF.Identity))

    pass_T(W1S, lambda dt, c: ctx.xlnT(dt)[:, c * 512:(c + 1) * 512],
           mk_evac_sc(dVT, AF.Identity))
    pass_T(W1T, lambda dt, c: ctx.xlnT(dt)[:, c * 512:(c + 1) * 512],
           mk_evac_sc(dQT, AF.Identity))

    sch.wait("vector", "T")
    for dt in range(NDT):
        v_red(sch, XS[:, dt:dt + 1], ctx.xlnT(dt), ALU.add)

    load_slabs(pfx + "_W0T", W0T)
    load_slabs(pfx + "_W1T", W1T)
    load_slabs(pfx + "_W1", W1S)
    load_slabs(pfx + "_WoutT", WOUT)
    sch.wait_queue("tensor", "gpsimd")
    sch.barrier()

    ldK = stream_blocks(dKT, "kT")
    def evacs_pre1T(m, c, ps):
        mk_evac_sc(dA1T, AF.Silu)(m, c, ps)
    for c in range(NTC):
        blk, tok = ldK(c)
        sch.wait_tok("tensor", tok)
        for m in range(NDT):
            ps = pool.acquire()
            for dt in range(NDT):
                t_mm(sch, ps, W0T[dt][:, m * 128:(m + 1) * 128],
                     blk[:, dt * 512:(dt + 1) * 512], dt == 0, dt == NDT - 1)
            evacs_pre1T(m, c, ps)
        state[("kT", c)] = sch.cnt["T"]

    for c in range(NTC):
        blk, tok = ldK(c + NTC)
        sch.wait_tok("tensor", tok)
        for sub in range(4):
            tt = c * 4 + sub
            for n in range(2):
                ps = pool.acquire()
                for dt in range(NDT):
                    t_mm(sch, ps, blk[:, dt * 512 + sub * 128:dt * 512 + sub * 128 + 128],
                         W0T[dt][:, n * 512:(n + 1) * 512], dt == 0, dt == NDT - 1)
                i2 = (tt * 2 + n) % 2
                sch.wait("scalar", "T")
                sgm = st2[i2]
                if ("sgm", i2) in state:
                    sch.wait("scalar", "V", state[("sgm", i2)])
                s1 = s_act(sch, sgm, ps, AF.Sigmoid)
                sch.wait("vector", "SC", s1)
                sch.wait_tok("vector", state["st"].get(i2))
                sch.wait_tok("vector", state["aux"].get(i2))
                v_tt(sch, stg[i2], ps, sgm, ALU.mult)
                tmp4 = AF_[:, 3072 + i2 * 512:3584 + i2 * 512]
                v_tt(sch, tmp4, stg[i2], sgm, ALU.mult)
                v_tt(sch, tmp4, stg[i2], tmp4, ALU.subtract)
                s2 = v_tt(sch, aux[i2], sgm, tmp4, ALU.add)
                state[("sgm", i2)] = s2
                pool.release("V", s2)
                sch.wait("sync", "V", s2)
                state["st"][i2] = sch.dma(
                    "sync", dA1R[tt * 128:(tt + 1) * 128, n * 512:(n + 1) * 512], stg[i2])
                state["aux"][i2] = sch.dma(
                    "sync", dSDR[tt * 128:(tt + 1) * 128, n * 512:(n + 1) * 512], aux[i2])
        state[("kT", c + NTC)] = sch.cnt["T"]

    sch.wait_queue("sync", "sync")
    ldA1 = stream_blocks(dA1T, "a1T")
    for c in range(NTC):
        blk, tok = ldA1(c)
        sch.wait_tok("tensor", tok)
        for m in range(NDT):
            ps = pool.acquire()
            for dt in range(NDT):
                t_mm(sch, ps, W1T[dt][:, m * 128:(m + 1) * 128],
                     blk[:, dt * 512:(dt + 1) * 512], dt == 0, dt == NDT - 1)
            i2 = (m * NTC + c) % 2
            if ("vtfree", i2) in state:
                sch.wait("sync", "V", state[("vtfree", i2)])
            sch.wait_tok("sync", state["aux"].get(i2))
            tokv = sch.dma("sync", aux[i2],
                           dVT[m * 128:(m + 1) * 128, c * 512:(c + 1) * 512])
            state["aux"][i2] = tokv
            sch.wait("vector", "T")
            sch.wait_tok("vector", tokv)
            sch.wait_tok("vector", state["st"].get(i2))
            v_tt(sch, stg[i2], ps, aux[i2], ALU.subtract)
            v_ts(sch, stg[i2], stg[i2], 5.0, -5.0, ALU.min, ALU.max)
            vcl = v_ts(sch, stg[i2], stg[i2], SC2, None, ALU.mult)
            state[("vtfree", i2)] = vcl
            pool.release("V", vcl)
            sch.wait("sync", "V", vcl)
            state["st"][i2] = sch.dma(
                "sync", dD2T[m * 128:(m + 1) * 128, c * 512:(c + 1) * 512], stg[i2])
            for j in range(4):
                jb = (m * NTC * 4 + c * 4 + j) % 2
                ptr = PS[:, 2048 + jb * 512:2048 + jb * 512 + 64].bitcast(BF16)
                tokp = state.get(("trv", jb))
                sch.wait("tensor", "V", vcl)
                if tokp is not None:
                    sch.wait("tensor", "V", tokp)
                ttr = t_tr(sch, ptr, stg[i2][0:128, j * 128:(j + 1) * 128],
                           ctx.IDEN[:, :])
                sch.wait("vector", "T", ttr)
                tokd = state.get(("trd", jb))
                if tokd is not None:
                    sch.wait_tok("vector", tokd)
                vc = v_copy(sch, trst[jb], ptr)
                state[("trv", jb)] = vc
                sch.wait("sync", "V", vc)
                state[("trd", jb)] = sch.dma(
                    "sync", dD2R[c * 512 + j * 128:c * 512 + (j + 1) * 128,
                                 m * 128:(m + 1) * 128], trst[jb])
        state[("a1T", c)] = sch.cnt["T"]

    sch.wait_queue("sync", "sync")
    ldD2 = stream_blocks(dD2T, "d2T")
    for c in range(NTC):
        blk, tok = ldD2(c)
        sch.wait_tok("tensor", tok)
        for sub in range(4):
            tt = c * 4 + sub
            for n in range(2):
                ps = pool.acquire()
                for dt in range(NDT):
                    t_mm(sch, ps, blk[:, dt * 512 + sub * 128:dt * 512 + sub * 128 + 128],
                         W1S[dt][:, n * 512:(n + 1) * 512], dt == 0, dt == NDT - 1)
                i2 = (tt * 2 + n) % 2
                if ("sdsfree", i2) in state:
                    sch.wait("sync", "V", state[("sdsfree", i2)])
                sch.wait_tok("sync", state["aux"].get(i2))
                toks = sch.dma("sync", aux[i2],
                               dSDR[tt * 128:(tt + 1) * 128, n * 512:(n + 1) * 512])
                state["aux"][i2] = toks
                sch.wait("vector", "T")
                sch.wait_tok("vector", toks)
                sch.wait_tok("vector", state["st"].get(i2))
                vm = v_tt(sch, stg[i2], ps, aux[i2], ALU.mult)
                state[("sdsfree", i2)] = vm
                pool.release("V", vm)
                sch.wait("sync", "V", vm)
                state["st"][i2] = sch.dma(
                    "sync", dD1R[tt * 128:(tt + 1) * 128, n * 512:(n + 1) * 512], stg[i2])
        state[("d2T", c)] = sch.cnt["T"]

    def grad_pass(lhs_dram, rhs_dram, out_dram):
        sch.wait_queue("sync", "sync")
        for n in range(2):
            rblk = AMID[:, 0:16384]
            sch.wait("sync", "T")
            rt = sch.dma("sync", rblk,
                         rhs_dram[:, n * 512:(n + 1) * 512]
                         .rearrange("(tt p) e -> p tt e", p=128))
            for m in range(NDT):
                lblk = AMID[:, 16384 + (m % 2) * 4096:16384 + (m % 2) * 4096 + 4096]
                g = state.get(("glb", m % 2), 0)
                if g:
                    sch.wait("sync", "T", g)
                lt = sch.dma("sync", lblk,
                             lhs_dram[:, m * 128:(m + 1) * 128]
                             .rearrange("(tt p) e -> p tt e", p=128))
                ps = pool.acquire()
                sch.wait_tok("tensor", rt)
                sch.wait_tok("tensor", lt)
                for tt in range(32):
                    t_mm(sch, ps, lblk[:, tt * 128:(tt + 1) * 128],
                         rblk[:, tt * 512:(tt + 1) * 512], tt == 0, tt == 31)
                state[("glb", m % 2)] = sch.cnt["T"]
                sl = AF_[:, (m % 2) * 512:(m % 2) * 512 + 512]
                sch.wait("vector", "T")
                tokd = state.get(("gev", m % 2))
                if tokd is not None:
                    sch.wait_tok("vector", tokd)
                vc = v_copy(sch, sl, ps)
                pool.release("V", vc)
                sch.wait("sync", "V", vc)
                state[("gev", m % 2)] = sch.dma(
                    "sync", out_dram[m * 128:(m + 1) * 128,
                                     n * 512:(n + 1) * 512], sl)
            state[("grb", 0)] = sch.cnt["T"]

    def do_allreduce(ins, outs):
        sch.wait_queue("gpsimd", "sync")
        sch.wait_queue("gpsimd", "gpsimd")
        sch.sem("CC")
        sch.cnt["CC"] += 1
        sch.q["gpsimd"].append(("op", lambda e, i=ins, o=outs:
                                e.collective_compute(
            "AllReduce", ALU.add, replica_groups=[list(range(NCORES))],
            ins=[i.opt()], outs=[o.opt()]), "CC", 1))
        return sch.cnt["CC"]

    sch.wait("gpsimd", "V")
    for j in range(NDT):
        sch.dma("gpsimd", AR1_in[D:D + 1, j * 128:(j + 1) * 128], XS[:, j:j + 1])
    grad_pass(dKR, dD1R, AR1_in)
    cc1 = do_allreduce(AR1_in, AR1_out)
    grad_pass(dA1R, dD2R, AR2_in)
    t_g1 = sch.cnt["T"]
    cc2 = do_allreduce(AR2_in, AR2_out)

    sch.wait("sync", "CC", cc1)
    for j in range(NDT):
        tok = sch.dma("sync", XS[:, j:j + 1],
                      AR1_out[D:D + 1, j * 128:(j + 1) * 128])
        state[("xs", j)] = tok
    sch.wait("vector", "CC", cc1)
    for j in range(NDT):
        sch.wait_tok("vector", state[("xs", j)])
    r0 = AF_[:, 0:8]
    rr = AF_[:, 8:10]
    for gi, wg in enumerate([WgD, WgL]):
        v_tt(sch, r0, XS, wg, ALU.mult)
        v_red(sch, rr[:, gi:gi + 1], r0, ALU.add)
    vv = sch.cnt["V"]
    sch.wait("tensor", "V", vv)
    t_mm(sch, PS[0:1, 3072:3073], ctx.ONEF[:, 0:1], rr[:, 0:1], True, True)
    tg = t_mm(sch, PS[0:1, 3073:3074], ctx.ONEF[:, 0:1], rr[:, 1:2],
              True, True)
    sch.wait("scalar", "T", tg)
    sa = s_act(sch, AF_[0:1, 16:17], PS[0:1, 3072:3073], AF.Sigmoid,
               bias=ctx.ACOL[0:1, i_bg:i_bg + 1], scale=cinv[0:1, :])
    sb = s_act(sch, AF_[0:1, 17:18], PS[0:1, 3073:3074], AF.Sigmoid,
               bias=ctx.ACOL[0:1, i_bg + 1:i_bg + 2], scale=cinv[0:1, :])
    sch.wait("vector", "SC", sb)
    v_ts(sch, AF_[0:1, 18:19], AF_[0:1, 16:17], -1.0, 1.0, ALU.mult, ALU.add)
    vg = v_ts(sch, AF_[0:1, 19:20], AF_[0:1, 17:18], -1.0, None, ALU.mult)
    sch.wait("sync", "V", vg)
    wg1 = sch.dma("sync", grow[0:1, 0:1], AF_[0:1, 18:19])
    wg2 = sch.dma("sync", grow[0:1, 1:2], AF_[0:1, 19:20])
    sch.wait_tok("sync", wg1)
    sch.wait_tok("sync", wg2)
    gb = sch.dma("sync", GC, grow[0:1, 0:2].to_broadcast((128, 2)))

    def build_Wn(src_out, slabs, ccv):
        sch.wait("sync", "CC", ccv)
        sch.wait_tok("vector", gb)
        vw = 0
        for dt in range(NDT):
            gl = AF_[:, 1024:2048]
            if vw:
                sch.wait("sync", "V", vw)
            tok = sch.dma("sync", gl, src_out[dt * 128:(dt + 1) * 128, :])
            sch.wait_tok("vector", tok)
            v_ts(sch, gl, gl, 1.0, -1.0, ALU.min, ALU.max)
            t1 = AF_[:, 2048:3072]
            v_ts(sch, t1, slabs[dt], GC[:, 0:1], None, ALU.mult)
            vw = v_stt(sch, slabs[dt], gl, GC[:, 1:2], t1, ALU.mult, ALU.add)
        return vw

    vw0 = build_Wn(AR1_out, W0T, cc1)

    ldQ = stream_blocks(dQT, "qT", tmin=t_g1)
    sch.wait("tensor", "V", vw0)
    for c in range(NTC):
        blk, tok = ldQ(c)
        sch.wait_tok("tensor", tok)
        for m in range(NDT):
            ps = pool.acquire()
            for dt in range(NDT):
                t_mm(sch, ps, W0T[dt][:, m * 128:(m + 1) * 128],
                     blk[:, dt * 512:(dt + 1) * 512], dt == 0, dt == NDT - 1)
            mk_evac_sc(dHT, AF.Silu)(m, c, ps)
        state[("qT", c)] = sch.cnt["T"]
    vw1 = build_Wn(AR2_out, W1T, cc2)
    ldH = stream_blocks(dHT, "hT")
    sch.wait("tensor", "V", vw1)
    for c in range(NTC):
        blk, tok = ldH(c)
        sch.wait_tok("tensor", tok)
        for m in range(NDT):
            ps = pool.acquire()
            for dt in range(NDT):
                t_mm(sch, ps, W1T[dt][:, m * 128:(m + 1) * 128],
                     blk[:, dt * 512:(dt + 1) * 512], dt == 0, dt == NDT - 1)
            mk_evac_sc(dH2T, AF.Identity)(m, c, ps)
        state[("hT", c)] = sch.cnt["T"]
    ldH2 = stream_blocks(dH2T, "h2T")
    for c in range(NTC):
        blk, tok = ldH2(c)
        sch.wait_tok("tensor", tok)
        for m in range(NDT):
            ps = pool.acquire()
            for dt in range(NDT):
                t_mm(sch, ps, WOUT[dt][:, m * 128:(m + 1) * 128],
                     blk[:, dt * 512:(dt + 1) * 512], dt == 0, dt == NDT - 1)
            i2 = (m * NTC + c) % 2
            tokx = state.get(("xr", i2))
            if tokx is not None:
                sch.wait("sync", "V", state[("xrfree", i2)])
            xsl = AF_[:, i2 * 512:(i2 + 1) * 512]
            tokx = sch.dma("sync", xsl,
                           x_dram[m * 128:(m + 1) * 128, c * 512:(c + 1) * 512])
            state[("xr", i2)] = tokx
            osl = AF_[:, 1024 + i2 * 512:1536 + i2 * 512]
            sch.wait("vector", "T")
            sch.wait_tok("vector", tokx)
            toko = state.get(("xo", i2))
            if toko is not None:
                sch.wait_tok("vector", toko)
            vo = v_tt(sch, osl, ps, xsl, ALU.add)
            state[("xrfree", i2)] = vo
            pool.release("V", vo)
            sch.wait("sync", "V", vo)
            state[("xo", i2)] = sch.dma(
                "sync", xo_dram[m * 128:(m + 1) * 128, c * 512:(c + 1) * 512], osl)
        state[("h2T", c)] = sch.cnt["T"]



def emit_ffn1(ctx, x_dram, xo_dram, pfx="L1F"):
    nc, sch = ctx.nc, ctx.sch
    ABIG, AMID, AF_, PS, PSS = ctx.ABIG, ctx.AMID, ctx.AF, ctx.PS, ctx.PSS
    NH = 4096
    NMO = NH // 128

    s1 = load_cols(ctx, pfx + "_s1", 32)
    s2 = load_cols(ctx, pfx + "_s2", 32)
    sh1 = load_cols(ctx, pfx + "_sh1", 32)
    sh2 = load_cols(ctx, pfx + "_sh2", 32)
    s3 = load_cols(ctx, pfx + "_s3", NDT)
    w1T = ctx.din(pfx + "_w1P", [32, 131072], BF16)
    w2T = ctx.din(pfx + "_w2P", [32, 131072], BF16)
    h1T = ctx.din(pfx + "_h1P", [32, 524288], BF16)
    h2T = ctx.din(pfx + "_h2P", [32, 524288], BF16)
    w3T = ctx.din(pfx + "_w3P", [8, 524288], BF16)

    Pb = lambda mo: AMID[:, mo * 1024:(mo + 1) * 1024]
    Qb = lambda mo: ABIG[:, mo * 1024:(mo + 1) * 1024]
    xlnB = AF_[:, 0:4096].bitcast(BF16)
    xlnBt = lambda dt: xlnB[:, dt * 1024:(dt + 1) * 1024]
    wslot = [AF_[:, 0:2048].bitcast(BF16), AF_[:, 2048:4096].bitcast(BF16)]
    w1slot = [ABIG[:, 32768 + i * 2048:32768 + (i + 1) * 2048] for i in range(2)]
    ostg = [AF_[:, 4608 + i * 512:5120 + i * 512] for i in range(2)]
    xstg = [AF_[:, 5632 + i * 512:6144 + i * 512] for i in range(2)]
    tstg = [AF_[:, 7680:8192].bitcast(BF16)[:, i * 512:(i + 1) * 512]
            for i in range(2)]

    pool = PsumPool(sch, [PS[:, i * 512:(i + 1) * 512] for i in range(6)])
    st = {}

    for B in range(4):
        sch.wait("gpsimd", "V")
        sch.wait("gpsimd", "T")
        sch.barrier()
        emit_ln(ctx, x_dram, chunks=[2 * B, 2 * B + 1], dst=xlnBt)
        sch.wait("tensor", "V")

        for mo in range(NMO):
            i2 = mo % 2
            tk = st.get(("w1l", i2))
            if tk is not None:
                sch.wait("gpsimd", "T", st[("w1f", i2)])
            t1 = sch.dma("gpsimd", w1slot[i2][:, 0:1024],
                         w1T[mo:mo + 1, :].rearrange("a (p f) -> (a p) f", p=128))
            t2 = sch.dma("gpsimd", w1slot[i2][:, 1024:2048],
                         w2T[mo:mo + 1, :].rearrange("a (p f) -> (a p) f", p=128))
            st[("w1l", i2)] = t2
            sch.wait_tok("tensor", t1)
            sch.wait_tok("tensor", t2)
            for tc in range(2):
                ps1 = pool.acquire()
                for dt in range(NDT):
                    t_mm(sch, ps1, w1slot[i2][:, dt * 128:dt * 128 + 128],
                         xlnBt(dt)[:, tc * 512:(tc + 1) * 512], dt == 0, dt == NDT - 1)
                ps2 = pool.acquire()
                for dt in range(NDT):
                    t_mm(sch, ps2, w1slot[i2][:, 1024 + dt * 128:1024 + dt * 128 + 128],
                         xlnBt(dt)[:, tc * 512:(tc + 1) * 512], dt == 0, dt == NDT - 1)
                j2 = (mo * 2 + tc) % 2
                sch.wait("scalar", "T")
                tk2 = st.get(("tst", j2))
                if tk2 is not None:
                    sch.wait("scalar", "V", tk2)
                sc = s_act(sch, tstg[j2], ps1, AF.Silu, scale=s1[:, mo:mo + 1])
                sch.wait("vector", "SC", sc)
                vv = v_stt(sch, Pb(mo)[:, tc * 512:(tc + 1) * 512], ps2,
                           s2[:, mo:mo + 1], tstg[j2], ALU.mult, ALU.mult)
                st[("tst", j2)] = vv
                cur2 = pool.cur
                pool.cur = (cur2 - 1) % len(pool.slots)
                pool.release("SC", sc)
                pool.cur = cur2
                pool.release("V", vv)
            st[("w1f", i2)] = sch.cnt["T"]
        for stage, (Wd, scol, SRC, DST) in enumerate(
                [(h1T, sh1, Pb, Qb), (h2T, sh2, Qb, Pb)]):
            sch.wait("gpsimd", "V")
            sch.wait("gpsimd", "T")
            sch.wait("tensor", "V")
            sch.wait("tensor", "SC")
            for mo in range(NMO):
                i2 = mo % 2
                tk = st.get(("wl", i2))
                if tk is not None:
                    sch.wait("gpsimd", "T", st[("wf", i2)])
                toks = [sch.dma(
                    "gpsimd", wslot[i2],
                    Wd[mo:mo + 1, :].rearrange("a (p f) -> (a p) f", p=128))]
                st[("wl", i2)] = toks[-1]
                for tk2 in toks:
                    sch.wait_tok("tensor", tk2)
                for tc in range(2):
                    ps = pool.acquire()
                    for kt in range(NMO):
                        t_mm(sch, ps, wslot[i2][:, kt * 128:(kt + 1) * 128],
                             SRC(kt)[:, tc * 512:(tc + 1) * 512],
                             kt == 0, kt == NMO - 1)
                    sch.wait("scalar", "T")
                    sc = s_act(sch, DST(mo)[:, tc * 512:(tc + 1) * 512], ps,
                               AF.Gelu, scale=scol[:, mo:mo + 1])
                    pool.release("SC", sc)
                st[("wf", i2)] = sch.cnt["T"]
        sch.wait("gpsimd", "V")
        sch.wait("gpsimd", "T")
        sch.wait("tensor", "SC")
        for mo in range(NDT):
            i2 = mo % 2
            tk = st.get(("wl3", i2))
            if tk is not None:
                sch.wait("gpsimd", "T", st[("wf3", i2)])
            toks = [sch.dma(
                "gpsimd", wslot[i2],
                w3T[mo:mo + 1, :].rearrange("a (p f) -> (a p) f", p=128))]
            st[("wl3", i2)] = toks[-1]
            for tk2 in toks:
                sch.wait_tok("tensor", tk2)
            for tc in range(2):
                ps = pool.acquire()
                for kt in range(NMO):
                    t_mm(sch, ps, wslot[i2][:, kt * 128:(kt + 1) * 128],
                         Pb(kt)[:, tc * 512:(tc + 1) * 512], kt == 0, kt == NMO - 1)
                j2 = (mo * 2 + tc) % 2
                xf = st.get(("xf", j2))
                if xf is not None:
                    sch.wait("sync", "V", xf)
                xt = sch.dma("sync", xstg[j2],
                             x_dram[mo * 128:(mo + 1) * 128,
                                    B * 1024 + tc * 512:B * 1024 + (tc + 1) * 512])
                sch.wait("vector", "T")
                sch.wait_tok("vector", xt)
                of = st.get(("of", j2))
                if of is not None:
                    sch.wait_tok("vector", of)
                vo = v_stt(sch, ostg[j2], ps, s3[:, mo:mo + 1], xstg[j2],
                           ALU.mult, ALU.add)
                st[("xf", j2)] = vo
                pool.release("V", vo)
                sch.wait("sync", "V", vo)
                st[("of", j2)] = sch.dma(
                    "sync", xo_dram[mo * 128:(mo + 1) * 128,
                                    B * 1024 + tc * 512:B * 1024 + (tc + 1) * 512],
                    ostg[j2])
            st[("wf3", i2)] = sch.cnt["T"]



def emit_mamba(ctx, x_dram, xo_dram):
    nc, sch = ctx.nc, ctx.sch
    ABIG, AMID, AF_, PS, PSS = ctx.ABIG, ctx.AMID, ctx.AF, ctx.PS, ctx.PSS
    pfx = "L1M"
    NS = 16
    TH = 2048

    WdbT = ctx.din(pfx + "_WdbT", [D, 96], BF16)
    WdtT = ctx.din(pfx + "_WdtT", [64, D], BF16)
    bdt = load_cols(ctx, pfx + "_bdt", NDT)
    Dcol = load_cols(ctx, pfx + "_Dcol", NDT)
    Acols_d = ctx.din(pfx + "_Acols", [128, 128], F32)
    brow = ctx.dtmp("mm_brow", [2, TH], F32)

    yacc = ABIG[:, 0:16384].bitcast(F32)
    d2q = ABIG[:, 16384:24576]
    dAb = ABIG[:, 24576:28672].bitcast(F32)
    BXb = ABIG[:, 28672:30720]
    hbuf = ABIG[:, 32768:36864].bitcast(F32)
    ymul = ABIG[:, 30720:32768]
    d1T = AF_[:, 2048:3072].bitcast(BF16)
    BT = AF_[:, 3072:4096].bitcast(BF16)
    CT = AF_[:, 4096:5120].bitcast(BF16)
    hlast = AF_[:, 5120:5248]
    stmp = [AF_[:, 5248 + i * 512:5760 + i * 512] for i in range(2)]
    Bb = AF_[:, 0:1024].bitcast(BF16)
    Cb = AF_[:, 1024:2048].bitcast(BF16)
    Acol = AF_[:, 6272:6400]
    xstg = [AF_[:, 8192 + i * 512:8704 + i * 512] for i in range(2)]
    ostg = [AF_[:, 6400 + 0:6912], AF_[:, 6912:7424]]
    sWdb = ABIG[:, 16384:17152]
    sWdt = AF_[:, 7424:7936].bitcast(BF16)

    emit_ln(ctx, x_dram)
    v_ln_done = sch.cnt["V"]
    sch.wait("gpsimd", "V")
    at = sch.dma("sync", Acol, Acols_d)
    wdt_t = sch.dma("gpsimd", sWdt[0:64, :], WdtT)
    sch.wait("scalar", "V")
    sch.wait_tok("scalar", at)
    sa = s_act(sch, Acol, Acol, AF.Exp)
    sch.wait("vector", "SC", sa)
    van = v_ts(sch, Acol, Acol, -1.0, None, ALU.mult)

    st = {}
    for th in range(2):
        t0 = th * TH
        sch.wait("gpsimd", "T")
        sch.wait("gpsimd", "V")
        wdb_t = []
        for dt in range(NDT):
            wdb_t.append(sch.dma("gpsimd", sWdb[:, dt * 96:(dt + 1) * 96],
                                 WdbT[dt * 128:(dt + 1) * 128, :]))
        for w in wdb_t:
            sch.wait_tok("tensor", w)
        sch.wait("tensor", "V")
        for tc in range(4):
            cs = t0 + tc * 512
            for dt in range(NDT):
                t_mm(sch, PS[0:64, 0:512], sWdb[:, dt * 96:dt * 96 + 64],
                     ctx.xlnT(dt)[:, cs:cs + 512], dt == 0, dt == NDT - 1)
            for dt in range(NDT):
                t_mm(sch, PS[0:16, 512:1024], sWdb[:, dt * 96 + 64:dt * 96 + 80],
                     ctx.xlnT(dt)[:, cs:cs + 512], dt == 0, dt == NDT - 1)
            for dt in range(NDT):
                tb = t_mm(sch, PS[0:16, 1024:1536], sWdb[:, dt * 96 + 80:dt * 96 + 96],
                          ctx.xlnT(dt)[:, cs:cs + 512], dt == 0, dt == NDT - 1)
            sch.wait("vector", "T", tb)
            v_copy(sch, d1T[0:64, tc * 512:(tc + 1) * 512], PS[0:64, 0:512])
            v_copy(sch, BT[0:16, tc * 512:(tc + 1) * 512], PS[0:16, 512:1024])
            dbc = v_copy(
                sch, CT[0:16, tc * 512:(tc + 1) * 512], PS[0:16, 1024:1536])
            sch.wait("tensor", "V", dbc)
        for dh in range(2):
            sch.wait_tok("tensor", wdt_t)
            sch.wait("tensor", "V")
            sch.wait("tensor", "SC")
            for dj in range(4):
                gdt = dh * 4 + dj
                for tc in range(4):
                    if ("d2s", tc % 2) in st:
                        sch.wait("tensor", "V", st[("d2s", tc % 2)][0])
                    tm = t_mm(sch, PS[:, 1536 + (tc % 2) * 512:2048 + (tc % 2) * 512],
                              sWdt[0:64, gdt * 128:(gdt + 1) * 128],
                              d1T[0:64, tc * 512:(tc + 1) * 512], True, True)
                    ps = PS[:, 1536 + (tc % 2) * 512:2048 + (tc % 2) * 512]
                    sch.wait("scalar", "T", tm)
                    if ("d2s", tc % 2) in st:
                        sch.wait("scalar", "V", st[("d2s", tc % 2)][0])
                    b_c = bdt[:, gdt:gdt + 1]
                    s_act(sch, stmp[tc % 2], ps, AF.Abs, bias=b_c)
                    s_act(sch, stmp[tc % 2], stmp[tc % 2], AF.Exp,
                          scale=ctx.const_col(-1.0))
                    s_act(sch, stmp[tc % 2], stmp[tc % 2], AF.Ln,
                          bias=ctx.const_col(1.0))
                    sr = s_act(sch, ostg[tc % 2], ps, AF.Relu, bias=b_c)
                    sch.wait("vector", "SC", sr)
                    vv = v_tt(sch, d2q[:, dj * 2048 + tc * 512:dj * 2048 + (tc + 1) * 512],
                              stmp[tc % 2], ostg[tc % 2], ALU.add)
                    st[("d2s", tc % 2)] = (vv,)
            BXs = [BXb, AF_[:, 5248:6272].bitcast(BF16)]
            iters = [(s, dj) for s in range(NS) for dj in range(4)]

            def issue_dx(i):
                s_, dj_ = iters[i]
                gdt_ = dh * 4 + dj_
                if i == 0:
                    sch.wait("gpsimd", "V")
                prev = st.get(("bxfree", i % 2))
                if prev is not None:
                    sch.wait("gpsimd", "V", prev)
                dsl = d2q[:, dj_ * 2048:(dj_ + 1) * 2048]
                st[("dxtok", i)] = g_tt(sch, BXs[i % 2], dsl,
                                        ctx.xlnT(gdt_)[:, t0:t0 + TH], ALU.mult)

            issue_dx(0)
            r1 = r2 = None
            for i, (s, dj) in enumerate(iters):
                gdt = dh * 4 + dj
                if dj == 0:
                    vprev = st.get("bc_used")
                    if vprev is not None:
                        sch.wait("gpsimd", "V", vprev)
                    b1 = sch.dma("gpsimd", brow[0:1, :], BT[s:s + 1, :])
                    b2 = sch.dma("gpsimd", brow[1:2, :], CT[s:s + 1, :])
                    sch.wait_tok("gpsimd", b1)
                    sch.wait_tok("gpsimd", b2)
                    r1 = sch.dma("gpsimd", Bb, brow[0:1, :].to_broadcast((128, TH)))
                    r2 = sch.dma("gpsimd", Cb, brow[1:2, :].to_broadcast((128, TH)))
                dslice = d2q[:, dj * 2048:(dj + 1) * 2048]
                sch.wait("scalar", "V", van)
                if "dA_used" in st:
                    sch.wait("scalar", "V", st["dA_used"])
                sdA = s_act(sch, dAb, dslice, AF.Exp,
                            scale=Acol[:, gdt * 16 + s:gdt * 16 + s + 1])
                sch.wait_tok("vector", r1)
                sch.wait_tok("vector", r2)
                sch.wait("vector", "P", st[("dxtok", i)])
                bx = BXs[i % 2]
                vbx = v_tt(sch, bx, bx, Bb, ALU.mult)
                if i + 1 < len(iters):
                    issue_dx(i + 1)
                sch.wait("vector", "SC", sdA)
                col = s * 8 + gdt
                init = 0.0 if th == 0 else hlast[:, col:col + 1]
                vsc = v_scan(sch, hbuf, dAb, bx, init)
                st["dA_used"] = vsc
                st[("bxfree", i % 2)] = vsc
                vcar = v_copy(sch, hlast[:, col:col + 1], hbuf[:, TH - 1:TH])
                ya = yacc[:, dj * 2048:(dj + 1) * 2048]
                if s == 0:
                    vy = v_tt(sch, ya, hbuf, Cb, ALU.mult)
                else:
                    if "ymul_free" in st:
                        sch.wait("vector", "P", st["ymul_free"])
                    vym = v_tt(sch, ymul, hbuf, Cb, ALU.mult)
                    sch.wait("gpsimd", "V", vym)
                    vy = sch.op("gpsimd", lambda e, o=ya, a=ya, b=ymul:
                                e.tensor_tensor(o, a, b, ALU.add), inc=True)
                    st["ymul_free"] = sch.cnt["P"]
                st["bc_used"] = sch.cnt["V"]
            sch.wait("vector", "P")
            for dj in range(4):
                gdt = dh * 4 + dj
                for tc in range(4):
                    cs = t0 + tc * 512
                    j2 = tc % 2
                    xf = st.get(("mxf", j2))
                    if xf is not None:
                        sch.wait("sync", "V", xf)
                    else:
                        sch.wait("sync", "V", v_ln_done)
                    xt = sch.dma("sync", xstg[j2],
                                 x_dram[gdt * 128:(gdt + 1) * 128, cs:cs + 512])
                    sch.wait_tok("vector", xt)
                    of = st.get(("mof", j2))
                    if of is not None:
                        sch.wait_tok("vector", of)
                    v_stt(sch, ostg[j2],
                          ctx.xlnT(gdt)[:, cs:cs + 512], Dcol[:, gdt:gdt + 1],
                          yacc[:, dj * 2048 + tc * 512:dj * 2048 + (tc + 1) * 512],
                          ALU.mult, ALU.add)
                    vo = v_tt(sch, ostg[j2], ostg[j2], xstg[j2], ALU.add)
                    st[("mxf", j2)] = vo
                    sch.wait("sync", "V", vo)
                    st[("mof", j2)] = sch.dma(
                        "sync", xo_dram[gdt * 128:(gdt + 1) * 128, cs:cs + 512],
                        ostg[j2])
            sch.barrier()



def emit_tno(ctx, x_dram, xo_dram):
    nc, sch = ctx.nc, ctx.sch
    ABIG, AMID, AF_, PS, PSS = ctx.ABIG, ctx.AMID, ctx.AF, ctx.PS, ctx.PSS
    pfx = "L0T"
    NK = 33

    cosF = ctx.din("C_cosF", [NK, 4096 * 128], BF16)
    sinF = ctx.din("C_sinF", [NK, 4096 * 128], BF16)
    icosT = ctx.din("C_icos", [32, NK * 128 * 128], BF16)
    isinT = ctx.din("C_isin", [32, NK * 128 * 128], BF16)
    ccol_d = ctx.din("C_ccol", [128, NK], F32)
    pos_d = ctx.din("C_pos", [128, 64], F32)
    gam_d = ctx.din("C_gam", [128, 64], F32)
    rpeb_d = ctx.din("C_rpeb", [128, 320], F32)
    fc_d = ctx.din("C_fc", [32, 96], F32)
    uvwu_d = ctx.din(pfx + "_uvwu", [NDT, 131072], BF16)
    uvwv_d = ctx.din(pfx + "_uvwv", [D, D], BF16)
    ow_d = ctx.din(pfx + "_ow", [D, D], BF16)
    outw_d = ctx.din(pfx + "_outw", [32, D], BF16)

    dUT = ctx.dtmp("tn_uT", [D, T], BF16)
    dVR = ctx.dtmp("tn_vR", [T, D], BF16)
    dYT = ctx.dtmp("tn_yT", [D, T], BF16)

    i_c = ctx.col(NK)
    ccol = ctx.ACOL[:, i_c:i_c + NK]
    st = {}
    h3 = lambda ap: ap.rearrange("p (j f) -> p j f", f=32)

    emit_ln(ctx, x_dram)

    SrT = AF_[:, 0:2112].bitcast(BF16)
    SiT = AF_[:, 2112:4224].bitcast(BF16)
    H = AF_[:, 4224:6272]
    T2 = AF_[:, 6272:8320]
    MU = AF_[:, 8320:8384]
    VA = AF_[:, 8384:8448]
    RS = AF_[:, 8448:8512]
    RPB = AF_[:, 8512:8832]
    POSG = AF_[:, 8832:8896]
    GAMG = AF_[:, 8896:8960]
    FCW = AF_[0:32, 9088:9184]
    sT_t = ABIG[:, 33792:34048].bitcast(F32)
    gsb = ABIG[:, 34816:35840]
    gdb = ABIG[:, 35840:36864]
    posb = lambda i: RPB[:, i * 32:(i + 1) * 32]

    sch.wait("sync", "V")
    toks = [sch.dma("sync", RPB, rpeb_d), sch.dma("sync", POSG, pos_d),
            sch.dma("sync", GAMG, gam_d), sch.dma("sync", FCW, fc_d),
            sch.dma("sync", ccol, ccol_d)]
    for tk in toks:
        sch.wait_tok("vector", tk)
    v_tt(sch, h3(H), posb(0).unsqueeze(1).to_broadcast((128, 64, 32)),
         POSG.unsqueeze(2).to_broadcast((128, 64, 32)), ALU.mult)
    for layer in range(4):
        sch.wait("scalar", "V")
        s_sq = s_act(sch, T2, H, AF.Square)
        v_red(sch, MU, h3(H), ALU.add)
        sch.wait("vector", "SC", s_sq)
        v_red(sch, VA, h3(T2), ALU.add)
        v_ts(sch, MU, MU, 1.0 / 32, None, ALU.mult)
        v_ts(sch, VA, VA, 1.0 / 32, None, ALU.mult)
        v_tt(sch, RS, MU, MU, ALU.mult)
        vv = v_tt(sch, VA, VA, RS, ALU.subtract)
        sch.wait("scalar", "V", vv)
        s_act(sch, VA, VA, AF.Ln, bias=ctx.const_col(EPS))
        sr = s_act(sch, RS, VA, AF.Exp, scale=ctx.const_col(-0.5))
        sch.wait("vector", "SC", sr)
        v_tt(sch, h3(T2), h3(H), MU.unsqueeze(2).to_broadcast((128, 64, 32)),
             ALU.subtract)
        v_tt(sch, h3(T2), h3(T2), RS.unsqueeze(2).to_broadcast((128, 64, 32)),
             ALU.mult)
        wi = 1 + layer if layer < 3 else 7
        bi = 4 + layer if layer < 3 else 8
        v_tt(sch, h3(T2), h3(T2),
             posb(wi).unsqueeze(1).to_broadcast((128, 64, 32)), ALU.mult)
        vb = v_tt(sch, h3(T2), h3(T2),
                  posb(bi).unsqueeze(1).to_broadcast((128, 64, 32)), ALU.add)
        sch.wait("scalar", "V", vb)
        ss = s_act(sch, T2, T2, AF.Silu)
        if layer == 3:
            break
        sch.wait("vector", "SC", ss)
        sch.wait("tensor", "SC", ss)
        for jt in range(64):
            tt1 = t_tr(sch, PS[0:32, 1024:1152], T2[:, jt * 32:(jt + 1) * 32],
                       ctx.IDENF[:, :])
            sch.wait("vector", "T", tt1)
            ve = v_copy(sch, sT_t[0:32, :], PS[0:32, 1024:1152])
            sch.wait("tensor", "V", ve)
            tm = t_mm(sch, PS[:, 1536:1568], sT_t[0:32, :],
                      FCW[0:32, layer * 32:(layer + 1) * 32], True, True)
            sch.wait("vector", "T", tm)
            v_copy(sch, H[:, jt * 32:(jt + 1) * 32], PS[:, 1536:1568])
    sch.wait("vector", "SC")
    v_tt(sch, h3(T2), h3(T2), GAMG.unsqueeze(2).to_broadcast((128, 64, 32)),
         ALU.mult)
    v_tt(sch, gsb, T2[:, 0:1024], T2[:, 1024:2048], ALU.add)
    vgd = v_tt(sch, gdb, T2[:, 0:1024], T2[:, 1024:2048], ALU.subtract)

    tbs = [ABIG[:, 24576 + i * 4096:24576 + (i + 1) * 4096] for i in range(2)]
    sch.wait("tensor", "V", vgd)
    for tab, gg, dst in [(cosF, gsb, SrT), (sinF, gdb, SiT)]:
        for kc in range(NK):
            i2 = kc % 2
            if ("tbf", i2) in st:
                sch.wait("sync", "T", st[("tbf", i2)])
            tc1 = sch.dma("sync", tbs[i2],
                          tab[kc:kc + 1, :].rearrange(
                              "a (p f) -> (a p) f", p=128))
            sch.wait_tok("tensor", tc1)
            if ("sev", kc % 2) in st:
                sch.wait("tensor", "V", st[("sev", kc % 2)])
            for jt in range(32):
                t_mm(sch, PS[0:32, 2048 + (kc % 2) * 512:2176 + (kc % 2) * 512],
                     gg[:, jt * 32:(jt + 1) * 32],
                     tbs[i2][:, jt * 128:(jt + 1) * 128], jt == 0, jt == 31)
            st[("tbf", i2)] = sch.cnt["T"]
            sch.wait("vector", "T")
            st[("sev", kc % 2)] = v_copy(
                sch, dst[0:32, kc * 128:(kc + 1) * 128],
                PS[0:32, 2048 + (kc % 2) * 512:2176 + (kc % 2) * 512])
    sch.barrier()

    uvs = [ABIG[:, i * 1024:(i + 1) * 1024] for i in range(2)]
    stg = [ABIG[:, 2048 + i * 512:2560 + i * 512] for i in range(2)]
    pool = PsumPool(sch, [PS[:, i * 512:(i + 1) * 512] for i in range(4)])
    for mo in range(NDT):
        i2 = mo % 2
        if ("uvf", i2) in st:
            sch.wait("gpsimd", "T", st[("uvf", i2)])
        tku = sch.dma("gpsimd", uvs[i2],
                      uvwu_d[mo:mo + 1, :].rearrange("a (p f) -> (a p) f", p=128))
        sch.wait_tok("tensor", tku)
        for tc in range(NTC):
            ps = pool.acquire()
            for dt in range(NDT):
                t_mm(sch, ps, uvs[i2][:, dt * 128:(dt + 1) * 128],
                     ctx.xlnT(dt)[:, tc * 512:(tc + 1) * 512], dt == 0, dt == NDT - 1)
            j2 = (mo * NTC + tc) % 2
            sch.wait("scalar", "T")
            tk2 = st.get(("ust", j2))
            if tk2 is not None:
                sch.wait_tok("scalar", tk2)
            sc = s_act(sch, stg[j2], ps, AF.Silu)
            pool.release("SC", sc)
            sch.wait("sync", "SC", sc)
            st[("ust", j2)] = sch.dma(
                "sync", dUT[mo * 128:(mo + 1) * 128, tc * 512:(tc + 1) * 512],
                stg[j2])
        st[("uvf", i2)] = sch.cnt["T"]
    vslab = [ABIG[:, 4096 + dt * 1024:4096 + (dt + 1) * 1024] for dt in range(NDT)]
    sch.wait("gpsimd", "T")
    vt = []
    for dt in range(NDT):
        vt.append(sch.dma("gpsimd", vslab[dt],
                          uvwv_d[dt * 128:(dt + 1) * 128, :]))
    for tk in vt:
        sch.wait_tok("tensor", tk)
    for tt in range(32):
        for n in range(2):
            ps = pool.acquire()
            for dt in range(NDT):
                t_mm(sch, ps, ctx.xlnT(dt)[:, tt * 128:(tt + 1) * 128],
                     vslab[dt][:, n * 512:(n + 1) * 512], dt == 0, dt == NDT - 1)
            j2 = (tt * 2 + n) % 2
            sch.wait("scalar", "T")
            tk2 = st.get(("ust", j2))
            if tk2 is not None:
                sch.wait_tok("scalar", tk2)
            sc = s_act(sch, stg[j2], ps, AF.Silu)
            pool.release("SC", sc)
            sch.wait("sync", "SC", sc)
            st[("ust", j2)] = sch.dma(
                "sync", dVR[tt * 128:(tt + 1) * 128, n * 512:(n + 1) * 512],
                stg[j2])
    sch.barrier()

    vfull = lambda tt: AMID[:, tt * 1024:(tt + 1) * 1024]
    PrQ = ABIG[:, 0:8448]
    PiQ = ABIG[:, 8448:16896]
    ftb = [ABIG[:, 16896 + i * 4096:16896 + (i + 1) * 4096] for i in range(2)]
    itb = [ABIG[:, 25088 + i * 4096:25088 + (i + 1) * 4096] for i in range(2)]
    ftb2 = [ABIG[:, 16896 + i * 4224:16896 + (i + 1) * 4224] for i in range(2)]
    itb2 = [ABIG[:, 25344 + i * 4224:25344 + (i + 1) * 4224] for i in range(2)]
    sow = ABIG[0:32, 33792:34816]
    ystg = [ABIG[:, 34816 + i * 256:35072 + i * 256] for i in range(2)]
    ytr = [ABIG[:, 35328 + i * 128:35456 + i * 128] for i in range(2)]
    vload = []
    for tt in range(32):
        vload.append(sch.dma("sync", vfull(tt), dVR[tt * 128:(tt + 1) * 128, :]))
    wo = sch.dma("gpsimd", sow, outw_d)
    for tk in vload:
        sch.wait_tok("tensor", tk)
    sch.wait_tok("tensor", wo)

    for dq in range(4):
        d0 = dq * 256
        for kc in range(NK):
            i2 = kc % 2
            if ("ftb", i2) in st:
                sch.wait("sync", "T", st[("ftb", i2)])
                sch.wait("gpsimd", "T", st[("ftb", i2)])
            cview = cosF[kc:kc + 1, :].rearrange("a (p f) -> (a p) f", p=128)
            sview = sinF[kc:kc + 1, :].rearrange("a (p f) -> (a p) f", p=128)
            f1 = sch.dma("sync", ftb[i2], cview)
            s1 = sch.dma("gpsimd", itb[i2], sview)
            for tk in (f1, s1):
                sch.wait_tok("tensor", tk)
            if ("pwv", kc % 2) in st:
                sch.wait("tensor", "V", st[("pwv", kc % 2)])
                sch.wait("tensor", "SC", st[("pwsc", kc % 2)])
            pb = (kc % 2) * 512
            for tt in range(32):
                t_mm(sch, PS[:, 0 + pb:256 + pb],
                     ftb[i2][:, tt * 128:(tt + 1) * 128],
                     vfull(tt)[:, d0:d0 + 256], tt == 0, tt == 31)
            for tt in range(32):
                t_mm(sch, PS[:, 1024 + pb:1280 + pb],
                     itb[i2][:, tt * 128:(tt + 1) * 128],
                     vfull(tt)[:, d0:d0 + 256], tt == 0, tt == 31)
            st[("ftb", i2)] = sch.cnt["T"]
            if "pwlast" in st:
                sch.wait("tensor", "V", st["pwlast"])
            tfa = t_mm(sch, PS[:, 2048:2304],
                       SrT[0:32, kc * 128:(kc + 1) * 128], sow[0:32, d0:d0 + 256],
                       True, True)
            tfb = t_mm(sch, PS[:, 2560:2816],
                       SiT[0:32, kc * 128:(kc + 1) * 128], sow[0:32, d0:d0 + 256],
                       True, True)
            sch.wait("scalar", "T", tfb)
            if "pwlast" in st:
                sch.wait("scalar", "V", st["pwlast"])
            sa1 = s_act(sch, ystg[0], PS[:, 0 + pb:256 + pb], AF.Identity,
                        scale=ccol[:, kc:kc + 1])
            sa2 = s_act(sch, ystg[1], PS[:, 1024 + pb:1280 + pb], AF.Identity,
                        scale=ccol[:, kc:kc + 1])
            sch.wait("vector", "SC", sa2)
            prs = PrQ[:, kc * 256:(kc + 1) * 256]
            pis = PiQ[:, kc * 256:(kc + 1) * 256]
            v_tt(sch, prs, ystg[0], PS[:, 2048:2304], ALU.mult)
            v_tt(sch, pis, ystg[1], PS[:, 2048:2304], ALU.mult)
            v_tt(sch, ystg[0], ystg[0], PS[:, 2560:2816], ALU.mult)
            v_tt(sch, ystg[1], ystg[1], PS[:, 2560:2816], ALU.mult)
            v_tt(sch, prs, prs, ystg[1], ALU.subtract)
            vpw = v_tt(sch, pis, pis, ystg[0], ALU.add)
            st[("pwv", kc % 2)] = vpw
            st[("pwsc", kc % 2)] = sa2
            st["pwlast"] = vpw
        sch.wait("tensor", "V")
        sch.wait("sync", "T")
        sch.wait("gpsimd", "T")
        for tt in range(32):
            i2 = tt % 2
            if ("itbf", i2) in st:
                sch.wait("sync", "T", st[("itbf", i2)])
                sch.wait("gpsimd", "T", st[("itbf", i2)])
            giv = icosT[tt:tt + 1, :].rearrange("a (p f) -> (a p) f", p=128)
            gsv = isinT[tt:tt + 1, :].rearrange("a (p f) -> (a p) f", p=128)
            g1 = sch.dma("sync", ftb2[i2], giv)
            g2 = sch.dma("gpsimd", itb2[i2], gsv)
            for tk in (g1, g2):
                sch.wait_tok("tensor", tk)
            pb = (tt % 2) * 512
            if ("syf", i2) in st:
                sch.wait("tensor", "SC", st[("syf", i2)])
            for kc in range(NK):
                t_mm(sch, PS[:, 0 + pb:256 + pb],
                     ftb2[i2][:, kc * 128:(kc + 1) * 128],
                     PrQ[:, kc * 256:(kc + 1) * 256], kc == 0, False)
            for kc in range(NK):
                t_mm(sch, PS[:, 0 + pb:256 + pb],
                     itb2[i2][:, kc * 128:(kc + 1) * 128],
                     PiQ[:, kc * 256:(kc + 1) * 256], False, kc == NK - 1)
            st[("itbf", i2)] = sch.cnt["T"]
            sch.wait("scalar", "T")
            if ("ytrf", i2) in st:
                sch.wait("scalar", "T", st[("ytrf", i2)])
            sy = s_act(sch, ystg[i2], PS[:, 0 + pb:256 + pb], AF.Identity)
            st[("syf", i2)] = sy
            for j in range(2):
                sch.wait("tensor", "SC", sy)
                ptr = PS[:, 1024 + j * 512:1024 + j * 512 + 64]
                if ("ytv", j) in st:
                    sch.wait("tensor", "V", st[("ytv", j)])
                ttr = t_tr(sch, ptr.bitcast(BF16), ystg[i2][:, j * 128:(j + 1) * 128],
                           ctx.IDEN[:, :])
                sch.wait("vector", "T", ttr)
                tk3 = st.get(("ytd", i2))
                if tk3 is not None:
                    sch.wait_tok("vector", tk3)
                vc = v_copy(sch, ytr[i2], ptr.bitcast(BF16))
                st[("ytv", j)] = vc
                sch.wait("sync", "V", vc)
                st[("ytd", i2)] = sch.dma(
                    "sync", dYT[d0 + j * 128:d0 + (j + 1) * 128,
                                tt * 128:(tt + 1) * 128], ytr[i2])
            st[("ytrf", i2)] = sch.cnt["T"]
        sch.barrier()

    ows = [ABIG[:, 16896 + dt * 1024:16896 + (dt + 1) * 1024] for dt in range(NDT)]
    oto = ABIG[:, 0:4096]
    ust = ABIG[:, 4096 + 0:8192]
    yst2 = ABIG[:, 8192:12288]
    xst4 = [AF_[:, 0 + i * 512:512 + i * 512] for i in range(2)]
    ost4 = [AF_[:, 1024 + i * 512:1536 + i * 512] for i in range(2)]
    wot = []
    sch.wait("gpsimd", "T")
    sch.wait("gpsimd", "V")
    for dt in range(NDT):
        wot.append(sch.dma("gpsimd", ows[dt], ow_d[dt * 128:(dt + 1) * 128, :]))
    for tk in wot:
        sch.wait_tok("tensor", tk)
    for tc in range(NTC):
        sch.wait("sync", "T")
        lu = sch.dma("sync", ust,
                     dUT[:, tc * 512:(tc + 1) * 512]
                     .rearrange("(dt p) t -> p dt t", p=128))
        ly = sch.dma("sync", yst2,
                     dYT[:, tc * 512:(tc + 1) * 512]
                     .rearrange("(dt p) t -> p dt t", p=128))
        sch.wait_tok("vector", lu)
        sch.wait_tok("vector", ly)
        sch.wait("vector", "T")
        vm = v_tt(sch, oto, ust, yst2, ALU.mult)
        sch.wait("tensor", "V", vm)
        for m in range(NDT):
            ps = pool.acquire()
            for dt in range(NDT):
                t_mm(sch, ps, ows[dt][:, m * 128:(m + 1) * 128],
                     oto[:, dt * 512:(dt + 1) * 512], dt == 0, dt == NDT - 1)
            j2 = (tc * NDT + m) % 2
            xf = st.get(("oxf", j2))
            if xf is not None:
                sch.wait("sync", "V", xf)
            xt = sch.dma("sync", xst4[j2],
                         x_dram[m * 128:(m + 1) * 128, tc * 512:(tc + 1) * 512])
            sch.wait("vector", "T")
            sch.wait_tok("vector", xt)
            of = st.get(("oof", j2))
            if of is not None:
                sch.wait_tok("vector", of)
            vo = v_tt(sch, ost4[j2], ps, xst4[j2], ALU.add)
            st[("oxf", j2)] = vo
            pool.release("V", vo)
            sch.wait("sync", "V", vo)
            st[("oof", j2)] = sch.dma(
                "sync", xo_dram[m * 128:(m + 1) * 128, tc * 512:(tc + 1) * 512],
                ost4[j2])



def _t(a):
    return np.ascontiguousarray(np.asarray(a).T)


def _c(a):
    return np.ascontiguousarray(np.asarray(a))


def prep_weights(inp):
    w = {}
    w["C_iden"] = np.eye(128, dtype=BF)
    w["C_idenf"] = np.eye(128, dtype=np.float32)
    w["L0A_WqT"] = _t(inp["L0_attn_a_Wq"]).astype(BF)
    w["L0A_WkT"] = _t(inp["L0_attn_a_Wk"]).astype(BF)
    w["L0A_WvT"] = _t(inp["L0_attn_a_Wv"]).astype(BF)
    w["L0A_WoT"] = _t(inp["L0_attn_a_Wo"]).astype(BF)
    w["L0A_bq"] = _c(inp["L0_attn_a_bq"]).reshape(D, 1)
    w["L0A_bk"] = _c(inp["L0_attn_a_bk"]).reshape(D, 1)
    w["L0A_bv"] = _c(inp["L0_attn_a_bv"]).reshape(D, 1)
    w["L0A_bo"] = _c(inp["L0_attn_a_bo"]).reshape(D, 1)
    w["L0A_gamma"] = _c(inp["L0_attn_a_gamma"]).reshape(D, 1)
    w["L0A_beta"] = _c(inp["L0_attn_a_beta"]).reshape(D, 1)
    w["L0A_arep"] = np.repeat(np.asarray(inp["L0_attn_a_adaptive_scale"]), 64).reshape(D, 1).astype(np.float32)
    M = 8192
    tt_ = np.arange(4096, dtype=np.float64)
    kk = np.arange(33 * 128, dtype=np.float64)
    ang = 2.0 * np.pi * np.outer(tt_, kk) / M
    cosF = np.cos(ang)
    sinF = -np.sin(ang)
    sinF[:, 4096] = 0.0
    w["C_cosF"] = np.ascontiguousarray(
        cosF.reshape(32, 128, 33, 128).transpose(2, 1, 0, 3)
        .reshape(33, 4096 * 128)).astype(BF)
    w["C_sinF"] = np.ascontiguousarray(
        sinF.reshape(32, 128, 33, 128).transpose(2, 1, 0, 3)
        .reshape(33, 4096 * 128)).astype(BF)
    angI = 2.0 * np.pi * np.outer(kk, tt_) / M
    icos = np.cos(angI)
    isin = -np.sin(angI)
    icos[4097:, :] = 0.0
    isin[4097:, :] = 0.0
    w["C_icos"] = np.ascontiguousarray(
        icos.reshape(33, 128, 32, 128).transpose(2, 1, 0, 3)
        .reshape(32, 33 * 128 * 128)).astype(BF)
    w["C_isin"] = np.ascontiguousarray(
        isin.reshape(33, 128, 32, 128).transpose(2, 1, 0, 3)
        .reshape(32, 33 * 128 * 128)).astype(BF)
    ck = np.full(33 * 128, 2.0 / M)
    ck[0] = 1.0 / M
    ck[4096] = 1.0 / M
    ck[4097:] = 0.0
    w["C_ccol"] = np.ascontiguousarray(
        ck.reshape(33, 128).T).astype(np.float32)
    posv = np.zeros(8192)
    posv[0:4096] = np.arange(4096)
    posv[4096:] = -np.arange(4096)
    gamv = np.zeros(8192)
    gamv[0:4096] = 0.99 ** np.arange(4096)
    gamv[4096:] = 0.99 ** np.arange(4096)
    gamv[4096] = 0.0
    w["C_pos"] = np.ascontiguousarray(
        posv.reshape(64, 128).T).astype(np.float32)
    w["C_gam"] = np.ascontiguousarray(
        gamv.reshape(64, 128).T).astype(np.float32)
    rpeb = np.zeros((128, 320), np.float32)
    rpeb[:, 0:32] = np.broadcast_to(_c(inp["L0_attn_b_rpe_pos_w"]), (128, 32))
    for i in range(3):
        rpeb[:, 32 + i * 32:64 + i * 32] = np.broadcast_to(
            _c(inp["L0_attn_b_rpe_ln_w"])[i], (128, 32))
        rpeb[:, 128 + i * 32:160 + i * 32] = np.broadcast_to(
            _c(inp["L0_attn_b_rpe_ln_b"])[i], (128, 32))
    rpeb[:, 224:256] = np.broadcast_to(_c(inp["L0_attn_b_rpe_out_ln_w"]), (128, 32))
    rpeb[:, 256:288] = np.broadcast_to(_c(inp["L0_attn_b_rpe_out_ln_b"]), (128, 32))
    w["C_rpeb"] = rpeb
    w["C_fc"] = np.ascontiguousarray(
        np.concatenate([_c(inp["L0_attn_b_rpe_fc_w"])[i] for i in range(3)],
                       axis=1)).astype(np.float32)
    uvw = _c(inp["L0_attn_b_uv_w"]).astype(BF)
    w["L0T_uvwu"] = np.ascontiguousarray(
        uvw[:, 0:1024].reshape(8, 128, 8, 128).transpose(2, 1, 0, 3)
        .reshape(8, 131072))
    w["L0T_uvwv"] = np.ascontiguousarray(uvw[:, 1024:2048])
    w["L0T_ow"] = _c(inp["L0_attn_b_o_w"]).astype(BF)
    w["L0T_outw"] = _c(inp["L0_attn_b_rpe_out_w"]).astype(BF)
    A = _c(inp["L1_attn_a_A_log"])
    w["L1M_Acols"] = np.ascontiguousarray(
        A.reshape(8, 128, 16).transpose(1, 0, 2).reshape(128, 128))
    w["L1M_WdbT"] = _t(inp["L1_attn_a_W_deltaBC"]).astype(BF)
    w["L1M_WdtT"] = _t(inp["L1_attn_a_W_dt"]).astype(BF)
    w["L1M_bdt"] = _c(inp["L1_attn_a_b_dt"]).reshape(D, 1)
    w["L1M_Dcol"] = _c(inp["L1_attn_a_D_param"]).reshape(D, 1)
    def _tile4(a, nmo):
        a = np.ascontiguousarray(np.asarray(a).T).astype(BF)
        kt = a.shape[0] // 128
        return np.ascontiguousarray(
            a.reshape(kt, 128, nmo, 128).transpose(2, 1, 0, 3)
            .reshape(nmo, kt * 128 * 128))
    w["L1F_w1P"] = _tile4(inp["L1_ffn_w1_w"], 32)
    w["L1F_w2P"] = _tile4(inp["L1_ffn_w2_w"], 32)
    w["L1F_h1P"] = _tile4(inp["L1_ffn_h1_w"], 32)
    w["L1F_h2P"] = _tile4(inp["L1_ffn_h2_w"], 32)
    w["L1F_w3P"] = _tile4(inp["L1_ffn_w3_w"], 8)
    w["L1F_s1"] = _c(inp["L1_ffn_w1_s"]).reshape(NH4, 1)
    w["L1F_s2"] = _c(inp["L1_ffn_w2_s"]).reshape(NH4, 1)
    w["L1F_sh1"] = _c(inp["L1_ffn_h1_s"]).reshape(NH4, 1)
    w["L1F_sh2"] = _c(inp["L1_ffn_h2_s"]).reshape(NH4, 1)
    w["L1F_s3"] = _c(inp["L1_ffn_w3_s"]).reshape(D, 1)
    w["L1B_WkT"] = _t(inp["L1_attn_b_Wk"]).astype(BF)
    w["L1B_WvT"] = _t(inp["L1_attn_b_Wv"]).astype(BF)
    w["L1B_WqT"] = _t(inp["L1_attn_b_Wq"]).astype(BF)
    w["L1B_WoutT"] = _t(inp["L1_attn_b_Wout"]).astype(BF)
    w["L1B_W0T"] = _t(inp["L1_attn_b_W0"]).astype(BF)
    w["L1B_W1T"] = _t(inp["L1_attn_b_W1"]).astype(BF)
    w["L1B_W1"] = _c(inp["L1_attn_b_W1"]).astype(BF)
    _ = (inp["L1_attn_b_Wg_mom"], inp["L1_attn_b_bg_mom"])
    w["L1B_WgD"] = _c(inp["L1_attn_b_Wg_decay"]).reshape(D, 1)
    w["L1B_WgL"] = _c(inp["L1_attn_b_Wg_lr"]).reshape(D, 1)
    w["L1B_bgD"] = _c(inp["L1_attn_b_bg_decay"]).reshape(1, 1)
    w["L1B_bgL"] = _c(inp["L1_attn_b_bg_lr"]).reshape(1, 1)
    w["L0_ffn_WT"] = _t(inp["L0_ffn_W"]).astype(BF)
    w["L0_ffn_Vr"] = _c(inp["L0_ffn_V_r"]).astype(BF)
    w["L0_ffn_CT"] = _t(inp["L0_ffn_C"]).astype(BF)
    w["L0_ffn_bcol"] = _c(inp["L0_ffn_b"]).reshape(D, 1)
    return w


SUBLAYERS = ["attn_a", "ffn0a", "tno", "ffn0b", "mamba", "ffn1a", "ttt", "ffn1b"]


def build_sublayer(name):
    nc = bass.Bass("TRN2", target_bir_lowering=False)
    ctx = Ctx(nc)
    xin = ctx.din("xin", [D, T], F32)
    xout = ctx.dout("xout", [D, T], F32)
    emit_setup(ctx)
    if name in ("ffn0a", "ffn0b"):
        emit_ffn0(ctx, xin, xout, "L0_ffn")
    elif name == "attn_a":
        emit_attn_a(ctx, xin, xout)
    elif name == "ttt":
        emit_ttt(ctx, xin, xout)
    elif name in ("ffn1a", "ffn1b"):
        emit_ffn1(ctx, xin, xout)
    elif name == "mamba":
        emit_mamba(ctx, xin, xout)
    elif name == "tno":
        emit_tno(ctx, xin, xout)
    else:
        raise NotImplementedError(name)
    ctx.sch.finalize()
    ctx.sch.build()
    return nc, ctx


def build_fused():
    nc = bass.Bass("TRN2", target_bir_lowering=False)
    ctx = Ctx(nc)
    xin = ctx.din("xin", [D, T], F32)
    xout = ctx.dout("xout", [D, T], F32)
    xs1 = ctx.dtmp("xs1", [D, T], F32)
    xs2 = ctx.dtmp("xs2", [D, T], F32)
    emit_setup(ctx)
    seq = [emit_attn_a,
           lambda c, a, b: emit_ffn0(c, a, b, "L0_ffn"),
           emit_tno,
           lambda c, a, b: emit_ffn0(c, a, b, "L0_ffn"),
           emit_mamba,
           emit_ffn1,
           emit_ttt,
           emit_ffn1]
    bufs = [xin, xs1, xs2, xs1, xs2, xs1, xs2, xs1, xout]
    for i, fn in enumerate(seq):
        fn(ctx, bufs[i], bufs[i + 1])
        ctx.sch.barrier()
    ctx.sch.finalize()
    ctx.sch.build()
    return nc, ctx


def run_fused(xT_list, weights):
    if "fused" not in _PROG_CACHE:
        _PROG_CACHE["fused"] = build_fused()
    nc, ctx = _PROG_CACHE["fused"]
    in_maps = []
    for c in range(NCORES):
        m = {k: (xT_list[c] if k == "xin" else weights[k]) for k in ctx.inputs}
        in_maps.append(m)
    res = run_bass_kernel_spmd(nc, in_maps, core_ids=list(range(NCORES)))
    return [np.asarray(res.results[c]["xout"]) for c in range(NCORES)]


_PROG_CACHE = {}


def run_sublayer(name, xT_list, weights):
    if name not in _PROG_CACHE:
        _PROG_CACHE[name] = build_sublayer(name)
    nc, ctx = _PROG_CACHE[name]
    in_maps = []
    for c in range(NCORES):
        m = {}
        for k in ctx.inputs:
            m[k] = xT_list[c] if k == "xin" else weights[k]
        in_maps.append(m)
    res = run_bass_kernel_spmd(nc, in_maps, core_ids=list(range(NCORES)))
    return [np.asarray(res.results[c]["xout"]) for c in range(NCORES)]


def kernel(**inputs):
    w = prep_weights(inputs)
    x = np.asarray(inputs["L0_attn_a_x"])
    xT = [np.ascontiguousarray(x[c].T) for c in range(NCORES)]
    xT = run_fused(xT, w)
    out = np.stack([np.asarray(t).T for t in xT], axis=0)
    return out.astype(np.float32)


# version 10 (final)
# speedup vs baseline: 1.1087x; 1.1087x over previous
import numpy as np
import ml_dtypes
import concourse.bass as bass
import concourse.mybir as mybir
from concourse.bass_utils import run_bass_kernel_spmd
from contextlib import ExitStack

F32 = mybir.dt.float32
BF16 = mybir.dt.bfloat16
AF = mybir.ActivationFunctionType
ALU = mybir.AluOpType
AX = mybir.AxisListType
BF = ml_dtypes.bfloat16

NCORES = 8
D = 1024
NH4 = 4096
T = 4096
NDT = D // 128
NTC = T // 512
EPS = 1e-5

ENGINES = ("sync", "gpsimd", "tensor", "vector", "scalar")
ESEM = {"tensor": "T", "vector": "V", "scalar": "SC", "gpsimd": "P"}


class Sched:

    def __init__(self, nc):
        self.nc = nc
        self.q = {e: [] for e in ENGINES}
        self.cnt = {}
        self.sems = {}
        self.es = ExitStack()

    def sem(self, name):
        if name not in self.sems:
            self.sems[name] = self.es.enter_context(self.nc.semaphore(name))
            self.cnt[name] = 0
        return self.sems[name]

    def op(self, eng, fn, inc=None, inc_n=1):
        if inc is True:
            inc = ESEM[eng]
        if eng in ("vector", "scalar", "gpsimd"):
            own = ESEM[eng]
            self.sem(own)
            if self.cnt[own] > 0:
                self.q[eng].append(("wait", own, self.cnt[own]))
        if inc is not None:
            self.sem(inc)
            self.cnt[inc] += inc_n
        self.q[eng].append(("op", fn, inc, inc_n))
        return self.cnt[inc] if inc is not None else None

    NSTREAM = {"sync": 6, "gpsimd": 6, "scalar": 2, "vector": 2}

    def dma(self, eng, out, in_, **kw):
        if not hasattr(self, "_rr"):
            self._rr = {}
        i = self._rr.get(eng, 0)
        self._rr[eng] = (i + 1) % self.NSTREAM[eng]
        sname = f"q_{eng}_{i}"
        self.sem(sname)
        if self.cnt[sname] > 0:
            self.q[eng].append(("wait", sname, self.cnt[sname]))
        self.cnt[sname] += 16
        self.q[eng].append(("dma", out, in_, sname, kw))
        return (sname, self.cnt[sname])

    def wait_tok(self, eng, tok):
        if tok is None:
            return
        self.q[eng].append(("wait", tok[0], tok[1]))

    def wait_queue(self, eng, qeng):
        for i in range(self.NSTREAM[qeng]):
            sname = f"q_{qeng}_{i}"
            if self.cnt.get(sname, 0) > 0:
                self.q[eng].append(("wait", sname, self.cnt[sname]))

    def wait(self, eng, sem, val=None):
        self.sem(sem)
        v = self.cnt[sem] if val is None else val
        if v > 0:
            self.q[eng].append(("wait", sem, v))
        return v

    def finalize(self):
        for s in list(self.cnt):
            self.q["sync"].append(("wait", s, self.cnt[s]))

    def barrier(self):
        snap = dict(self.cnt)
        for e in ENGINES:
            for s, v in snap.items():
                if v > 0:
                    self.q[e].append(("wait", s, v))

    def build(self):
        nc = self.nc
        with self.es, nc.Block() as block:
            def runner(eng_name):
                def _run(eng):
                    for item in self.q[eng_name]:
                        kind = item[0]
                        if kind == "op":
                            _, fn, inc, inc_n = item
                            instr = fn(eng)
                            if inc is not None:
                                instr.then_inc(self.sems[inc], inc_n)
                        elif kind == "dma":
                            _, out, in_, semn, kw = item
                            eng.dma_start(out=out, in_=in_, **kw).then_inc(
                                self.sems[semn], 16)
                        else:
                            _, semn, v = item
                            eng.wait_ge(self.sems[semn], v)
                return _run
            block.sync(runner("sync"))
            block.gpsimd(runner("gpsimd"))
            block.tensor(runner("tensor"))
            block.vector(runner("vector"))
            block.scalar(runner("scalar"))



def t_mm(sch, out, lhsT, rhs, start, stop, inc=True):
    return sch.op("tensor",
                  lambda e, o=out, l=lhsT, r=rhs, s=start, p=stop:
                  e.matmul(o, l, r, start=s, stop=p), inc=inc)


def t_tr(sch, out, in_, iden, inc=True):
    return sch.op("tensor", lambda e, o=out, i=in_, d=iden: e.transpose(o, i, d),
                  inc=inc)


def v_tt(sch, out, in0, in1, alu, inc=True):
    return sch.op("vector", lambda e, o=out, a=in0, b=in1, p=alu:
                  e.tensor_tensor(o, a, b, p), inc=inc)


def v_ts(sch, out, in0, s1, s2, op0, op1=None, inc=True):
    if op1 is None:
        return sch.op("vector", lambda e, o=out, a=in0, s=s1, p=op0:
                      e.tensor_scalar(o, a, s, None, p), inc=inc)
    return sch.op("vector", lambda e, o=out, a=in0, x=s1, y=s2, p=op0, q=op1:
                  e.tensor_scalar(o, a, x, y, p, q), inc=inc)


def v_stt(sch, out, in0, scalar, in1, op0, op1, inc=True):
    return sch.op("vector", lambda e, o=out, a=in0, s=scalar, b=in1, p=op0, q=op1:
                  e.scalar_tensor_tensor(o, a, s, b, p, q), inc=inc)


def v_copy(sch, out, in_, inc=True):
    return sch.op("vector", lambda e, o=out, a=in_: e.tensor_copy(o, a), inc=inc)


def v_red(sch, out, in_, alu, negate=False, inc=True):
    return sch.op("vector", lambda e, o=out, a=in_, p=alu, n=negate:
                  e.tensor_reduce(o, a, axis=AX.X, op=p, negate=n), inc=inc)


def v_scan(sch, out, d0, d1, init, inc=True):
    return sch.op("vector", lambda e, o=out, a=d0, b=d1, i=init:
                  e.tensor_tensor_scan(o, a, b, i, ALU.mult, ALU.add), inc=inc)


def v_recip(sch, out, in_, inc=True):
    return sch.op("vector", lambda e, o=out, a=in_: e.reciprocal(o, a), inc=inc)


def v_memset(sch, ap, c, inc=True):
    return sch.op("vector", lambda e, o=ap, k=c: e.memset(o, k), inc=inc)


def s_act(sch, out, in_, func, bias=None, scale=None, accum=None, inc=True):
    kw = {}
    if bias is not None:
        kw["bias"] = bias
    if scale is not None:
        kw["scale"] = scale
    if accum is not None:
        kw["accum_out"] = accum
    return sch.op("scalar", lambda e, o=out, a=in_, f=func, k=dict(kw):
                  e.activation(out=o, in_=a, func=f, **k), inc=inc)


def g_tt(sch, out, in0, in1, alu, inc=True):
    return sch.op("gpsimd", lambda e, o=out, a=in0, b=in1, p=alu:
                  e.tensor_tensor(o, a, b, p), inc=inc)


class PsumPool:

    def __init__(self, sch, slots, sem="V"):
        self.sch = sch
        self.slots = [[ap, 0, "V"] for ap in slots]
        self.i = 0

    def acquire(self):
        ap, cnt, sem = self.slots[self.i % len(self.slots)]
        self.cur = self.i % len(self.slots)
        self.i += 1
        if cnt > 0:
            self.sch.wait("tensor", sem, cnt)
        return ap

    def release(self, sem, cnt):
        self.slots[self.cur][1] = cnt
        self.slots[self.cur][2] = sem


class Ctx:
    def __init__(self, nc):
        self.nc = nc
        self.sch = Sched(nc)
        es = self.sch.es
        sb = lambda name, shape, dt: es.enter_context(nc.sbuf_tensor(name, shape, dt))
        self.AMID = sb("AMID", [128, 32768], BF16)
        self.ABIG = sb("ABIG", [128, 36864], BF16)
        self.AF = sb("AF", [128, 9216], F32)
        self.ACOL = sb("ACOL", [128, 320], F32)
        self.ONE = sb("ONE", [128, 72], BF16)
        self.ONEF = sb("ONEF", [128, 2], F32)
        self.IDEN = sb("IDEN", [128, 128], BF16)
        self.IDENF = sb("IDENF", [128, 128], F32)
        self.PS = es.enter_context(nc.psum_tensor("PS", [128, 3584], F32))
        self.PSS = es.enter_context(nc.psum_tensor("PSS", [128, 512], F32))
        self.dram = {}
        self.inputs = set()
        self.acol_used = 0
        self.ln_count = 0

    def din(self, name, shape, dtype=F32):
        if name not in self.dram:
            self.dram[name] = self.nc.dram_tensor(
                name, list(shape), dtype, kind="ExternalInput").ap()
            self.inputs.add(name)
        return self.dram[name]

    def dout(self, name, shape, dtype=F32):
        if name not in self.dram:
            self.dram[name] = self.nc.dram_tensor(
                name, list(shape), dtype, kind="ExternalOutput").ap()
        return self.dram[name]

    def dtmp(self, name, shape, dtype=F32):
        if name not in self.dram:
            self.dram[name] = self.nc.dram_tensor(name, list(shape), dtype).ap()
        return self.dram[name]

    def col(self, n=1):
        i = self.acol_used
        self.acol_used += n
        assert self.acol_used <= 320, "ACOL overflow"
        return i

    def const_col(self, value):
        if not hasattr(self, "_ccols"):
            self._ccols = {}
        if value not in self._ccols:
            i = self.col(1)
            ap = self.ACOL[:, i:i + 1]
            self.sch.op("vector", lambda e, a=ap, v=value: e.memset(a, v), inc=True)
            self._ccols[value] = ap
        return self._ccols[value]

    def xlnT(self, dt):
        return self.AMID[:, dt * T:(dt + 1) * T]


def emit_setup(ctx):
    sch = ctx.sch
    sch.op("vector", lambda e: e.memset(ctx.ONE[:, :], 1.0), inc=True)
    sch.op("vector", lambda e: e.memset(ctx.ONEF[:, :], 1.0), inc=True)
    iden = ctx.din("C_iden", [128, 128], BF16)
    sch.dma("sync", ctx.IDEN[:, :], iden)
    idenf = ctx.din("C_idenf", [128, 128], F32)
    sch.dma("sync", ctx.IDENF[:, :], idenf)
    ctx.const_col(EPS)
    ctx.const_col(-0.5)
    ctx.const_col(-1.0)
    ctx.const_col(1.0)
    ctx.const_col(1.0 / 32768.0)
    sch.barrier()


def load_cols(ctx, name, n):
    if not hasattr(ctx, "_colcache"):
        ctx._colcache = {}
    if name in ctx._colcache:
        return ctx._colcache[name]
    i = ctx.col(n)
    ap = ctx.din(name, [n * 128, 1], F32)
    for j in range(n):
        ctx.sch.dma("sync", ctx.ACOL[:, i + j:i + j + 1],
                    ap[j * 128:(j + 1) * 128, :])
    ctx._colcache[name] = ctx.ACOL[:, i:i + n]
    return ctx._colcache[name]



def emit_ln(ctx, x_dram, xln_dram=None, chunks=None, dst=None):
    nc, sch = ctx.nc, ctx.sch
    ABIG, AF_, PS, PSS = ctx.ABIG, ctx.AF, ctx.PS, ctx.PSS
    tag = ctx.ln_count
    ctx.ln_count += 1
    scr = ctx.dtmp(f"lnrow{tag}", [2, T], F32)
    if chunks is None:
        chunks = list(range(NTC))
    if dst is None:
        dst = ctx.xlnT
    c0 = chunks[0]
    nck = len(chunks)
    sls = 512 * nck * NDT

    xbf = [ABIG[:, dt * 512 * nck:(dt + 1) * 512 * nck] for dt in range(NDT)]
    ld_done = []
    for dt in range(NDT):
        ld_done.append(sch.dma("gpsimd", xbf[dt],
                               x_dram[dt * 128:(dt + 1) * 128,
                                      c0 * 512:(c0 + nck) * 512]))

    v_after_stats = 0
    bcast_reads = []
    for ci, c in enumerate(chunks):
        t0 = (c - c0) * 512
        sq_done = []
        for dt in range(NDT):
            xsq = ABIG[:, sls + (dt % 2) * 512: sls + 512 + (dt % 2) * 512]
            sch.wait_tok("scalar", ld_done[dt])
            if dt >= 2:
                sch.wait("scalar", "T", sq_done[dt - 2][1])
            s = s_act(sch, xsq, xbf[dt][:, t0:t0 + 512], AF.Square)
            sch.wait("tensor", "SC", s)
            if dt == 0 and v_after_stats:
                sch.wait("tensor", "V", v_after_stats)
            t_mm(sch, PSS[0:1, 0:512], ctx.ONE[:, 0:1], xbf[dt][:, t0:t0 + 512],
                 dt == 0, dt == NDT - 1, inc=None)
            tt = t_mm(sch, PS[0:1, 3072:3584], ctx.ONE[:, 0:1], xsq,
                      dt == 0, dt == NDT - 1)
            sq_done.append((s, tt))
        mu = AF_[0:1, 6656:7168]
        var = AF_[0:1, 7168:7680]
        tmp = AF_[0:1, 7680:8192]
        a_r = AF_[0:1, 8192:8704]
        b_r = AF_[0:1, 8704:9216]
        sch.wait("vector", "T")
        v_ts(sch, mu, PSS[0:1, 0:512], 1.0 / D, None, ALU.mult)
        v_ts(sch, var, PS[0:1, 3072:3584], 1.0 / D, None, ALU.mult)
        v_after_stats = sch.cnt["V"]
        v_tt(sch, tmp, mu, mu, ALU.mult)
        vv = v_tt(sch, var, var, tmp, ALU.subtract)
        sch.wait("scalar", "V", vv)
        s_act(sch, var, var, AF.Ln, bias=ctx.const_col(EPS)[0:1, :])
        ss = s_act(sch, a_r, var, AF.Exp, scale=ctx.const_col(-0.5)[0:1, :])
        sch.wait("vector", "SC", ss)
        v_tt(sch, b_r, mu, a_r, ALU.mult)
        vb = v_ts(sch, b_r, b_r, -1.0, None, ALU.mult)
        sch.wait("gpsimd", "V", vb)
        w1 = sch.dma("gpsimd", scr[0:1, t0:t0 + 512], a_r)
        w2 = sch.dma("gpsimd", scr[1:2, t0:t0 + 512], b_r)
        sch.wait_tok("gpsimd", w1)
        sch.wait_tok("gpsimd", w2)
        Ab = ABIG[:, sls + 1024 + (ci % 2) * 1024: sls + 1536 + (ci % 2) * 1024]
        Bb = ABIG[:, sls + 1536 + (ci % 2) * 1024: sls + 2048 + (ci % 2) * 1024]
        if ci >= 2:
            sch.wait("gpsimd", "V", bcast_reads[ci - 2])
        ra = sch.dma("gpsimd", Ab, scr[0:1, t0:t0 + 512].to_broadcast((128, 512)))
        rb = sch.dma("gpsimd", Bb, scr[1:2, t0:t0 + 512].to_broadcast((128, 512)))
        sch.wait_tok("vector", ra)
        sch.wait_tok("vector", rb)
        last = None
        for dt in range(NDT):
            if ci == 0:
                sch.wait_tok("vector", ld_done[dt])
            tmpb = ABIG[:, sls + 3072 + (dt % 2) * 512: sls + 3584 + (dt % 2) * 512]
            v_tt(sch, tmpb, xbf[dt][:, t0:t0 + 512], Ab, ALU.mult)
            last = v_tt(sch, dst(dt)[:, t0:t0 + 512], tmpb, Bb, ALU.add)
        bcast_reads.append(last)
    if xln_dram is not None:
        sch.wait("sync", "V")
        for dt in range(NDT):
            sch.dma("sync", xln_dram[dt * 128:(dt + 1) * 128, :], dst(dt))



def emit_ffn0(ctx, x_dram, xo_dram, pfx="L0_ffn"):
    nc, sch = ctx.nc, ctx.sch
    ABIG, AF_, PS, PSS = ctx.ABIG, ctx.AF, ctx.PS, ctx.PSS
    emit_ln(ctx, x_dram)
    bcol = load_cols(ctx, pfx + "_bcol", NDT)

    WT = ctx.din(pfx + "_WT", [D, D], BF16)
    Vr = ctx.din(pfx + "_Vr", [D, 32], BF16)
    CT = ctx.din(pfx + "_CT", [32, D], BF16)
    sWT = [ABIG[:, dt * 1024:(dt + 1) * 1024] for dt in range(NDT)]
    sVr = [ABIG[:, 8192 + dt * 32: 8224 + dt * 32] for dt in range(NDT)]
    sCT = ABIG[0:32, 8448:9472]
    rT = ABIG[0:32, 9472:13568]

    sch.wait("gpsimd", "V")
    for dt in range(NDT):
        sch.dma("gpsimd", sWT[dt], WT[dt * 128:(dt + 1) * 128, :])
        sch.dma("gpsimd", sVr[dt], Vr[dt * 128:(dt + 1) * 128, :])
    wload = sch.dma("gpsimd", sCT, CT)

    sch.wait_queue("tensor", "gpsimd")
    sch.wait("tensor", "V")
    ev = 0
    for c in range(NTC):
        if c >= 1 and ev:
            sch.wait("tensor", "V", ev)
        for dt in range(NDT):
            t_mm(sch, PSS[0:32, 0:512], sVr[dt], ctx.xlnT(dt)[:, c * 512:(c + 1) * 512],
                 dt == 0, dt == NDT - 1)
        sch.wait("vector", "T")
        ev = v_copy(sch, rT[:, c * 512:(c + 1) * 512], PSS[0:32, 0:512])

    pool = PsumPool(sch, [ctx.PS[:, i * 512:(i + 1) * 512] for i in range(4)])
    xin_ld = {}
    out_st = {}
    for m in range(NDT):
        for c in range(NTC):
            idx = m * NTC + c
            slot = idx % 2
            xin = AF_[:, slot * 512:(slot + 1) * 512]
            if idx >= 2:
                sch.wait("sync", "V", xin_ld[idx - 2])
            ldx = sch.dma("sync", xin,
                          x_dram[m * 128:(m + 1) * 128, c * 512:(c + 1) * 512])
            ps = pool.acquire()
            for dt in range(NDT):
                t_mm(sch, ps, sWT[dt][:, m * 128:(m + 1) * 128],
                     ctx.xlnT(dt)[:, c * 512:(c + 1) * 512], dt == 0, False)
            sch.wait("tensor", "V", ev)
            tdone = t_mm(sch, ps, sCT[:, m * 128:(m + 1) * 128],
                         rT[:, c * 512:(c + 1) * 512], False, True)
            out = AF_[:, 1024 + slot * 512: 1536 + slot * 512]
            sch.wait("vector", "T", tdone)
            sch.wait_tok("vector", ldx)
            if idx >= 2:
                sch.wait_tok("vector", out_st[idx - 2])
            vdone = v_stt(sch, out, ps, bcol[:, m:m + 1], xin, ALU.add, ALU.add)
            pool.release("V", vdone)
            xin_ld[idx] = vdone
            sch.wait("sync", "V", vdone)
            out_st[idx] = sch.dma(
                "sync", xo_dram[m * 128:(m + 1) * 128, c * 512:(c + 1) * 512], out)



def emit_attn_a(ctx, x_dram, xo_dram):
    nc, sch = ctx.nc, ctx.sch
    ABIG, AMID, AF_, PS, PSS = ctx.ABIG, ctx.AMID, ctx.AF, ctx.PS, ctx.PSS
    pfx = "L0A"
    NH = 16

    arep = load_cols(ctx, pfx + "_arep", NDT)
    bq = load_cols(ctx, pfx + "_bq", NDT)
    bk = load_cols(ctx, pfx + "_bk", NDT)
    bv = load_cols(ctx, pfx + "_bv", NDT)
    bo = load_cols(ctx, pfx + "_bo", NDT)
    gam = load_cols(ctx, pfx + "_gamma", NDT)
    bet = load_cols(ctx, pfx + "_beta", NDT)
    qs_i = ctx.col(NDT)
    qscale = ctx.ACOL[:, qs_i:qs_i + NDT]
    bq_i = ctx.col(NDT)
    bqs = ctx.ACOL[:, bq_i:bq_i + NDT]
    sch.wait_queue("vector", "sync")
    v_ts(sch, qscale, arep, 0.125, None, ALU.mult)
    v_tt(sch, bqs, bq, qscale, ALU.mult)

    zT_d = ctx.dtmp("att_zT", [D, T], BF16)
    QT_d = ctx.dtmp("att_QT", [D, T], BF16)
    KT_d = ctx.dtmp("att_KT", [D, T], BF16)
    VT_d = ctx.dtmp("att_VT", [D, T], BF16)

    emit_ln(ctx, x_dram, xln_dram=zT_d)

    WqT = ctx.din(pfx + "_WqT", [D, D], BF16)
    WkT = ctx.din(pfx + "_WkT", [D, D], BF16)
    WvT = ctx.din(pfx + "_WvT", [D, D], BF16)
    sch.wait("gpsimd", "V")
    slabs = {}
    for wi, Wd in enumerate([WqT, WkT, WvT]):
        for dt in range(NDT):
            sl = ABIG[:, (wi * NDT + dt) * 1024:(wi * NDT + dt + 1) * 1024]
            slabs[(wi, dt)] = sl
            sch.dma("gpsimd", sl, Wd[dt * 128:(dt + 1) * 128, :])
    sch.wait_queue("tensor", "gpsimd")

    pool = PsumPool(sch, [PS[:, i * 512:(i + 1) * 512] for i in range(4)])
    stage = [ABIG[:, 24576 + i * 512:25088 + i * 512] for i in range(2)]
    st_done = {}
    for wi, (dst, bcolv, scolv) in enumerate(
            [(QT_d, bqs, qscale), (KT_d, bk, None), (VT_d, bv, None)]):
        for hp in range(NDT):
            for c in range(NTC):
                idx = (wi * NDT + hp) * NTC + c
                ps = pool.acquire()
                for dt in range(NDT):
                    t_mm(sch, ps, slabs[(wi, dt)][:, hp * 128:(hp + 1) * 128],
                         ctx.xlnT(dt)[:, c * 512:(c + 1) * 512],
                         dt == 0, dt == NDT - 1)
                sg = stage[idx % 2]
                sch.wait("scalar", "T")
                if idx >= 2:
                    sch.wait_tok("scalar", st_done[idx - 2])
                sc = s_act(sch, sg, ps, AF.Identity, bias=bcolv[:, hp:hp + 1],
                           scale=(scolv[:, hp:hp + 1] if scolv is not None else None))
                pool.release("SC", sc)
                sch.wait("sync", "SC", sc)
                st_done[idx] = sch.dma(
                    "sync", dst[hp * 128:(hp + 1) * 128, c * 512:(c + 1) * 512], sg)

    sch.barrier()

    KTa = [ABIG[:, 0:4096], ABIG[:, 4096:8192]]
    QTa = [ABIG[:, 8192:12288], ABIG[:, 12288:16384]]
    VTs = [ABIG[:, 16384:20480], ABIG[:, 20480:24576]]
    VA = [ABIG[:, 24576 + i * 2080:24576 + (i + 1) * 2080] for i in range(2)]
    eTb = [ABIG[:, 28736 + i * 512:29248 + i * 512] for i in range(2)]
    rrB = ABIG[:, 29760:30272]
    astg = ABIG[:, 30272 + 0:30784]
    afin = ABIG[:, 30784:31296]
    SQB = ABIG[:, 31296:35392]
    rrF = AF_[:, 512:1024]
    qbrow = AF_[0:1, 0:4096]
    knrow = AF_[0:1, 4096:8192]
    knmax = AF_[0:1, 8192:8193]

    buf_free = {0: 0, 1: 0}
    flat_done = {}
    attnT = lambda ct: AMID[:, ct * T:(ct + 1) * T]

    prep_state = {"last_flat": None, "sq_T": 0, "nbank": {}}

    def prep(h):
        b = h % 2
        if buf_free[b]:
            sch.wait("sync", "T", buf_free[b])
        kld = sch.dma("sync", KTa[b][0:64, :], KT_d[h * 64:(h + 1) * 64, :])
        qld = sch.dma("sync", QTa[b][0:64, :], QT_d[h * 64:(h + 1) * 64, :])
        vld = sch.dma("sync", VTs[b][0:64, :], VT_d[h * 64:(h + 1) * 64, :])
        if h < 2:
            sch.wait_tok("vector", kld)
            v_memset(sch, KTa[b][64:65, :], 1.0)
            for kt in range(32):
                v_memset(sch, VA[b][:, kt * 65 + 64:kt * 65 + 65], 1.0)
        sch.wait_tok("tensor", vld)
        ev_prev = {}
        tr_done = {}
        for kt in range(32):
            slot = kt % 2
            pstr = PS[:, 1024 + slot * 512:1024 + slot * 512 + 32].bitcast(BF16)
            if kt >= 2:
                sch.wait("tensor", "V", ev_prev[kt - 2])
            tr_done[kt] = t_tr(sch, pstr, VTs[b][0:64, kt * 128:(kt + 1) * 128],
                               ctx.IDEN[0:64, 0:64])
            sch.wait("vector", "T", tr_done[kt])
            ev_prev[kt] = v_copy(sch, VA[b][:, kt * 65:kt * 65 + 64], pstr)
        if prep_state["sq_T"]:
            sch.wait("scalar", "T", prep_state["sq_T"])
        if prep_state["last_flat"] is not None:
            sch.wait_tok("scalar", prep_state["last_flat"])
            sch.wait_tok("vector", prep_state["last_flat"])
        sch.wait_tok("scalar", kld)
        sqk = s_act(sch, SQB[0:64, :], KTa[b][0:64, :], AF.Square)
        sch.wait("tensor", "SC", sqk)
        ev = prep_state["nbank"]
        for c8 in range(8):
            bank = PS[0:1, (c8 % 2) * 512:(c8 % 2) * 512 + 512]
            if (c8 % 2) in ev:
                sch.wait("tensor", "SC", ev[c8 % 2])
            mm = t_mm(sch, bank, ctx.ONE[0:64, 0:1], SQB[0:64, c8 * 512:(c8 + 1) * 512],
                      True, True)
            sch.wait("scalar", "T", mm)
            ev[c8 % 2] = s_act(sch, knrow[:, c8 * 512:(c8 + 1) * 512], bank, AF.Ln)
        sch.wait("vector", "SC", ev[1])
        vkm = v_red(sch, knmax, knrow, ALU.max)
        sch.wait_tok("scalar", qld)
        sch.wait("scalar", "T")
        sqq = s_act(sch, SQB[0:64, :], QTa[b][0:64, :], AF.Square)
        sch.wait("tensor", "SC", sqq)
        for c8 in range(8):
            bank = PS[0:1, (c8 % 2) * 512:(c8 % 2) * 512 + 512]
            sch.wait("tensor", "SC", ev[c8 % 2])
            mm = t_mm(sch, bank, ctx.ONE[0:64, 0:1], SQB[0:64, c8 * 512:(c8 + 1) * 512],
                      True, True)
            sch.wait("scalar", "T", mm)
            ev[c8 % 2] = s_act(sch, qbrow[:, c8 * 512:(c8 + 1) * 512], bank, AF.Ln)
        prep_state["sq_T"] = sch.cnt["T"]
        sch.wait("vector", "SC", ev[0])
        sch.wait("vector", "SC", ev[1])
        sch.wait("vector", "V", vkm)
        vq = v_ts(sch, qbrow, qbrow, knmax[0:1, 0:1], None, ALU.add)
        sch.wait("scalar", "V", vq)
        se = s_act(sch, qbrow, qbrow, AF.Exp, scale=ctx.const_col(0.5)[0:1, :])
        sch.wait("vector", "SC", se)
        vrow = v_ts(sch, qbrow, qbrow, -1.02, 70.0, ALU.mult, ALU.add)
        sch.wait("gpsimd", "V", vrow)
        flat_done[h] = sch.dma("gpsimd", QTa[b][64:65, 0:4096], qbrow)
        prep_state["last_flat"] = flat_done[h]

    def stream(h):
        b = h % 2
        ct = h // 2
        pbase = (h % 2) * 64
        sch.wait_tok("tensor", flat_done[h])
        av_done = {}
        exp_done = {}
        st_done = {}
        kg = 0
        for qc in range(NTC):
            for kt in range(32):
                slot = kg % 2
                ps_sT = PS[:, 2048 + slot * 512:2560 + slot * 512]
                if kg >= 2:
                    sch.wait("tensor", "SC", exp_done[kg - 2])
                st_done[kg] = t_mm(
                    sch, ps_sT, KTa[b][0:65, kt * 128:(kt + 1) * 128],
                    QTa[b][0:65, qc * 512:(qc + 1) * 512], True, True)
                if kg >= 1 and (kg - 1) not in av_done:
                    pk = kg - 1
                    sch.wait("tensor", "SC", exp_done[pk])
                    av_done[pk] = t_mm(
                        sch, PSS[0:65, 0:512],
                        VA[b][:, (pk % 32) * 65:(pk % 32) * 65 + 65],
                        eTb[pk % 2], (pk % 32) == 0, (pk % 32) == 31)
                sch.wait("scalar", "T", st_done[kg])
                if kg >= 2:
                    sch.wait("scalar", "T", av_done[kg - 2])
                exp_done[kg] = s_act(sch, eTb[slot], ps_sT, AF.Exp)
                kg += 1
            pk = kg - 1
            sch.wait("tensor", "SC", exp_done[pk])
            av_done[pk] = t_mm(sch, PSS[0:65, 0:512],
                               VA[b][:, 31 * 65:31 * 65 + 65],
                               eTb[pk % 2], False, True)
            sch.wait("vector", "T", av_done[pk])
            if prep_state.get("dd_prev") is not None:
                sch.wait_tok("vector", prep_state["dd_prev"])
            vr = v_recip(sch, rrF[64:65, :], PSS[64:65, 0:512])
            sch.wait("scalar", "V", vr)
            sb_ = s_act(sch, rrB[64:65, :], rrF[64:65, :], AF.Identity)
            sch.wait("tensor", "SC", sb_)
            bc = t_mm(sch, PS[0:64, 3072:3584], ctx.ONE[64:65, 0:64],
                      rrB[64:65, :], True, True)
            sa = s_act(sch, astg[0:64, :], PSS[0:64, 0:512], AF.Identity)
            sch.wait("vector", "T", bc)
            sch.wait("vector", "SC", sa)
            vf = v_tt(sch, afin[0:64, :], astg[0:64, :], PS[0:64, 3072:3584],
                      ALU.mult)
            sch.wait("gpsimd", "V", vf)
            prep_state["dd_prev"] = sch.dma(
                "gpsimd", attnT(ct)[pbase:pbase + 64, qc * 512:(qc + 1) * 512],
                afin[0:64, :])
            sch.wait("tensor", "V", vf)
            sch.wait("tensor", "SC", sa)
        buf_free[b] = sch.cnt["T"]

    for h in range(NH + 1):
        if h < NH:
            prep(h)
        if h >= 1:
            stream(h - 1)

    sch.barrier()

    WoT = ctx.din(pfx + "_WoT", [D, D], BF16)
    wslab = [ABIG[:, dt * 1024:(dt + 1) * 1024] for dt in range(NDT)]
    for dt in range(NDT):
        sch.dma("gpsimd", wslab[dt], WoT[dt * 128:(dt + 1) * 128, :])
    sch.wait_queue("tensor", "gpsimd")
    zslot = [ABIG[:, 8192 + i * 512:8704 + i * 512] for i in range(2)]
    pool4 = PsumPool(sch, [PS[:, i * 512:(i + 1) * 512] for i in range(4)])
    tmp = AF_[:, 4096:4608]
    Ab4 = AF_[:, 4608:5120]
    Bb4 = AF_[:, 5120:5632]
    xslot = [AF_[:, 5632 + i * 512:6144 + i * 512] for i in range(2)]
    scr4 = ctx.dtmp("att_lnrow", [2, 512], F32)
    mu = AF_[0:1, 6656:7168]
    var = AF_[0:1, 7168:7680]
    tmr = AF_[0:1, 7680:8192]
    a_r = AF_[0:1, 8192:8704]
    b_r = AF_[0:1, 8704:9216]
    v_stats_prev = 0
    xin_free = {}
    out_free = {}
    z_free = {}
    for tc in range(NTC):
        for m in range(NDT):
            zidx = tc * NDT + m
            if zidx >= 2:
                sch.wait("sync", "V", z_free[zidx - 2])
            zld = sch.dma("sync", zslot[m % 2],
                          zT_d[m * 128:(m + 1) * 128, tc * 512:(tc + 1) * 512])
            ps = pool4.acquire()
            for ctt in range(NDT):
                t_mm(sch, ps, wslab[ctt][:, m * 128:(m + 1) * 128],
                     attnT(ctt)[:, tc * 512:(tc + 1) * 512], ctt == 0, ctt == NDT - 1)
            y = AF_[:, m * 512:(m + 1) * 512]
            sch.wait("vector", "T")
            sch.wait_tok("vector", zld)
            vy = v_stt(sch, y, ps, bo[:, m:m + 1], zslot[m % 2], ALU.add, ALU.add)
            z_free[zidx] = vy
            pool4.release("V", vy)
            sch.wait("scalar", "V", vy)
            sq = s_act(sch, tmp, y, AF.Square)
            sch.wait("tensor", "SC", sq)
            if m == 0 and v_stats_prev:
                sch.wait("tensor", "V", v_stats_prev)
            t_mm(sch, PSS[0:1, 0:512], ctx.ONEF[:, 0:1], y, m == 0, m == NDT - 1)
            t_mm(sch, PS[0:1, 3072:3584], ctx.ONEF[:, 0:1], tmp,
                 m == 0, m == NDT - 1)
        sch.wait("vector", "T")
        v_ts(sch, mu, PSS[0:1, 0:512], 1.0 / D, None, ALU.mult)
        v_ts(sch, var, PS[0:1, 3072:3584], 1.0 / D, None, ALU.mult)
        v_stats_prev = sch.cnt["V"]
        v_tt(sch, tmr, mu, mu, ALU.mult)
        vv = v_tt(sch, var, var, tmr, ALU.subtract)
        sch.wait("scalar", "V", vv)
        s_act(sch, var, var, AF.Ln, bias=ctx.const_col(EPS)[0:1, :])
        ss = s_act(sch, a_r, var, AF.Exp, scale=ctx.const_col(-0.5)[0:1, :])
        sch.wait("vector", "SC", ss)
        v_tt(sch, b_r, mu, a_r, ALU.mult)
        vb = v_ts(sch, b_r, b_r, -1.0, None, ALU.mult)
        sch.wait("gpsimd", "V", vb)
        w1 = sch.dma("gpsimd", scr4[0:1, :], a_r)
        w2 = sch.dma("gpsimd", scr4[1:2, :], b_r)
        sch.wait_tok("gpsimd", w1)
        sch.wait_tok("gpsimd", w2)
        sch.wait("gpsimd", "V")
        ra = sch.dma("gpsimd", Ab4, scr4[0:1, :].to_broadcast((128, 512)))
        rb = sch.dma("gpsimd", Bb4, scr4[1:2, :].to_broadcast((128, 512)))
        sch.wait_tok("vector", ra)
        sch.wait_tok("vector", rb)
        for m in range(NDT):
            idx = tc * NDT + m
            if idx >= 2:
                sch.wait("sync", "V", xin_free[idx - 2])
            xld = sch.dma("sync", xslot[m % 2],
                          x_dram[m * 128:(m + 1) * 128, tc * 512:(tc + 1) * 512])
            y = AF_[:, m * 512:(m + 1) * 512]
            v_tt(sch, tmp, y, Ab4, ALU.mult)
            v_tt(sch, tmp, tmp, Bb4, ALU.add)
            v_ts(sch, tmp, tmp, gam[:, m:m + 1], bet[:, m:m + 1],
                 ALU.mult, ALU.add)
            sch.wait_tok("vector", xld)
            if idx >= 2:
                sch.wait_tok("vector", out_free[idx - 2])
            vo = v_tt(sch, y, tmp, xslot[m % 2], ALU.add)
            xin_free[idx] = vo
            sch.wait("sync", "V", vo)
            out_free[idx] = sch.dma(
                "sync", xo_dram[m * 128:(m + 1) * 128, tc * 512:(tc + 1) * 512], y)



def emit_ttt(ctx, x_dram, xo_dram):
    nc, sch = ctx.nc, ctx.sch
    ABIG, AMID, AF_, PS, PSS = ctx.ABIG, ctx.AMID, ctx.AF, ctx.PS, ctx.PSS
    pfx = "L1B"
    SC2 = 2.0 / 4096.0

    dKT = ctx.dtmp("tt_kT", [D, T], BF16)
    dKR = ctx.dtmp("tt_kR", [T, D], BF16)
    dVT = ctx.dtmp("tt_vT", [D, T], BF16)
    dQT = ctx.dtmp("tt_qT", [D, T], BF16)
    dA1T = ctx.dtmp("tt_a1T", [D, T], BF16)
    dA1R = ctx.dtmp("tt_a1R", [T, D], BF16)
    dSDR = ctx.dtmp("tt_sdR", [T, D], BF16)
    dD2T = ctx.dtmp("tt_d2T", [D, T], BF16)
    dD2R = ctx.dtmp("tt_d2R", [T, D], BF16)
    dD1R = ctx.dtmp("tt_d1R", [T, D], BF16)
    dHT = ctx.dtmp("tt_hT", [D, T], BF16)
    dH2T = ctx.dtmp("tt_h2T", [D, T], BF16)
    AR1_in = ctx.dtmp("tt_ar1_in", [D + 1, D], F32)
    AR1_out = ctx.dtmp("tt_ar1_out", [D + 1, D], F32)
    AR2_in = ctx.dtmp("tt_ar2_in", [D, D], F32)
    AR2_out = ctx.dtmp("tt_ar2_out", [D, D], F32)

    WgD = load_cols(ctx, pfx + "_WgD", NDT)
    WgL = load_cols(ctx, pfx + "_WgL", NDT)
    bgD = ctx.din(pfx + "_bgD", [1, 1], F32)
    bgL = ctx.din(pfx + "_bgL", [1, 1], F32)
    i_bg = ctx.col(2)
    sch.dma("sync", ctx.ACOL[0:1, i_bg:i_bg + 1], bgD)
    sch.dma("sync", ctx.ACOL[0:1, i_bg + 1:i_bg + 2], bgL)
    i_xs = ctx.col(NDT)
    XS = ctx.ACOL[:, i_xs:i_xs + NDT]
    i_g = ctx.col(2)
    GC = ctx.ACOL[:, i_g:i_g + 2]
    cinv = ctx.const_col(1.0 / 32768.0)
    grow = ctx.dtmp("tt_grow", [1, 2], F32)

    emit_ln(ctx, x_dram)

    stg = [ABIG[:, 32768 + i * 512:33280 + i * 512] for i in range(4)]
    aux = [ABIG[:, 34816 + i * 512:35328 + i * 512] for i in range(2)]
    st2 = [ABIG[:, 35840 + i * 512:36352 + i * 512] for i in range(2)]
    trst = [AF_[:, 7936 + i * 64:8000 + i * 64].bitcast(BF16) for i in range(2)]
    W0T = [ABIG[:, dt * 1024:(dt + 1) * 1024] for dt in range(NDT)]
    W1S = [ABIG[:, 8192 + dt * 1024:9216 + dt * 1024] for dt in range(NDT)]
    W1T = [ABIG[:, 16384 + dt * 1024:17408 + dt * 1024] for dt in range(NDT)]
    WOUT = [ABIG[:, 24576 + dt * 1024:25600 + dt * 1024] for dt in range(NDT)]
    ROT = W0T
    W0nT = W0T

    pool = PsumPool(sch, [PS[:, i * 512:(i + 1) * 512] for i in range(4)])
    state = {"st": {}, "aux": {}, "st2": {}}

    def load_slabs(dname, region):
        Wd = ctx.din(dname, [D, D], BF16)
        sch.wait("gpsimd", "T")
        toks = []
        for dt in range(NDT):
            toks.append(sch.dma("gpsimd", region[dt], Wd[dt * 128:(dt + 1) * 128, :]))
        return toks

    def pass_T(lhs_slabs, rhs_src, evac):
        for c in range(NTC):
            for m in range(NDT):
                ps = pool.acquire()
                for dt in range(NDT):
                    rhs = rhs_src(dt, c)
                    t_mm(sch, ps, lhs_slabs[dt][:, m * 128:(m + 1) * 128], rhs,
                         dt == 0, dt == NDT - 1)
                evac(m, c, ps)

    def stream_blocks(dram, tag, tmin=None):
        slots = [AMID[:, i * 4096:(i + 1) * 4096] for i in range(2)]
        def load(c):
            blk = slots[c % 2]
            if tmin is not None and c < 2:
                sch.wait("sync", "T", tmin)
            key = (tag, c - 2)
            if key in state:
                sch.wait("sync", "T", state[key])
            cc = c % NTC
            tok = sch.dma("sync", blk,
                          dram[:, cc * 512:(cc + 1) * 512]
                          .rearrange("(dt p) t -> p dt t", p=128))
            return blk, tok
        return load

    def mk_evac_sc(dst_dram, func):
        def evac(m, c, ps):
            sl = stg[(m * NTC + c) % 4]
            sch.wait("scalar", "T")
            tok = state["st"].get((m * NTC + c) % 4)
            if tok is not None:
                sch.wait_tok("scalar", tok)
            scc = s_act(sch, sl, ps, func)
            pool.release("SC", scc)
            sch.wait("sync", "SC", scc)
            state["st"][(m * NTC + c) % 4] = sch.dma(
                "sync", dst_dram[m * 128:(m + 1) * 128, c * 512:(c + 1) * 512], sl)
        return evac

    sch.wait("gpsimd", "V")
    load_slabs(pfx + "_WkT", ROT)
    load_slabs(pfx + "_WvT", W1S)
    load_slabs(pfx + "_WqT", W1T)
    sch.wait("tensor", "V")
    sch.wait_queue("tensor", "gpsimd")
    pass_T(ROT, lambda dt, c: ctx.xlnT(dt)[:, c * 512:(c + 1) * 512],
           mk_evac_sc(dKT, AF.Identity))

    def pass_row_from_xln(rhs_slabs, evac):
        for tt in range(32):
            for n in range(2):
                ps = pool.acquire()
                for dt in range(NDT):
                    t_mm(sch, ps, ctx.xlnT(dt)[:, tt * 128:(tt + 1) * 128],
                         rhs_slabs[dt][:, n * 512:(n + 1) * 512],
                         dt == 0, dt == NDT - 1)
                evac(tt, n, ps)

    def mk_evac_row(dst_dram, func):
        def evac(tt, n, ps):
            sl = stg[(tt * 2 + n) % 4]
            sch.wait("scalar", "T")
            tok = state["st"].get((tt * 2 + n) % 4)
            if tok is not None:
                sch.wait_tok("scalar", tok)
            scc = s_act(sch, sl, ps, func)
            pool.release("SC", scc)
            sch.wait("sync", "SC", scc)
            state["st"][(tt * 2 + n) % 4] = sch.dma(
                "sync", dst_dram[tt * 128:(tt + 1) * 128, n * 512:(n + 1) * 512], sl)
        return evac

    pass_row_from_xln(ROT, mk_evac_row(dKR, AF.Identity))

    pass_T(W1S, lambda dt, c: ctx.xlnT(dt)[:, c * 512:(c + 1) * 512],
           mk_evac_sc(dVT, AF.Identity))
    pass_T(W1T, lambda dt, c: ctx.xlnT(dt)[:, c * 512:(c + 1) * 512],
           mk_evac_sc(dQT, AF.Identity))

    sch.wait("vector", "T")
    for dt in range(NDT):
        v_red(sch, XS[:, dt:dt + 1], ctx.xlnT(dt), ALU.add)

    load_slabs(pfx + "_W0T", W0T)
    load_slabs(pfx + "_W1T", W1T)
    load_slabs(pfx + "_W1", W1S)
    load_slabs(pfx + "_WoutT", WOUT)
    sch.wait_queue("tensor", "gpsimd")
    sch.barrier()

    ldK = stream_blocks(dKT, "kT")
    def evacs_pre1T(m, c, ps):
        mk_evac_sc(dA1T, AF.Silu)(m, c, ps)
    for c in range(NTC):
        blk, tok = ldK(c)
        sch.wait_tok("tensor", tok)
        for m in range(NDT):
            ps = pool.acquire()
            for dt in range(NDT):
                t_mm(sch, ps, W0T[dt][:, m * 128:(m + 1) * 128],
                     blk[:, dt * 512:(dt + 1) * 512], dt == 0, dt == NDT - 1)
            evacs_pre1T(m, c, ps)
        state[("kT", c)] = sch.cnt["T"]

    for c in range(NTC):
        blk, tok = ldK(c + NTC)
        sch.wait_tok("tensor", tok)
        for sub in range(4):
            tt = c * 4 + sub
            for n in range(2):
                ps = pool.acquire()
                for dt in range(NDT):
                    t_mm(sch, ps, blk[:, dt * 512 + sub * 128:dt * 512 + sub * 128 + 128],
                         W0T[dt][:, n * 512:(n + 1) * 512], dt == 0, dt == NDT - 1)
                i2 = (tt * 2 + n) % 2
                sch.wait("scalar", "T")
                sgm = st2[i2]
                if ("sgm", i2) in state:
                    sch.wait("scalar", "V", state[("sgm", i2)])
                s1 = s_act(sch, sgm, ps, AF.Sigmoid)
                sch.wait("vector", "SC", s1)
                sch.wait_tok("vector", state["st"].get(i2))
                sch.wait_tok("vector", state["aux"].get(i2))
                v_tt(sch, stg[i2], ps, sgm, ALU.mult)
                tmp4 = AF_[:, 3072 + i2 * 512:3584 + i2 * 512]
                v_tt(sch, tmp4, stg[i2], sgm, ALU.mult)
                v_tt(sch, tmp4, stg[i2], tmp4, ALU.subtract)
                s2 = v_tt(sch, aux[i2], sgm, tmp4, ALU.add)
                state[("sgm", i2)] = s2
                pool.release("V", s2)
                sch.wait("sync", "V", s2)
                state["st"][i2] = sch.dma(
                    "sync", dA1R[tt * 128:(tt + 1) * 128, n * 512:(n + 1) * 512], stg[i2])
                state["aux"][i2] = sch.dma(
                    "sync", dSDR[tt * 128:(tt + 1) * 128, n * 512:(n + 1) * 512], aux[i2])
        state[("kT", c + NTC)] = sch.cnt["T"]

    sch.wait_queue("sync", "sync")
    ldA1 = stream_blocks(dA1T, "a1T")
    for c in range(NTC):
        blk, tok = ldA1(c)
        sch.wait_tok("tensor", tok)
        for m in range(NDT):
            ps = pool.acquire()
            for dt in range(NDT):
                t_mm(sch, ps, W1T[dt][:, m * 128:(m + 1) * 128],
                     blk[:, dt * 512:(dt + 1) * 512], dt == 0, dt == NDT - 1)
            i2 = (m * NTC + c) % 2
            if ("vtfree", i2) in state:
                sch.wait("sync", "V", state[("vtfree", i2)])
            sch.wait_tok("sync", state["aux"].get(i2))
            tokv = sch.dma("sync", aux[i2],
                           dVT[m * 128:(m + 1) * 128, c * 512:(c + 1) * 512])
            state["aux"][i2] = tokv
            sch.wait("vector", "T")
            sch.wait_tok("vector", tokv)
            sch.wait_tok("vector", state["st"].get(i2))
            v_tt(sch, stg[i2], ps, aux[i2], ALU.subtract)
            v_ts(sch, stg[i2], stg[i2], 5.0, -5.0, ALU.min, ALU.max)
            vcl = v_ts(sch, stg[i2], stg[i2], SC2, None, ALU.mult)
            state[("vtfree", i2)] = vcl
            pool.release("V", vcl)
            sch.wait("sync", "V", vcl)
            state["st"][i2] = sch.dma(
                "sync", dD2T[m * 128:(m + 1) * 128, c * 512:(c + 1) * 512], stg[i2])
            for j in range(4):
                jb = (m * NTC * 4 + c * 4 + j) % 2
                ptr = PS[:, 2048 + jb * 512:2048 + jb * 512 + 64].bitcast(BF16)
                tokp = state.get(("trv", jb))
                sch.wait("tensor", "V", vcl)
                if tokp is not None:
                    sch.wait("tensor", "V", tokp)
                ttr = t_tr(sch, ptr, stg[i2][0:128, j * 128:(j + 1) * 128],
                           ctx.IDEN[:, :])
                sch.wait("vector", "T", ttr)
                tokd = state.get(("trd", jb))
                if tokd is not None:
                    sch.wait_tok("vector", tokd)
                vc = v_copy(sch, trst[jb], ptr)
                state[("trv", jb)] = vc
                sch.wait("sync", "V", vc)
                state[("trd", jb)] = sch.dma(
                    "sync", dD2R[c * 512 + j * 128:c * 512 + (j + 1) * 128,
                                 m * 128:(m + 1) * 128], trst[jb])
        state[("a1T", c)] = sch.cnt["T"]

    sch.wait_queue("sync", "sync")
    ldD2 = stream_blocks(dD2T, "d2T")
    for c in range(NTC):
        blk, tok = ldD2(c)
        sch.wait_tok("tensor", tok)
        for sub in range(4):
            tt = c * 4 + sub
            for n in range(2):
                ps = pool.acquire()
                for dt in range(NDT):
                    t_mm(sch, ps, blk[:, dt * 512 + sub * 128:dt * 512 + sub * 128 + 128],
                         W1S[dt][:, n * 512:(n + 1) * 512], dt == 0, dt == NDT - 1)
                i2 = (tt * 2 + n) % 2
                if ("sdsfree", i2) in state:
                    sch.wait("sync", "V", state[("sdsfree", i2)])
                sch.wait_tok("sync", state["aux"].get(i2))
                toks = sch.dma("sync", aux[i2],
                               dSDR[tt * 128:(tt + 1) * 128, n * 512:(n + 1) * 512])
                state["aux"][i2] = toks
                sch.wait("vector", "T")
                sch.wait_tok("vector", toks)
                sch.wait_tok("vector", state["st"].get(i2))
                vm = v_tt(sch, stg[i2], ps, aux[i2], ALU.mult)
                state[("sdsfree", i2)] = vm
                pool.release("V", vm)
                sch.wait("sync", "V", vm)
                state["st"][i2] = sch.dma(
                    "sync", dD1R[tt * 128:(tt + 1) * 128, n * 512:(n + 1) * 512], stg[i2])
        state[("d2T", c)] = sch.cnt["T"]

    def grad_pass(lhs_dram, rhs_dram, out_dram):
        sch.wait_queue("sync", "sync")
        for n in range(2):
            rblk = AMID[:, 0:16384]
            sch.wait("sync", "T")
            rt = sch.dma("sync", rblk,
                         rhs_dram[:, n * 512:(n + 1) * 512]
                         .rearrange("(tt p) e -> p tt e", p=128))
            for m in range(NDT):
                lblk = AMID[:, 16384 + (m % 2) * 4096:16384 + (m % 2) * 4096 + 4096]
                g = state.get(("glb", m % 2), 0)
                if g:
                    sch.wait("sync", "T", g)
                lt = sch.dma("sync", lblk,
                             lhs_dram[:, m * 128:(m + 1) * 128]
                             .rearrange("(tt p) e -> p tt e", p=128))
                ps = pool.acquire()
                sch.wait_tok("tensor", rt)
                sch.wait_tok("tensor", lt)
                for tt in range(32):
                    t_mm(sch, ps, lblk[:, tt * 128:(tt + 1) * 128],
                         rblk[:, tt * 512:(tt + 1) * 512], tt == 0, tt == 31)
                state[("glb", m % 2)] = sch.cnt["T"]
                sl = AF_[:, (m % 2) * 512:(m % 2) * 512 + 512]
                sch.wait("vector", "T")
                tokd = state.get(("gev", m % 2))
                if tokd is not None:
                    sch.wait_tok("vector", tokd)
                vc = v_copy(sch, sl, ps)
                pool.release("V", vc)
                sch.wait("sync", "V", vc)
                state[("gev", m % 2)] = sch.dma(
                    "sync", out_dram[m * 128:(m + 1) * 128,
                                     n * 512:(n + 1) * 512], sl)
            state[("grb", 0)] = sch.cnt["T"]

    def do_allreduce(ins, outs):
        sch.wait_queue("gpsimd", "sync")
        sch.wait_queue("gpsimd", "gpsimd")
        sch.sem("CC")
        sch.cnt["CC"] += 1
        sch.q["gpsimd"].append(("op", lambda e, i=ins, o=outs:
                                e.collective_compute(
            "AllReduce", ALU.add, replica_groups=[list(range(NCORES))],
            ins=[i.opt()], outs=[o.opt()]), "CC", 1))
        return sch.cnt["CC"]

    sch.wait("gpsimd", "V")
    for j in range(NDT):
        sch.dma("gpsimd", AR1_in[D:D + 1, j * 128:(j + 1) * 128], XS[:, j:j + 1])
    grad_pass(dKR, dD1R, AR1_in)
    cc1 = do_allreduce(AR1_in, AR1_out)
    grad_pass(dA1R, dD2R, AR2_in)
    t_g1 = sch.cnt["T"]
    cc2 = do_allreduce(AR2_in, AR2_out)

    sch.wait("sync", "CC", cc1)
    for j in range(NDT):
        tok = sch.dma("sync", XS[:, j:j + 1],
                      AR1_out[D:D + 1, j * 128:(j + 1) * 128])
        state[("xs", j)] = tok
    sch.wait("vector", "CC", cc1)
    for j in range(NDT):
        sch.wait_tok("vector", state[("xs", j)])
    r0 = AF_[:, 0:8]
    rr = AF_[:, 8:10]
    for gi, wg in enumerate([WgD, WgL]):
        v_tt(sch, r0, XS, wg, ALU.mult)
        v_red(sch, rr[:, gi:gi + 1], r0, ALU.add)
    vv = sch.cnt["V"]
    sch.wait("tensor", "V", vv)
    t_mm(sch, PS[0:1, 3072:3073], ctx.ONEF[:, 0:1], rr[:, 0:1], True, True)
    tg = t_mm(sch, PS[0:1, 3073:3074], ctx.ONEF[:, 0:1], rr[:, 1:2],
              True, True)
    sch.wait("scalar", "T", tg)
    sa = s_act(sch, AF_[0:1, 16:17], PS[0:1, 3072:3073], AF.Sigmoid,
               bias=ctx.ACOL[0:1, i_bg:i_bg + 1], scale=cinv[0:1, :])
    sb = s_act(sch, AF_[0:1, 17:18], PS[0:1, 3073:3074], AF.Sigmoid,
               bias=ctx.ACOL[0:1, i_bg + 1:i_bg + 2], scale=cinv[0:1, :])
    sch.wait("vector", "SC", sb)
    v_ts(sch, AF_[0:1, 18:19], AF_[0:1, 16:17], -1.0, 1.0, ALU.mult, ALU.add)
    vg = v_ts(sch, AF_[0:1, 19:20], AF_[0:1, 17:18], -1.0, None, ALU.mult)
    sch.wait("sync", "V", vg)
    wg1 = sch.dma("sync", grow[0:1, 0:1], AF_[0:1, 18:19])
    wg2 = sch.dma("sync", grow[0:1, 1:2], AF_[0:1, 19:20])
    sch.wait_tok("sync", wg1)
    sch.wait_tok("sync", wg2)
    gb = sch.dma("sync", GC, grow[0:1, 0:2].to_broadcast((128, 2)))

    def build_Wn(src_out, slabs, ccv):
        sch.wait("sync", "CC", ccv)
        sch.wait_tok("vector", gb)
        vw = 0
        for dt in range(NDT):
            gl = AF_[:, 1024:2048]
            if vw:
                sch.wait("sync", "V", vw)
            tok = sch.dma("sync", gl, src_out[dt * 128:(dt + 1) * 128, :])
            sch.wait_tok("vector", tok)
            v_ts(sch, gl, gl, 1.0, -1.0, ALU.min, ALU.max)
            t1 = AF_[:, 2048:3072]
            v_ts(sch, t1, slabs[dt], GC[:, 0:1], None, ALU.mult)
            vw = v_stt(sch, slabs[dt], gl, GC[:, 1:2], t1, ALU.mult, ALU.add)
        return vw

    vw0 = build_Wn(AR1_out, W0T, cc1)

    ldQ = stream_blocks(dQT, "qT", tmin=t_g1)
    sch.wait("tensor", "V", vw0)
    for c in range(NTC):
        blk, tok = ldQ(c)
        sch.wait_tok("tensor", tok)
        for m in range(NDT):
            ps = pool.acquire()
            for dt in range(NDT):
                t_mm(sch, ps, W0T[dt][:, m * 128:(m + 1) * 128],
                     blk[:, dt * 512:(dt + 1) * 512], dt == 0, dt == NDT - 1)
            mk_evac_sc(dHT, AF.Silu)(m, c, ps)
        state[("qT", c)] = sch.cnt["T"]
    vw1 = build_Wn(AR2_out, W1T, cc2)
    ldH = stream_blocks(dHT, "hT")
    sch.wait("tensor", "V", vw1)
    for c in range(NTC):
        blk, tok = ldH(c)
        sch.wait_tok("tensor", tok)
        for m in range(NDT):
            ps = pool.acquire()
            for dt in range(NDT):
                t_mm(sch, ps, W1T[dt][:, m * 128:(m + 1) * 128],
                     blk[:, dt * 512:(dt + 1) * 512], dt == 0, dt == NDT - 1)
            mk_evac_sc(dH2T, AF.Identity)(m, c, ps)
        state[("hT", c)] = sch.cnt["T"]
    ldH2 = stream_blocks(dH2T, "h2T")
    for c in range(NTC):
        blk, tok = ldH2(c)
        sch.wait_tok("tensor", tok)
        for m in range(NDT):
            ps = pool.acquire()
            for dt in range(NDT):
                t_mm(sch, ps, WOUT[dt][:, m * 128:(m + 1) * 128],
                     blk[:, dt * 512:(dt + 1) * 512], dt == 0, dt == NDT - 1)
            i2 = (m * NTC + c) % 2
            tokx = state.get(("xr", i2))
            if tokx is not None:
                sch.wait("sync", "V", state[("xrfree", i2)])
            xsl = AF_[:, i2 * 512:(i2 + 1) * 512]
            tokx = sch.dma("sync", xsl,
                           x_dram[m * 128:(m + 1) * 128, c * 512:(c + 1) * 512])
            state[("xr", i2)] = tokx
            osl = AF_[:, 1024 + i2 * 512:1536 + i2 * 512]
            sch.wait("vector", "T")
            sch.wait_tok("vector", tokx)
            toko = state.get(("xo", i2))
            if toko is not None:
                sch.wait_tok("vector", toko)
            vo = v_tt(sch, osl, ps, xsl, ALU.add)
            state[("xrfree", i2)] = vo
            pool.release("V", vo)
            sch.wait("sync", "V", vo)
            state[("xo", i2)] = sch.dma(
                "sync", xo_dram[m * 128:(m + 1) * 128, c * 512:(c + 1) * 512], osl)
        state[("h2T", c)] = sch.cnt["T"]



def emit_ffn1(ctx, x_dram, xo_dram, pfx="L1F"):
    nc, sch = ctx.nc, ctx.sch
    ABIG, AMID, AF_, PS, PSS = ctx.ABIG, ctx.AMID, ctx.AF, ctx.PS, ctx.PSS
    NH = 4096
    NMO = NH // 128

    s1 = load_cols(ctx, pfx + "_s1", 32)
    s2 = load_cols(ctx, pfx + "_s2", 32)
    sh1 = load_cols(ctx, pfx + "_sh1", 32)
    sh2 = load_cols(ctx, pfx + "_sh2", 32)
    s3 = load_cols(ctx, pfx + "_s3", NDT)
    w1T = ctx.din(pfx + "_w1P", [32, 131072], BF16)
    w2T = ctx.din(pfx + "_w2P", [32, 131072], BF16)
    h1T = ctx.din(pfx + "_h1P", [32, 524288], BF16)
    h2T = ctx.din(pfx + "_h2P", [32, 524288], BF16)
    w3T = ctx.din(pfx + "_w3P", [8, 524288], BF16)

    Pb = lambda mo: AMID[:, mo * 1024:(mo + 1) * 1024]
    Qb = lambda mo: ABIG[:, mo * 1024:(mo + 1) * 1024]
    xlnB = AF_[:, 0:4096].bitcast(BF16)
    xlnBt = lambda dt: xlnB[:, dt * 1024:(dt + 1) * 1024]
    wslot = [AF_[:, 0:2048].bitcast(BF16), AF_[:, 2048:4096].bitcast(BF16)]
    w1slot = [ABIG[:, 32768 + i * 2048:32768 + (i + 1) * 2048] for i in range(2)]
    ostg = [AF_[:, 4608 + i * 512:5120 + i * 512] for i in range(2)]
    xstg = [AF_[:, 5632 + i * 512:6144 + i * 512] for i in range(2)]
    tstg = [AF_[:, 7680:8192].bitcast(BF16)[:, i * 512:(i + 1) * 512]
            for i in range(2)]

    pool = PsumPool(sch, [PS[:, i * 512:(i + 1) * 512] for i in range(6)])
    st = {}

    for B in range(4):
        sch.wait("gpsimd", "V")
        sch.wait("gpsimd", "T")
        sch.barrier()
        emit_ln(ctx, x_dram, chunks=[2 * B, 2 * B + 1], dst=xlnBt)
        sch.wait("tensor", "V")

        for mo in range(NMO):
            i2 = mo % 2
            tk = st.get(("w1l", i2))
            if tk is not None:
                sch.wait("gpsimd", "T", st[("w1f", i2)])
            t1 = sch.dma("gpsimd", w1slot[i2][:, 0:1024],
                         w1T[mo:mo + 1, :].rearrange("a (p f) -> (a p) f", p=128))
            t2 = sch.dma("gpsimd", w1slot[i2][:, 1024:2048],
                         w2T[mo:mo + 1, :].rearrange("a (p f) -> (a p) f", p=128))
            st[("w1l", i2)] = t2
            sch.wait_tok("tensor", t1)
            sch.wait_tok("tensor", t2)
            for tc in range(2):
                ps1 = pool.acquire()
                for dt in range(NDT):
                    t_mm(sch, ps1, w1slot[i2][:, dt * 128:dt * 128 + 128],
                         xlnBt(dt)[:, tc * 512:(tc + 1) * 512], dt == 0, dt == NDT - 1)
                ps2 = pool.acquire()
                for dt in range(NDT):
                    t_mm(sch, ps2, w1slot[i2][:, 1024 + dt * 128:1024 + dt * 128 + 128],
                         xlnBt(dt)[:, tc * 512:(tc + 1) * 512], dt == 0, dt == NDT - 1)
                j2 = (mo * 2 + tc) % 2
                sch.wait("scalar", "T")
                tk2 = st.get(("tst", j2))
                if tk2 is not None:
                    sch.wait("scalar", "V", tk2)
                sc = s_act(sch, tstg[j2], ps1, AF.Silu, scale=s1[:, mo:mo + 1])
                sch.wait("vector", "SC", sc)
                vv = v_stt(sch, Pb(mo)[:, tc * 512:(tc + 1) * 512], ps2,
                           s2[:, mo:mo + 1], tstg[j2], ALU.mult, ALU.mult)
                st[("tst", j2)] = vv
                cur2 = pool.cur
                pool.cur = (cur2 - 1) % len(pool.slots)
                pool.release("SC", sc)
                pool.cur = cur2
                pool.release("V", vv)
            st[("w1f", i2)] = sch.cnt["T"]
        for stage, (Wd, scol, SRC, DST) in enumerate(
                [(h1T, sh1, Pb, Qb), (h2T, sh2, Qb, Pb)]):
            sch.wait("gpsimd", "V")
            sch.wait("gpsimd", "T")
            sch.wait("tensor", "V")
            sch.wait("tensor", "SC")
            for mo in range(NMO):
                i2 = mo % 2
                tk = st.get(("wl", i2))
                if tk is not None:
                    sch.wait("gpsimd", "T", st[("wf", i2)])
                toks = [sch.dma(
                    "gpsimd", wslot[i2],
                    Wd[mo:mo + 1, :].rearrange("a (p f) -> (a p) f", p=128))]
                st[("wl", i2)] = toks[-1]
                for tk2 in toks:
                    sch.wait_tok("tensor", tk2)
                for tc in range(2):
                    ps = pool.acquire()
                    for kt in range(NMO):
                        t_mm(sch, ps, wslot[i2][:, kt * 128:(kt + 1) * 128],
                             SRC(kt)[:, tc * 512:(tc + 1) * 512],
                             kt == 0, kt == NMO - 1)
                    sch.wait("scalar", "T")
                    sc = s_act(sch, DST(mo)[:, tc * 512:(tc + 1) * 512], ps,
                               AF.Gelu, scale=scol[:, mo:mo + 1])
                    pool.release("SC", sc)
                st[("wf", i2)] = sch.cnt["T"]
        sch.wait("gpsimd", "V")
        sch.wait("gpsimd", "T")
        sch.wait("tensor", "SC")
        for mo in range(NDT):
            i2 = mo % 2
            tk = st.get(("wl3", i2))
            if tk is not None:
                sch.wait("gpsimd", "T", st[("wf3", i2)])
            toks = [sch.dma(
                "gpsimd", wslot[i2],
                w3T[mo:mo + 1, :].rearrange("a (p f) -> (a p) f", p=128))]
            st[("wl3", i2)] = toks[-1]
            for tk2 in toks:
                sch.wait_tok("tensor", tk2)
            for tc in range(2):
                ps = pool.acquire()
                for kt in range(NMO):
                    t_mm(sch, ps, wslot[i2][:, kt * 128:(kt + 1) * 128],
                         Pb(kt)[:, tc * 512:(tc + 1) * 512], kt == 0, kt == NMO - 1)
                j2 = (mo * 2 + tc) % 2
                xf = st.get(("xf", j2))
                if xf is not None:
                    sch.wait("sync", "V", xf)
                xt = sch.dma("sync", xstg[j2],
                             x_dram[mo * 128:(mo + 1) * 128,
                                    B * 1024 + tc * 512:B * 1024 + (tc + 1) * 512])
                sch.wait("vector", "T")
                sch.wait_tok("vector", xt)
                of = st.get(("of", j2))
                if of is not None:
                    sch.wait_tok("vector", of)
                vo = v_stt(sch, ostg[j2], ps, s3[:, mo:mo + 1], xstg[j2],
                           ALU.mult, ALU.add)
                st[("xf", j2)] = vo
                pool.release("V", vo)
                sch.wait("sync", "V", vo)
                st[("of", j2)] = sch.dma(
                    "sync", xo_dram[mo * 128:(mo + 1) * 128,
                                    B * 1024 + tc * 512:B * 1024 + (tc + 1) * 512],
                    ostg[j2])
            st[("wf3", i2)] = sch.cnt["T"]



def emit_mamba(ctx, x_dram, xo_dram):
    nc, sch = ctx.nc, ctx.sch
    ABIG, AMID, AF_, PS, PSS = ctx.ABIG, ctx.AMID, ctx.AF, ctx.PS, ctx.PSS
    pfx = "L1M"
    NS = 16
    TH = 2048

    WdbT = ctx.din(pfx + "_WdbT", [D, 96], BF16)
    WdtT = ctx.din(pfx + "_WdtT", [64, D], BF16)
    bdt = load_cols(ctx, pfx + "_bdt", NDT)
    Dcol = load_cols(ctx, pfx + "_Dcol", NDT)
    Acols_d = ctx.din(pfx + "_Acols", [128, 128], F32)
    brow = ctx.dtmp("mm_brow", [2, TH], F32)

    yacc = ABIG[:, 0:16384].bitcast(F32)
    d2q = ABIG[:, 16384:24576]
    dAb = ABIG[:, 24576:28672].bitcast(F32)
    BXb = ABIG[:, 28672:30720]
    hbuf = ABIG[:, 32768:36864].bitcast(F32)
    ymul = ABIG[:, 30720:32768]
    d1T = AF_[:, 2048:3072].bitcast(BF16)
    BT = AF_[:, 3072:4096].bitcast(BF16)
    CT = AF_[:, 4096:5120].bitcast(BF16)
    hlast = AF_[:, 5120:5248]
    stmp = [AF_[:, 5248 + i * 512:5760 + i * 512] for i in range(2)]
    Bb = AF_[:, 0:1024].bitcast(BF16)
    Cb = AF_[:, 1024:2048].bitcast(BF16)
    Acol = AF_[:, 6272:6400]
    xstg = [AF_[:, 8192 + i * 512:8704 + i * 512] for i in range(2)]
    ostg = [AF_[:, 6400 + 0:6912], AF_[:, 6912:7424]]
    sWdb = ABIG[:, 16384:17152]
    sWdt = AF_[:, 7424:7936].bitcast(BF16)

    emit_ln(ctx, x_dram)
    v_ln_done = sch.cnt["V"]
    sch.wait("gpsimd", "V")
    at = sch.dma("sync", Acol, Acols_d)
    wdt_t = sch.dma("gpsimd", sWdt[0:64, :], WdtT)
    sch.wait("scalar", "V")
    sch.wait_tok("scalar", at)
    sa = s_act(sch, Acol, Acol, AF.Exp)
    sch.wait("vector", "SC", sa)
    van = v_ts(sch, Acol, Acol, -1.0, None, ALU.mult)

    st = {}
    for th in range(2):
        t0 = th * TH
        sch.wait("gpsimd", "T")
        sch.wait("gpsimd", "V")
        wdb_t = []
        for dt in range(NDT):
            wdb_t.append(sch.dma("gpsimd", sWdb[:, dt * 96:(dt + 1) * 96],
                                 WdbT[dt * 128:(dt + 1) * 128, :]))
        for w in wdb_t:
            sch.wait_tok("tensor", w)
        sch.wait("tensor", "V")
        for tc in range(4):
            cs = t0 + tc * 512
            for dt in range(NDT):
                t_mm(sch, PS[0:64, 0:512], sWdb[:, dt * 96:dt * 96 + 64],
                     ctx.xlnT(dt)[:, cs:cs + 512], dt == 0, dt == NDT - 1)
            for dt in range(NDT):
                t_mm(sch, PS[0:16, 512:1024], sWdb[:, dt * 96 + 64:dt * 96 + 80],
                     ctx.xlnT(dt)[:, cs:cs + 512], dt == 0, dt == NDT - 1)
            for dt in range(NDT):
                tb = t_mm(sch, PS[0:16, 1024:1536], sWdb[:, dt * 96 + 80:dt * 96 + 96],
                          ctx.xlnT(dt)[:, cs:cs + 512], dt == 0, dt == NDT - 1)
            sch.wait("vector", "T", tb)
            v_copy(sch, d1T[0:64, tc * 512:(tc + 1) * 512], PS[0:64, 0:512])
            v_copy(sch, BT[0:16, tc * 512:(tc + 1) * 512], PS[0:16, 512:1024])
            dbc = v_copy(
                sch, CT[0:16, tc * 512:(tc + 1) * 512], PS[0:16, 1024:1536])
            sch.wait("tensor", "V", dbc)
        for dh in range(2):
            sch.wait_tok("tensor", wdt_t)
            sch.wait("tensor", "V")
            sch.wait("tensor", "SC")
            for dj in range(4):
                gdt = dh * 4 + dj
                for tc in range(4):
                    if ("d2s", tc % 2) in st:
                        sch.wait("tensor", "V", st[("d2s", tc % 2)][0])
                    tm = t_mm(sch, PS[:, 1536 + (tc % 2) * 512:2048 + (tc % 2) * 512],
                              sWdt[0:64, gdt * 128:(gdt + 1) * 128],
                              d1T[0:64, tc * 512:(tc + 1) * 512], True, True)
                    ps = PS[:, 1536 + (tc % 2) * 512:2048 + (tc % 2) * 512]
                    sch.wait("scalar", "T", tm)
                    if ("d2s", tc % 2) in st:
                        sch.wait("scalar", "V", st[("d2s", tc % 2)][0])
                    b_c = bdt[:, gdt:gdt + 1]
                    s_act(sch, stmp[tc % 2], ps, AF.Abs, bias=b_c)
                    s_act(sch, stmp[tc % 2], stmp[tc % 2], AF.Exp,
                          scale=ctx.const_col(-1.0))
                    s_act(sch, stmp[tc % 2], stmp[tc % 2], AF.Ln,
                          bias=ctx.const_col(1.0))
                    sr = s_act(sch, ostg[tc % 2], ps, AF.Relu, bias=b_c)
                    sch.wait("vector", "SC", sr)
                    vv = v_tt(sch, d2q[:, dj * 2048 + tc * 512:dj * 2048 + (tc + 1) * 512],
                              stmp[tc % 2], ostg[tc % 2], ALU.add)
                    st[("d2s", tc % 2)] = (vv,)
            BXs = [BXb, AF_[:, 5248:6272].bitcast(BF16)]
            iters = [(s, dj) for s in range(NS) for dj in range(4)]

            def issue_dx(i):
                s_, dj_ = iters[i]
                gdt_ = dh * 4 + dj_
                if i == 0:
                    sch.wait("gpsimd", "V")
                prev = st.get(("bxfree", i % 2))
                if prev is not None:
                    sch.wait("gpsimd", "V", prev)
                dsl = d2q[:, dj_ * 2048:(dj_ + 1) * 2048]
                st[("dxtok", i)] = g_tt(sch, BXs[i % 2], dsl,
                                        ctx.xlnT(gdt_)[:, t0:t0 + TH], ALU.mult)

            issue_dx(0)
            r1 = r2 = None
            for i, (s, dj) in enumerate(iters):
                gdt = dh * 4 + dj
                if dj == 0:
                    vprev = st.get("bc_used")
                    if vprev is not None:
                        sch.wait("gpsimd", "V", vprev)
                    b1 = sch.dma("gpsimd", brow[0:1, :], BT[s:s + 1, :])
                    b2 = sch.dma("gpsimd", brow[1:2, :], CT[s:s + 1, :])
                    sch.wait_tok("gpsimd", b1)
                    sch.wait_tok("gpsimd", b2)
                    r1 = sch.dma("gpsimd", Bb, brow[0:1, :].to_broadcast((128, TH)))
                    r2 = sch.dma("gpsimd", Cb, brow[1:2, :].to_broadcast((128, TH)))
                dslice = d2q[:, dj * 2048:(dj + 1) * 2048]
                sch.wait("scalar", "V", van)
                if "dA_used" in st:
                    sch.wait("scalar", "V", st["dA_used"])
                sdA = s_act(sch, dAb, dslice, AF.Exp,
                            scale=Acol[:, gdt * 16 + s:gdt * 16 + s + 1])
                sch.wait_tok("vector", r1)
                sch.wait_tok("vector", r2)
                sch.wait("vector", "P", st[("dxtok", i)])
                bx = BXs[i % 2]
                vbx = v_tt(sch, bx, bx, Bb, ALU.mult)
                if i + 1 < len(iters):
                    issue_dx(i + 1)
                sch.wait("vector", "SC", sdA)
                col = s * 8 + gdt
                init = 0.0 if th == 0 else hlast[:, col:col + 1]
                vsc = v_scan(sch, hbuf, dAb, bx, init)
                st["dA_used"] = vsc
                st[("bxfree", i % 2)] = vsc
                vcar = v_copy(sch, hlast[:, col:col + 1], hbuf[:, TH - 1:TH])
                ya = yacc[:, dj * 2048:(dj + 1) * 2048]
                if s == 0:
                    vy = v_tt(sch, ya, hbuf, Cb, ALU.mult)
                else:
                    if "ymul_free" in st:
                        sch.wait("vector", "P", st["ymul_free"])
                    vym = v_tt(sch, ymul, hbuf, Cb, ALU.mult)
                    sch.wait("gpsimd", "V", vym)
                    vy = sch.op("gpsimd", lambda e, o=ya, a=ya, b=ymul:
                                e.tensor_tensor(o, a, b, ALU.add), inc=True)
                    st["ymul_free"] = sch.cnt["P"]
                st["bc_used"] = sch.cnt["V"]
            sch.wait("vector", "P")
            for dj in range(4):
                gdt = dh * 4 + dj
                for tc in range(4):
                    cs = t0 + tc * 512
                    j2 = tc % 2
                    xf = st.get(("mxf", j2))
                    if xf is not None:
                        sch.wait("sync", "V", xf)
                    else:
                        sch.wait("sync", "V", v_ln_done)
                    xt = sch.dma("sync", xstg[j2],
                                 x_dram[gdt * 128:(gdt + 1) * 128, cs:cs + 512])
                    sch.wait_tok("vector", xt)
                    of = st.get(("mof", j2))
                    if of is not None:
                        sch.wait_tok("vector", of)
                    v_stt(sch, ostg[j2],
                          ctx.xlnT(gdt)[:, cs:cs + 512], Dcol[:, gdt:gdt + 1],
                          yacc[:, dj * 2048 + tc * 512:dj * 2048 + (tc + 1) * 512],
                          ALU.mult, ALU.add)
                    vo = v_tt(sch, ostg[j2], ostg[j2], xstg[j2], ALU.add)
                    st[("mxf", j2)] = vo
                    sch.wait("sync", "V", vo)
                    st[("mof", j2)] = sch.dma(
                        "sync", xo_dram[gdt * 128:(gdt + 1) * 128, cs:cs + 512],
                        ostg[j2])
            sch.barrier()



def emit_tno(ctx, x_dram, xo_dram):
    nc, sch = ctx.nc, ctx.sch
    ABIG, AMID, AF_, PS, PSS = ctx.ABIG, ctx.AMID, ctx.AF, ctx.PS, ctx.PSS
    pfx = "L0T"
    NK = 33

    cosF = ctx.din("C_cosF", [NK, 4096 * 128], BF16)
    sinF = ctx.din("C_sinF", [NK, 4096 * 128], BF16)
    icosT = ctx.din("C_icos", [32, NK * 128 * 128], BF16)
    isinT = ctx.din("C_isin", [32, NK * 128 * 128], BF16)
    ccol_d = ctx.din("C_ccol", [128, NK], F32)
    pos_d = ctx.din("C_pos", [128, 64], F32)
    gam_d = ctx.din("C_gam", [128, 64], F32)
    rpeb_d = ctx.din("C_rpeb", [128, 320], F32)
    fc_d = ctx.din("C_fc", [32, 96], F32)
    uvwu_d = ctx.din(pfx + "_uvwu", [NDT, 131072], BF16)
    uvwv_d = ctx.din(pfx + "_uvwv", [D, D], BF16)
    ow_d = ctx.din(pfx + "_ow", [D, D], BF16)
    outw_d = ctx.din(pfx + "_outw", [32, D], BF16)

    dUT = ctx.dtmp("tn_uT", [D, T], BF16)
    dVR = ctx.dtmp("tn_vR", [T, D], BF16)
    dYT = ctx.dtmp("tn_yT", [D, T], BF16)

    i_c = ctx.col(NK)
    ccol = ctx.ACOL[:, i_c:i_c + NK]
    st = {}
    h3 = lambda ap: ap.rearrange("p (j f) -> p j f", f=32)

    emit_ln(ctx, x_dram)

    SrT = AF_[:, 0:2112].bitcast(BF16)
    SiT = AF_[:, 2112:4224].bitcast(BF16)
    H = AF_[:, 4224:6272]
    T2 = AF_[:, 6272:8320]
    MU = AF_[:, 8320:8384]
    VA = AF_[:, 8384:8448]
    RS = AF_[:, 8448:8512]
    RPB = AF_[:, 8512:8832]
    POSG = AF_[:, 8832:8896]
    GAMG = AF_[:, 8896:8960]
    FCW = AF_[0:32, 9088:9184]
    sT_t = ABIG[:, 33792:34048].bitcast(F32)
    gsb = ABIG[:, 34816:35840]
    gdb = ABIG[:, 35840:36864]
    posb = lambda i: RPB[:, i * 32:(i + 1) * 32]

    sch.wait("sync", "V")
    toks = [sch.dma("sync", RPB, rpeb_d), sch.dma("sync", POSG, pos_d),
            sch.dma("sync", GAMG, gam_d), sch.dma("sync", FCW, fc_d),
            sch.dma("sync", ccol, ccol_d)]
    for tk in toks:
        sch.wait_tok("vector", tk)
    v_tt(sch, h3(H), posb(0).unsqueeze(1).to_broadcast((128, 64, 32)),
         POSG.unsqueeze(2).to_broadcast((128, 64, 32)), ALU.mult)
    for layer in range(4):
        sch.wait("scalar", "V")
        s_sq = s_act(sch, T2, H, AF.Square)
        v_red(sch, MU, h3(H), ALU.add)
        sch.wait("vector", "SC", s_sq)
        v_red(sch, VA, h3(T2), ALU.add)
        v_ts(sch, MU, MU, 1.0 / 32, None, ALU.mult)
        v_ts(sch, VA, VA, 1.0 / 32, None, ALU.mult)
        v_tt(sch, RS, MU, MU, ALU.mult)
        vv = v_tt(sch, VA, VA, RS, ALU.subtract)
        sch.wait("scalar", "V", vv)
        s_act(sch, VA, VA, AF.Ln, bias=ctx.const_col(EPS))
        sr = s_act(sch, RS, VA, AF.Exp, scale=ctx.const_col(-0.5))
        sch.wait("vector", "SC", sr)
        v_tt(sch, h3(T2), h3(H), MU.unsqueeze(2).to_broadcast((128, 64, 32)),
             ALU.subtract)
        v_tt(sch, h3(T2), h3(T2), RS.unsqueeze(2).to_broadcast((128, 64, 32)),
             ALU.mult)
        wi = 1 + layer if layer < 3 else 7
        bi = 4 + layer if layer < 3 else 8
        v_tt(sch, h3(T2), h3(T2),
             posb(wi).unsqueeze(1).to_broadcast((128, 64, 32)), ALU.mult)
        vb = v_tt(sch, h3(T2), h3(T2),
                  posb(bi).unsqueeze(1).to_broadcast((128, 64, 32)), ALU.add)
        sch.wait("scalar", "V", vb)
        ss = s_act(sch, T2, T2, AF.Silu)
        if layer == 3:
            break
        sch.wait("vector", "SC", ss)
        sch.wait("tensor", "SC", ss)
        for jt in range(64):
            tt1 = t_tr(sch, PS[0:32, 1024:1152], T2[:, jt * 32:(jt + 1) * 32],
                       ctx.IDENF[:, :])
            sch.wait("vector", "T", tt1)
            ve = v_copy(sch, sT_t[0:32, :], PS[0:32, 1024:1152])
            sch.wait("tensor", "V", ve)
            tm = t_mm(sch, PS[:, 1536:1568], sT_t[0:32, :],
                      FCW[0:32, layer * 32:(layer + 1) * 32], True, True)
            sch.wait("vector", "T", tm)
            v_copy(sch, H[:, jt * 32:(jt + 1) * 32], PS[:, 1536:1568])
    sch.wait("vector", "SC")
    v_tt(sch, h3(T2), h3(T2), GAMG.unsqueeze(2).to_broadcast((128, 64, 32)),
         ALU.mult)
    v_tt(sch, gsb, T2[:, 0:1024], T2[:, 1024:2048], ALU.add)
    vgd = v_tt(sch, gdb, T2[:, 0:1024], T2[:, 1024:2048], ALU.subtract)

    tbs = [ABIG[:, 24576 + i * 4096:24576 + (i + 1) * 4096] for i in range(2)]
    sch.wait("tensor", "V", vgd)
    for tab, gg, dst in [(cosF, gsb, SrT), (sinF, gdb, SiT)]:
        for kc in range(NK):
            i2 = kc % 2
            if ("tbf", i2) in st:
                sch.wait("sync", "T", st[("tbf", i2)])
            tc1 = sch.dma("sync", tbs[i2],
                          tab[kc:kc + 1, :].rearrange(
                              "a (p f) -> (a p) f", p=128))
            sch.wait_tok("tensor", tc1)
            if ("sev", kc % 2) in st:
                sch.wait("tensor", "V", st[("sev", kc % 2)])
            for jt in range(32):
                t_mm(sch, PS[0:32, 2048 + (kc % 2) * 512:2176 + (kc % 2) * 512],
                     gg[:, jt * 32:(jt + 1) * 32],
                     tbs[i2][:, jt * 128:(jt + 1) * 128], jt == 0, jt == 31)
            st[("tbf", i2)] = sch.cnt["T"]
            sch.wait("vector", "T")
            st[("sev", kc % 2)] = v_copy(
                sch, dst[0:32, kc * 128:(kc + 1) * 128],
                PS[0:32, 2048 + (kc % 2) * 512:2176 + (kc % 2) * 512])
    sch.barrier()

    uvs = [ABIG[:, i * 1024:(i + 1) * 1024] for i in range(2)]
    stg = [ABIG[:, 2048 + i * 512:2560 + i * 512] for i in range(2)]
    pool = PsumPool(sch, [PS[:, i * 512:(i + 1) * 512] for i in range(4)])
    for mo in range(NDT):
        i2 = mo % 2
        if ("uvf", i2) in st:
            sch.wait("gpsimd", "T", st[("uvf", i2)])
        tku = sch.dma("gpsimd", uvs[i2],
                      uvwu_d[mo:mo + 1, :].rearrange("a (p f) -> (a p) f", p=128))
        sch.wait_tok("tensor", tku)
        for tc in range(NTC):
            ps = pool.acquire()
            for dt in range(NDT):
                t_mm(sch, ps, uvs[i2][:, dt * 128:(dt + 1) * 128],
                     ctx.xlnT(dt)[:, tc * 512:(tc + 1) * 512], dt == 0, dt == NDT - 1)
            j2 = (mo * NTC + tc) % 2
            sch.wait("scalar", "T")
            tk2 = st.get(("ust", j2))
            if tk2 is not None:
                sch.wait_tok("scalar", tk2)
            sc = s_act(sch, stg[j2], ps, AF.Silu)
            pool.release("SC", sc)
            sch.wait("sync", "SC", sc)
            st[("ust", j2)] = sch.dma(
                "sync", dUT[mo * 128:(mo + 1) * 128, tc * 512:(tc + 1) * 512],
                stg[j2])
        st[("uvf", i2)] = sch.cnt["T"]
    vslab = [ABIG[:, 4096 + dt * 1024:4096 + (dt + 1) * 1024] for dt in range(NDT)]
    sch.wait("gpsimd", "T")
    vt = []
    for dt in range(NDT):
        vt.append(sch.dma("gpsimd", vslab[dt],
                          uvwv_d[dt * 128:(dt + 1) * 128, :]))
    for tk in vt:
        sch.wait_tok("tensor", tk)
    for tt in range(32):
        for n in range(2):
            ps = pool.acquire()
            for dt in range(NDT):
                t_mm(sch, ps, ctx.xlnT(dt)[:, tt * 128:(tt + 1) * 128],
                     vslab[dt][:, n * 512:(n + 1) * 512], dt == 0, dt == NDT - 1)
            j2 = (tt * 2 + n) % 2
            sch.wait("scalar", "T")
            tk2 = st.get(("ust", j2))
            if tk2 is not None:
                sch.wait_tok("scalar", tk2)
            sc = s_act(sch, stg[j2], ps, AF.Silu)
            pool.release("SC", sc)
            sch.wait("sync", "SC", sc)
            st[("ust", j2)] = sch.dma(
                "sync", dVR[tt * 128:(tt + 1) * 128, n * 512:(n + 1) * 512],
                stg[j2])
    sch.barrier()

    vfull = lambda tt: AMID[:, tt * 1024:(tt + 1) * 1024]
    PrQ = ABIG[:, 0:8448]
    PiQ = ABIG[:, 8448:16896]
    ftb = [ABIG[:, 16896 + i * 4096:16896 + (i + 1) * 4096] for i in range(2)]
    itb = [ABIG[:, 25088 + i * 4096:25088 + (i + 1) * 4096] for i in range(2)]
    ftb2 = [ABIG[:, 16896 + i * 4224:16896 + (i + 1) * 4224] for i in range(2)]
    itb2 = [ABIG[:, 25344 + i * 4224:25344 + (i + 1) * 4224] for i in range(2)]
    sow = ABIG[0:32, 33792:34816]
    ystg = [ABIG[:, 34816 + i * 256:35072 + i * 256] for i in range(2)]
    ytr = [ABIG[:, 35328 + i * 128:35456 + i * 128] for i in range(2)]
    vload = []
    for tt in range(32):
        vload.append(sch.dma("sync", vfull(tt), dVR[tt * 128:(tt + 1) * 128, :]))
    wo = sch.dma("gpsimd", sow, outw_d)
    for tk in vload:
        sch.wait_tok("tensor", tk)
    sch.wait_tok("tensor", wo)

    for dq in range(4):
        d0 = dq * 256
        for kc in range(NK):
            i2 = kc % 2
            if ("ftb", i2) in st:
                sch.wait("sync", "T", st[("ftb", i2)])
                sch.wait("gpsimd", "T", st[("ftb", i2)])
            cview = cosF[kc:kc + 1, :].rearrange("a (p f) -> (a p) f", p=128)
            sview = sinF[kc:kc + 1, :].rearrange("a (p f) -> (a p) f", p=128)
            f1 = sch.dma("sync", ftb[i2], cview)
            s1 = sch.dma("gpsimd", itb[i2], sview)
            for tk in (f1, s1):
                sch.wait_tok("tensor", tk)
            if ("pwv", kc % 2) in st:
                sch.wait("tensor", "V", st[("pwv", kc % 2)])
                sch.wait("tensor", "SC", st[("pwsc", kc % 2)])
            pb = (kc % 2) * 512
            for tt in range(32):
                t_mm(sch, PS[:, 0 + pb:256 + pb],
                     ftb[i2][:, tt * 128:(tt + 1) * 128],
                     vfull(tt)[:, d0:d0 + 256], tt == 0, tt == 31)
            for tt in range(32):
                t_mm(sch, PS[:, 1024 + pb:1280 + pb],
                     itb[i2][:, tt * 128:(tt + 1) * 128],
                     vfull(tt)[:, d0:d0 + 256], tt == 0, tt == 31)
            st[("ftb", i2)] = sch.cnt["T"]
            if "pwlast" in st:
                sch.wait("tensor", "V", st["pwlast"])
            tfa = t_mm(sch, PS[:, 2048:2304],
                       SrT[0:32, kc * 128:(kc + 1) * 128], sow[0:32, d0:d0 + 256],
                       True, True)
            tfb = t_mm(sch, PS[:, 2560:2816],
                       SiT[0:32, kc * 128:(kc + 1) * 128], sow[0:32, d0:d0 + 256],
                       True, True)
            sch.wait("scalar", "T", tfb)
            if "pwlast" in st:
                sch.wait("scalar", "V", st["pwlast"])
            sa1 = s_act(sch, ystg[0], PS[:, 0 + pb:256 + pb], AF.Identity,
                        scale=ccol[:, kc:kc + 1])
            sa2 = s_act(sch, ystg[1], PS[:, 1024 + pb:1280 + pb], AF.Identity,
                        scale=ccol[:, kc:kc + 1])
            sch.wait("vector", "SC", sa2)
            prs = PrQ[:, kc * 256:(kc + 1) * 256]
            pis = PiQ[:, kc * 256:(kc + 1) * 256]
            v_tt(sch, prs, ystg[0], PS[:, 2048:2304], ALU.mult)
            v_tt(sch, pis, ystg[1], PS[:, 2048:2304], ALU.mult)
            v_tt(sch, ystg[0], ystg[0], PS[:, 2560:2816], ALU.mult)
            v_tt(sch, ystg[1], ystg[1], PS[:, 2560:2816], ALU.mult)
            v_tt(sch, prs, prs, ystg[1], ALU.subtract)
            vpw = v_tt(sch, pis, pis, ystg[0], ALU.add)
            st[("pwv", kc % 2)] = vpw
            st[("pwsc", kc % 2)] = sa2
            st["pwlast"] = vpw
        sch.wait("tensor", "V")
        sch.wait("sync", "T")
        sch.wait("gpsimd", "T")
        for tt in range(32):
            i2 = tt % 2
            if ("itbf", i2) in st:
                sch.wait("sync", "T", st[("itbf", i2)])
                sch.wait("gpsimd", "T", st[("itbf", i2)])
            giv = icosT[tt:tt + 1, :].rearrange("a (p f) -> (a p) f", p=128)
            gsv = isinT[tt:tt + 1, :].rearrange("a (p f) -> (a p) f", p=128)
            g1 = sch.dma("sync", ftb2[i2], giv)
            g2 = sch.dma("gpsimd", itb2[i2], gsv)
            for tk in (g1, g2):
                sch.wait_tok("tensor", tk)
            pb = (tt % 2) * 512
            if ("syf", i2) in st:
                sch.wait("tensor", "SC", st[("syf", i2)])
            for kc in range(NK):
                t_mm(sch, PS[:, 0 + pb:256 + pb],
                     ftb2[i2][:, kc * 128:(kc + 1) * 128],
                     PrQ[:, kc * 256:(kc + 1) * 256], kc == 0, False)
            for kc in range(NK):
                t_mm(sch, PS[:, 0 + pb:256 + pb],
                     itb2[i2][:, kc * 128:(kc + 1) * 128],
                     PiQ[:, kc * 256:(kc + 1) * 256], False, kc == NK - 1)
            st[("itbf", i2)] = sch.cnt["T"]
            sch.wait("scalar", "T")
            if ("ytrf", i2) in st:
                sch.wait("scalar", "T", st[("ytrf", i2)])
            sy = s_act(sch, ystg[i2], PS[:, 0 + pb:256 + pb], AF.Identity)
            st[("syf", i2)] = sy
            for j in range(2):
                sch.wait("tensor", "SC", sy)
                ptr = PS[:, 1024 + j * 512:1024 + j * 512 + 64]
                if ("ytv", j) in st:
                    sch.wait("tensor", "V", st[("ytv", j)])
                ttr = t_tr(sch, ptr.bitcast(BF16), ystg[i2][:, j * 128:(j + 1) * 128],
                           ctx.IDEN[:, :])
                sch.wait("vector", "T", ttr)
                tk3 = st.get(("ytd", i2))
                if tk3 is not None:
                    sch.wait_tok("vector", tk3)
                vc = v_copy(sch, ytr[i2], ptr.bitcast(BF16))
                st[("ytv", j)] = vc
                sch.wait("sync", "V", vc)
                st[("ytd", i2)] = sch.dma(
                    "sync", dYT[d0 + j * 128:d0 + (j + 1) * 128,
                                tt * 128:(tt + 1) * 128], ytr[i2])
            st[("ytrf", i2)] = sch.cnt["T"]
        sch.barrier()

    ows = [ABIG[:, 16896 + dt * 1024:16896 + (dt + 1) * 1024] for dt in range(NDT)]
    oto = ABIG[:, 0:4096]
    ust = ABIG[:, 4096 + 0:8192]
    yst2 = ABIG[:, 8192:12288]
    xst4 = [AF_[:, 0 + i * 512:512 + i * 512] for i in range(2)]
    ost4 = [AF_[:, 1024 + i * 512:1536 + i * 512] for i in range(2)]
    wot = []
    sch.wait("gpsimd", "T")
    sch.wait("gpsimd", "V")
    for dt in range(NDT):
        wot.append(sch.dma("gpsimd", ows[dt], ow_d[dt * 128:(dt + 1) * 128, :]))
    for tk in wot:
        sch.wait_tok("tensor", tk)
    for tc in range(NTC):
        sch.wait("sync", "T")
        lu = sch.dma("sync", ust,
                     dUT[:, tc * 512:(tc + 1) * 512]
                     .rearrange("(dt p) t -> p dt t", p=128))
        ly = sch.dma("sync", yst2,
                     dYT[:, tc * 512:(tc + 1) * 512]
                     .rearrange("(dt p) t -> p dt t", p=128))
        sch.wait_tok("vector", lu)
        sch.wait_tok("vector", ly)
        sch.wait("vector", "T")
        vm = v_tt(sch, oto, ust, yst2, ALU.mult)
        sch.wait("tensor", "V", vm)
        for m in range(NDT):
            ps = pool.acquire()
            for dt in range(NDT):
                t_mm(sch, ps, ows[dt][:, m * 128:(m + 1) * 128],
                     oto[:, dt * 512:(dt + 1) * 512], dt == 0, dt == NDT - 1)
            j2 = (tc * NDT + m) % 2
            xf = st.get(("oxf", j2))
            if xf is not None:
                sch.wait("sync", "V", xf)
            xt = sch.dma("sync", xst4[j2],
                         x_dram[m * 128:(m + 1) * 128, tc * 512:(tc + 1) * 512])
            sch.wait("vector", "T")
            sch.wait_tok("vector", xt)
            of = st.get(("oof", j2))
            if of is not None:
                sch.wait_tok("vector", of)
            vo = v_tt(sch, ost4[j2], ps, xst4[j2], ALU.add)
            st[("oxf", j2)] = vo
            pool.release("V", vo)
            sch.wait("sync", "V", vo)
            st[("oof", j2)] = sch.dma(
                "sync", xo_dram[m * 128:(m + 1) * 128, tc * 512:(tc + 1) * 512],
                ost4[j2])



def _t(a):
    return np.ascontiguousarray(np.asarray(a).T)


def _c(a):
    return np.ascontiguousarray(np.asarray(a))


def prep_weights(inp):
    w = {}
    w["C_iden"] = np.eye(128, dtype=BF)
    w["C_idenf"] = np.eye(128, dtype=np.float32)
    w["L0A_WqT"] = _t(inp["L0_attn_a_Wq"]).astype(BF)
    w["L0A_WkT"] = _t(inp["L0_attn_a_Wk"]).astype(BF)
    w["L0A_WvT"] = _t(inp["L0_attn_a_Wv"]).astype(BF)
    w["L0A_WoT"] = _t(inp["L0_attn_a_Wo"]).astype(BF)
    w["L0A_bq"] = _c(inp["L0_attn_a_bq"]).reshape(D, 1)
    w["L0A_bk"] = _c(inp["L0_attn_a_bk"]).reshape(D, 1)
    w["L0A_bv"] = _c(inp["L0_attn_a_bv"]).reshape(D, 1)
    w["L0A_bo"] = _c(inp["L0_attn_a_bo"]).reshape(D, 1)
    w["L0A_gamma"] = _c(inp["L0_attn_a_gamma"]).reshape(D, 1)
    w["L0A_beta"] = _c(inp["L0_attn_a_beta"]).reshape(D, 1)
    w["L0A_arep"] = np.repeat(np.asarray(inp["L0_attn_a_adaptive_scale"]), 64).reshape(D, 1).astype(np.float32)
    M = 8192
    tt_ = np.arange(4096, dtype=np.float64)
    kk = np.arange(33 * 128, dtype=np.float64)
    ang = 2.0 * np.pi * np.outer(tt_, kk) / M
    cosF = np.cos(ang)
    sinF = -np.sin(ang)
    sinF[:, 4096] = 0.0
    w["C_cosF"] = np.ascontiguousarray(
        cosF.reshape(32, 128, 33, 128).transpose(2, 1, 0, 3)
        .reshape(33, 4096 * 128)).astype(BF)
    w["C_sinF"] = np.ascontiguousarray(
        sinF.reshape(32, 128, 33, 128).transpose(2, 1, 0, 3)
        .reshape(33, 4096 * 128)).astype(BF)
    angI = 2.0 * np.pi * np.outer(kk, tt_) / M
    icos = np.cos(angI)
    isin = -np.sin(angI)
    icos[4097:, :] = 0.0
    isin[4097:, :] = 0.0
    w["C_icos"] = np.ascontiguousarray(
        icos.reshape(33, 128, 32, 128).transpose(2, 1, 0, 3)
        .reshape(32, 33 * 128 * 128)).astype(BF)
    w["C_isin"] = np.ascontiguousarray(
        isin.reshape(33, 128, 32, 128).transpose(2, 1, 0, 3)
        .reshape(32, 33 * 128 * 128)).astype(BF)
    ck = np.full(33 * 128, 2.0 / M)
    ck[0] = 1.0 / M
    ck[4096] = 1.0 / M
    ck[4097:] = 0.0
    w["C_ccol"] = np.ascontiguousarray(
        ck.reshape(33, 128).T).astype(np.float32)
    posv = np.zeros(8192)
    posv[0:4096] = np.arange(4096)
    posv[4096:] = -np.arange(4096)
    gamv = np.zeros(8192)
    gamv[0:4096] = 0.99 ** np.arange(4096)
    gamv[4096:] = 0.99 ** np.arange(4096)
    gamv[4096] = 0.0
    w["C_pos"] = np.ascontiguousarray(
        posv.reshape(64, 128).T).astype(np.float32)
    w["C_gam"] = np.ascontiguousarray(
        gamv.reshape(64, 128).T).astype(np.float32)
    rpeb = np.zeros((128, 320), np.float32)
    rpeb[:, 0:32] = np.broadcast_to(_c(inp["L0_attn_b_rpe_pos_w"]), (128, 32))
    for i in range(3):
        rpeb[:, 32 + i * 32:64 + i * 32] = np.broadcast_to(
            _c(inp["L0_attn_b_rpe_ln_w"])[i], (128, 32))
        rpeb[:, 128 + i * 32:160 + i * 32] = np.broadcast_to(
            _c(inp["L0_attn_b_rpe_ln_b"])[i], (128, 32))
    rpeb[:, 224:256] = np.broadcast_to(_c(inp["L0_attn_b_rpe_out_ln_w"]), (128, 32))
    rpeb[:, 256:288] = np.broadcast_to(_c(inp["L0_attn_b_rpe_out_ln_b"]), (128, 32))
    w["C_rpeb"] = rpeb
    w["C_fc"] = np.ascontiguousarray(
        np.concatenate([_c(inp["L0_attn_b_rpe_fc_w"])[i] for i in range(3)],
                       axis=1)).astype(np.float32)
    uvw = _c(inp["L0_attn_b_uv_w"]).astype(BF)
    w["L0T_uvwu"] = np.ascontiguousarray(
        uvw[:, 0:1024].reshape(8, 128, 8, 128).transpose(2, 1, 0, 3)
        .reshape(8, 131072))
    w["L0T_uvwv"] = np.ascontiguousarray(uvw[:, 1024:2048])
    w["L0T_ow"] = _c(inp["L0_attn_b_o_w"]).astype(BF)
    w["L0T_outw"] = _c(inp["L0_attn_b_rpe_out_w"]).astype(BF)
    A = _c(inp["L1_attn_a_A_log"])
    w["L1M_Acols"] = np.ascontiguousarray(
        A.reshape(8, 128, 16).transpose(1, 0, 2).reshape(128, 128))
    w["L1M_WdbT"] = _t(inp["L1_attn_a_W_deltaBC"]).astype(BF)
    w["L1M_WdtT"] = _t(inp["L1_attn_a_W_dt"]).astype(BF)
    w["L1M_bdt"] = _c(inp["L1_attn_a_b_dt"]).reshape(D, 1)
    w["L1M_Dcol"] = _c(inp["L1_attn_a_D_param"]).reshape(D, 1)
    def _tile4(a, nmo):
        a = np.ascontiguousarray(np.asarray(a).T).astype(BF)
        kt = a.shape[0] // 128
        return np.ascontiguousarray(
            a.reshape(kt, 128, nmo, 128).transpose(2, 1, 0, 3)
            .reshape(nmo, kt * 128 * 128))
    w["L1F_w1P"] = _tile4(inp["L1_ffn_w1_w"], 32)
    w["L1F_w2P"] = _tile4(inp["L1_ffn_w2_w"], 32)
    w["L1F_h1P"] = _tile4(inp["L1_ffn_h1_w"], 32)
    w["L1F_h2P"] = _tile4(inp["L1_ffn_h2_w"], 32)
    w["L1F_w3P"] = _tile4(inp["L1_ffn_w3_w"], 8)
    w["L1F_s1"] = _c(inp["L1_ffn_w1_s"]).reshape(NH4, 1)
    w["L1F_s2"] = _c(inp["L1_ffn_w2_s"]).reshape(NH4, 1)
    w["L1F_sh1"] = _c(inp["L1_ffn_h1_s"]).reshape(NH4, 1)
    w["L1F_sh2"] = _c(inp["L1_ffn_h2_s"]).reshape(NH4, 1)
    w["L1F_s3"] = _c(inp["L1_ffn_w3_s"]).reshape(D, 1)
    w["L1B_WkT"] = _t(inp["L1_attn_b_Wk"]).astype(BF)
    w["L1B_WvT"] = _t(inp["L1_attn_b_Wv"]).astype(BF)
    w["L1B_WqT"] = _t(inp["L1_attn_b_Wq"]).astype(BF)
    w["L1B_WoutT"] = _t(inp["L1_attn_b_Wout"]).astype(BF)
    w["L1B_W0T"] = _t(inp["L1_attn_b_W0"]).astype(BF)
    w["L1B_W1T"] = _t(inp["L1_attn_b_W1"]).astype(BF)
    w["L1B_W1"] = _c(inp["L1_attn_b_W1"]).astype(BF)
    _ = (inp["L1_attn_b_Wg_mom"], inp["L1_attn_b_bg_mom"])
    w["L1B_WgD"] = _c(inp["L1_attn_b_Wg_decay"]).reshape(D, 1)
    w["L1B_WgL"] = _c(inp["L1_attn_b_Wg_lr"]).reshape(D, 1)
    w["L1B_bgD"] = _c(inp["L1_attn_b_bg_decay"]).reshape(1, 1)
    w["L1B_bgL"] = _c(inp["L1_attn_b_bg_lr"]).reshape(1, 1)
    w["L0_ffn_WT"] = _t(inp["L0_ffn_W"]).astype(BF)
    w["L0_ffn_Vr"] = _c(inp["L0_ffn_V_r"]).astype(BF)
    w["L0_ffn_CT"] = _t(inp["L0_ffn_C"]).astype(BF)
    w["L0_ffn_bcol"] = _c(inp["L0_ffn_b"]).reshape(D, 1)
    return w


SUBLAYERS = ["attn_a", "ffn0a", "tno", "ffn0b", "mamba", "ffn1a", "ttt", "ffn1b"]


def build_sublayer(name):
    nc = bass.Bass("TRN2", target_bir_lowering=False)
    ctx = Ctx(nc)
    xin = ctx.din("xin", [D, T], F32)
    xout = ctx.dout("xout", [D, T], F32)
    emit_setup(ctx)
    if name in ("ffn0a", "ffn0b"):
        emit_ffn0(ctx, xin, xout, "L0_ffn")
    elif name == "attn_a":
        emit_attn_a(ctx, xin, xout)
    elif name == "ttt":
        emit_ttt(ctx, xin, xout)
    elif name in ("ffn1a", "ffn1b"):
        emit_ffn1(ctx, xin, xout)
    elif name == "mamba":
        emit_mamba(ctx, xin, xout)
    elif name == "tno":
        emit_tno(ctx, xin, xout)
    else:
        raise NotImplementedError(name)
    ctx.sch.finalize()
    ctx.sch.build()
    return nc, ctx


def build_fused():
    nc = bass.Bass("TRN2", target_bir_lowering=False)
    ctx = Ctx(nc)
    xin = ctx.din("xin", [D, T], F32)
    xout = ctx.dout("xout", [D, T], F32)
    xs1 = ctx.dtmp("xs1", [D, T], F32)
    xs2 = ctx.dtmp("xs2", [D, T], F32)
    emit_setup(ctx)
    seq = [emit_attn_a,
           lambda c, a, b: emit_ffn0(c, a, b, "L0_ffn"),
           emit_tno,
           lambda c, a, b: emit_ffn0(c, a, b, "L0_ffn"),
           emit_mamba,
           emit_ffn1,
           emit_ttt,
           emit_ffn1]
    bufs = [xin, xs1, xs2, xs1, xs2, xs1, xs2, xs1, xout]
    for i, fn in enumerate(seq):
        fn(ctx, bufs[i], bufs[i + 1])
        ctx.sch.barrier()
    ctx.sch.finalize()
    ctx.sch.build()
    return nc, ctx


def run_fused(xT_list, weights):
    if "fused" not in _PROG_CACHE:
        _PROG_CACHE["fused"] = build_fused()
    nc, ctx = _PROG_CACHE["fused"]
    in_maps = []
    for c in range(NCORES):
        m = {k: (xT_list[c] if k == "xin" else weights[k]) for k in ctx.inputs}
        in_maps.append(m)
    res = run_bass_kernel_spmd(nc, in_maps, core_ids=list(range(NCORES)))
    return [np.asarray(res.results[c]["xout"]) for c in range(NCORES)]


_PROG_CACHE = {}


def run_sublayer(name, xT_list, weights):
    if name not in _PROG_CACHE:
        _PROG_CACHE[name] = build_sublayer(name)
    nc, ctx = _PROG_CACHE[name]
    in_maps = []
    for c in range(NCORES):
        m = {}
        for k in ctx.inputs:
            m[k] = xT_list[c] if k == "xin" else weights[k]
        in_maps.append(m)
    res = run_bass_kernel_spmd(nc, in_maps, core_ids=list(range(NCORES)))
    return [np.asarray(res.results[c]["xout"]) for c in range(NCORES)]


def kernel(**inputs):
    w = prep_weights(inputs)
    x = np.asarray(inputs["L0_attn_a_x"])
    xT = [np.ascontiguousarray(x[c].T) for c in range(NCORES)]
    xT = run_fused(xT, w)
    out = np.stack([np.asarray(t).T for t in xT], axis=0)
    return out.astype(np.float32)

